# version 21
# speedup vs baseline: 1.1086x; 1.0364x over previous
import numpy as np
import ml_dtypes as md
import concourse.bass as bass
import concourse.mybir as mybir
from concourse import dve_ops
from concourse.bass_utils import run_bass_kernel_spmd

F32 = mybir.dt.float32
BF16 = mybir.dt.bfloat16
AF = mybir.ActivationFunctionType
ALU = mybir.AluOpType
AXX = mybir.AxisListType.X
AP = bass.AP

NCORES = 8
SEQS, S, D = 2, 2048, 1024
NPAIR, TT, QC = 8, 16, 4
NITER = 3
NJ, NTF = 8, 8
OPLIMIT = [None]


ENGS = ("tensor", "vector", "scalar", "sync", "gpsimd")
DRAIN_ENGS = frozenset(("vector", "scalar"))
SELFSEM_ENGS = frozenset(("sync", "gpsimd"))


class Buf:
    __slots__ = ("name", "w", "rs", "psum")

    def __init__(self, name="b", init_deps=(), psum=False):
        self.name, self.w, self.rs = name, None, list(init_deps)
        self.psum = psum


class Op:
    __slots__ = ("eng", "idx", "fn", "raw", "war", "inc", "flag", "semval", "tag", "skey")


class Prog:
    def __init__(self):
        self.ops = {e: [] for e in ENGS}

    def add(self, eng, fn, reads=(), writes=(), dma=False, tag=""):
        if OPLIMIT[0] is not None:
            self.total = getattr(self, "total", 0) + 1
            if self.total > OPLIMIT[0]:
                return ("dropped", -1)
        idx = len(self.ops[eng])
        tok = (eng, idx)
        op = Op()
        op.eng, op.idx, op.fn, op.tag = eng, idx, fn, tag
        op.inc = 16 if dma else 1
        op.flag = False
        op.semval = 0
        raw, war = set(), set()
        for b in reads:
            if b.w is not None:
                raw.add(b.w)
            if b.psum and eng in ("vector", "scalar"):
                for rtok in b.rs:
                    if rtok[0] != eng and rtok[0] in ("vector", "scalar"):
                        raw.add(rtok)
        for b in writes:
            if b.w is not None:
                raw.add(b.w)
            war.update(b.rs)
        raw.discard(tok)
        war.discard(tok)
        op.raw, op.war = raw, war
        self.ops[eng].append(op)
        for b in reads:
            b.rs.append(tok)
        for b in writes:
            b.w = tok
            b.rs = []
        return tok

    def prepare(self):
        for e in ENGS:
            for op in self.ops[e]:
                op.skey = (e + "_c") if (e == "gpsimd" and op.inc == 1) else e
                if op.inc == 16:
                    op.flag = True
                for (de, di) in list(op.raw) + list(op.war):
                    if de != e or de in SELFSEM_ENGS:
                        self.ops[de][di].flag = True
        for e in ENGS:
            cnt = {}
            for op in self.ops[e]:
                if op.flag:
                    cnt[op.skey] = cnt.get(op.skey, 0) + op.inc
                op.semval = cnt.get(op.skey, 0)
        self.collect_waitvals()

    def _op_needs(self, ename, op, last_drain):
        needs = {}
        drain_upto = -1
        for kind, deps in (("raw", op.raw), ("war", op.war)):
            for (de, di) in deps:
                if de == ename and de not in SELFSEM_ENGS:
                    if di > last_drain and ename in DRAIN_ENGS:
                        drain_upto = max(drain_upto, di)
                    continue
                dep = self.ops[de][di]
                needs[dep.skey] = max(needs.get(dep.skey, 0), dep.semval)
        return needs, drain_upto

    def collect_waitvals(self):
        self.waitvals = {}
        for ename in ENGS:
            seen = {}
            last_drain = -1
            for op in self.ops[ename]:
                needs, drain_upto = self._op_needs(ename, op, last_drain)
                if drain_upto >= 0:
                    last_drain = op.idx - 1
                for dk, v in needs.items():
                    if v > seen.get(dk, 0):
                        self.waitvals.setdefault(dk, set()).add(v)
                        seen[dk] = v

    def emit_engine(self, ename, eng, sems):
        seen = {}
        last_drain = -1
        qval = 0
        wv = self.waitvals.get(ename, ()) if ename in SELFSEM_ENGS else ()
        for op in self.ops[ename]:
            needs, drain_upto = self._op_needs(ename, op, last_drain)
            if drain_upto >= 0:
                eng.drain()
                last_drain = op.idx - 1
            if op.flag and op.inc == 16 and qval in wv and qval > seen.get(ename, 0):
                needs[ename] = max(needs.get(ename, 0), qval)
            for dk in sorted(needs):
                if needs[dk] > seen.get(dk, 0):
                    eng.wait_ge(sems[dk], needs[dk])
                    seen[dk] = needs[dk]
            inst = op.fn(eng)
            if op.flag:
                assert inst is not None, f"flagged op lacks inst: {ename}#{op.idx} {op.tag}"
                inst.then_inc(sems[op.skey], op.inc)
                if op.inc == 16:
                    qval = op.semval


class Arena:
    def __init__(self, nbytes):
        self.nbytes = nbytes
        self.active = []
        self.cur = 0
        self.peak = 0

    def reset(self):
        self.cur = 0

    def alloc(self, name, nbytes, nbufs=1):
        start = (self.cur + 63) & ~63
        end = start + nbytes
        assert end <= self.nbytes, f"arena overflow: {name} needs {end}"
        self.cur = end
        self.peak = max(self.peak, end)
        init, keep = [], []
        for (s0, e0, b) in self.active:
            if s0 < end and e0 > start:
                if b.w is not None:
                    init.append(b.w)
                init.extend(b.rs)
                if s0 < start:
                    keep.append((s0, start, b))
                if e0 > end:
                    keep.append((end, e0, b))
            else:
                keep.append((s0, e0, b))
        bufs = [Buf(f"{name}{i}", init) for i in range(nbufs)]
        for b in bufs:
            keep.append((start, end, b))
        self.active = keep
        return start, (bufs if nbufs > 1 else bufs[0])



def build_nc():
    nc = bass.Bass("TRN2", target_bir_lowering=False)

    dx = nc.dram_tensor("x", [SEQS, S, D], F32, kind="ExternalInput")
    dwqkv = nc.dram_tensor("wqkv", [3, NPAIR, 128, 1024], F32, kind="ExternalInput")
    dowt = nc.dram_tensor("owt", [128, 8192], F32, kind="ExternalInput")
    dw1 = nc.dram_tensor("w1t", [128, 8192], F32, kind="ExternalInput")
    dw2 = nc.dram_tensor("w2t", [128, 4096], F32, kind="ExternalInput")
    dw3 = nc.dram_tensor("w3t", [128, 1024], F32, kind="ExternalInput")
    dw4 = nc.dram_tensor("w4t", [128, 2], F32, kind="ExternalInput")
    dgw1 = nc.dram_tensor("gw1t", [128, 4096], F32, kind="ExternalInput")
    dgw2 = nc.dram_tensor("gw2t", [128, 1024], F32, kind="ExternalInput")
    dgw3 = nc.dram_tensor("gw3t", [128, 2], F32, kind="ExternalInput")
    db1 = nc.dram_tensor("b1", [128, 8], F32, kind="ExternalInput")
    db2 = nc.dram_tensor("b2", [128, 4], F32, kind="ExternalInput")
    db3 = nc.dram_tensor("b3", [128, 2], F32, kind="ExternalInput")
    db4 = nc.dram_tensor("b4", [1, 1], F32, kind="ExternalInput")
    dgb1 = nc.dram_tensor("gb1", [128, 4], F32, kind="ExternalInput")
    dgb2 = nc.dram_tensor("gb2", [128, 2], F32, kind="ExternalInput")
    dgb3 = nc.dram_tensor("gb3", [1, 1], F32, kind="ExternalInput")
    dar = nc.dram_tensor("ar", [1, 1], F32, kind="ExternalInput")
    dident = nc.dram_tensor("ident", [128, 128], BF16, kind="ExternalInput")
    dmasks = nc.dram_tensor("masks", [128, 2048], BF16, kind="ExternalInput")
    dones = nc.dram_tensor("ones_f", [1, 128], F32, kind="ExternalInput")
    dones128 = nc.dram_tensor("ones128", [128, 1], F32, kind="ExternalInput")
    dselp = nc.dram_tensor("selp", [128, 256], BF16, kind="ExternalInput")
    dnb = nc.dram_tensor("nb128", [128, 2], BF16, kind="ExternalInput")
    donec = nc.dram_tensor("onec", [128, 1], BF16, kind="ExternalInput")
    dout = nc.dram_tensor("out", [SEQS, S, D], F32, kind="ExternalOutput")

    from contextlib import ExitStack
    es = ExitStack()

    def sbt(name, shape, dt):
        return es.enter_context(nc.sbuf_tensor(name, shape, dt))

    stream = sbt("stream", [128, TT * 1024], F32)
    lnsc = sbt("lnsc", [128, 1024], F32)
    lnsc2 = sbt("lnsc2", [128, 1024], F32)
    castb = sbt("castb", [128, 4096], BF16)
    stats = sbt("stats", [128, 184], F32)
    masks = sbt("masks_s", [128, 2048], BF16)
    ident = sbt("ident_s", [128, 128], BF16)
    ones_f = sbt("ones_fs", [1, 128], F32)
    ones_f128c = sbt("ones_f128", [128, 1], F32)
    selp = sbt("selp_s", [128, 256], BF16)
    nb128 = sbt("nb128_s", [128, 2], BF16)
    onec = sbt("onec_s", [128, 1], BF16)
    eps = sbt("eps", [128, 1], F32)
    w4t = sbt("w4t_s", [128, 2], BF16)
    b1s = sbt("b1s", [128, 8], F32)
    b2s = sbt("b2s", [128, 4], F32)
    b3s = sbt("b3s", [128, 2], F32)
    b4bc = sbt("b4bc", [128, 1], F32)
    b4h = sbt("b4h", [128, 1], F32)
    gb3h = sbt("gb3h", [1, 1], F32)
    gb1s = sbt("gb1s", [128, 4], F32)
    gb2s = sbt("gb2s", [128, 2], F32)
    gb3s = sbt("gb3s", [1, 1], F32)
    ars = sbt("ars", [1, 1], F32)
    rowz = sbt("rowz", [128, 1024], F32)
    rowzb = sbt("rowzb", [128, 1024], BF16)
    zbc = sbt("zbc", [128, 512], BF16)
    negM = sbt("negM", [128, 2], F32)
    sc12 = sbt("sc12", [128, 2], F32)
    cpl = sbt("cpl", [128, 8], F32)
    ncpl = sbt("ncpl", [128, 8], F32)
    sigt = sbt("sigt", [128, 8], F32)
    gcrowb = sbt("gcrowb", [1, 1024], BF16)
    gcT = sbt("gcT", [128, 8], BF16)
    hg1 = sbt("hg1", [128, 4], BF16)
    hg2 = sbt("hg2", [128, 2], BF16)
    grow = sbt("grow", [1, 8], F32)
    normrow = sbt("normrow", [2, 8], F32)

    ARENA_F32 = 28688
    arena_t = sbt("arena", [128, ARENA_F32], F32)
    arena_bt = arena_t.bitcast(BF16)
    AB_ROW = 2 * ARENA_F32
    arena = Arena(ARENA_F32 * 4)

    def af(off_b, cols, p0=0, np_=128):
        assert off_b % 4 == 0
        return AP(arena_t, (off_b // 4) + p0 * ARENA_F32, [[ARENA_F32, np_], [1, cols]])

    def ab(off_b, cols, p0=0, np_=128):
        assert off_b % 2 == 0
        return AP(arena_bt, (off_b // 2) + p0 * AB_ROW, [[AB_ROW, np_], [1, cols]])

    ps_s = [es.enter_context(nc.psum_tensor(f"ps_s{i}", [128, 1024], F32)) for i in range(2)]
    ps_a = es.enter_context(nc.psum_tensor("ps_a", [128, 512], F32))
    ps_b = es.enter_context(nc.psum_tensor("ps_b", [128, 512], F32))
    ps_t0 = es.enter_context(nc.psum_tensor("ps_t0", [128, 1024], BF16))
    ps_c = es.enter_context(nc.psum_tensor("ps_c", [128, 512], F32))
    ps_t = [ps_t0, ps_c.bitcast(BF16)]

    sems = {e: es.enter_context(nc.semaphore(f"sem_{e}")) for e in ENGS}
    sems["gpsimd_c"] = es.enter_context(nc.semaphore("sem_gpsimd_c"))
    block = es.enter_context(nc.Block())

    P = Prog()

    def ZR(qc):
        return (32 * qc) if qc < 3 else 0
    b_s = [[Buf("s0a", psum=True), Buf("s0b", psum=True)],
           [Buf("s1a", psum=True), Buf("s1b", psum=True)]]
    b_pa, b_pb = Buf("pa", psum=True), Buf("pb", psum=True)
    b_pt = [Buf("pt0", psum=True), Buf("pt1", psum=True)]
    b_stream = [Buf(f"st{t}") for t in range(TT)]
    b_lnsc = Buf("lnsc")
    b_lnsc2 = Buf("lnsc2")
    dumpctr = [0]
    b_castb = [Buf(f"cb{i}") for i in range(4)]
    b_masks, b_ident, b_ones, b_selp, b_nb = Buf("mk"), Buf("id"), Buf("on"), Buf("sp"), Buf("nb")
    b_onec = Buf("onec")
    b_bias, b_eps, b_w4, b_ar, b_b4 = Buf("bi"), Buf("ep"), Buf("w4"), Buf("ar"), Buf("b4")
    b_zq = [Buf(f"zq{i}") for i in range(4)]
    b_zrecb = Buf("zrecb")
    b_zbc = Buf("zbc")
    b_negMs = [Buf("negM0"), Buf("negM1")]
    b_norm = Buf("norm")
    b_cpl, b_sig, b_sc = Buf("cpl"), Buf("sig"), Buf("sc")
    b_grow, b_gc = Buf("grow"), Buf("gc")
    b_stored = [Buf(f"so{i}") for i in range(SEQS * TT)]
    b_stats = {}
    SCOL = [0]

    def stat_col(name, width=16):
        if name not in b_stats:
            b_stats[name] = (SCOL[0], Buf(name))
            SCOL[0] += width
            assert SCOL[0] <= 184, name
        return b_stats[name]

    def dma_to(dst, src, eng="sync", w=(), r=(), tag=""):
        P.add(eng, (lambda d0, s0: (lambda e: e.dma_start(out=d0, in_=s0)))(dst, src),
              reads=list(r), writes=list(w), dma=True, tag=tag)

    dma_to(masks[:], dmasks.ap(), w=[b_masks], tag="ldmask")
    dma_to(ident[:], dident.ap(), w=[b_ident])
    dma_to(ones_f[:], dones.ap(), w=[b_ones])
    dma_to(ones_f128c[:], dones128.ap(), r=[b_ones], w=[b_ones])
    dma_to(selp[:], dselp.ap(), w=[b_selp])
    dma_to(nb128[:], dnb.ap(), w=[b_nb])
    dma_to(onec[:], donec.ap(), w=[b_onec])
    dma_to(b1s[:], db1.ap(), w=[b_bias])
    dma_to(b2s[:], db2.ap(), r=[b_bias], w=[b_bias])
    dma_to(b3s[:], db3.ap(), r=[b_bias], w=[b_bias])
    dma_to(gb1s[:], dgb1.ap(), r=[b_bias], w=[b_bias])
    dma_to(gb2s[:], dgb2.ap(), r=[b_bias], w=[b_bias])
    dma_to(gb3s[:], dgb3.ap(), r=[b_bias], w=[b_bias])
    dma_to(ars[:], dar.ap(), w=[b_ar])
    dma_to(grow[0:1, 0:1], db4.ap(), w=[b_grow], tag="ldb4")
    P.add("vector", lambda e: e.memset(eps[:], 1e-5), writes=[b_eps])
    dma_to(w4t[:], dw4.ap(), eng="gpsimd", w=[b_w4])
    P.add("tensor", lambda e: e.matmul(ps_a[:, 0:1], ones_f[:], grow[0:1, 0:1],
                                       start=True, stop=True, skip_group_check=True),
          reads=[b_ones, b_grow], writes=[b_pa], tag="b4bc")
    P.add("vector", lambda e: e.tensor_copy(out=b4bc[:], in_=ps_a[:, 0:1]),
          reads=[b_pa], writes=[b_b4])
    P.add("vector", lambda e: e.tensor_scalar_mul(out=b4h[:], in0=b4bc[:], scalar1=0.5),
          reads=[b_b4], writes=[b_b4])
    P.add("vector", lambda e: e.tensor_scalar_mul(out=gb3h[0:1, :], in0=gb3s[0:1, :],
                                                  scalar1=0.5),
          reads=[b_bias], writes=[b_bias])

    def ln_stats(tile_ap, col_sum, col_ssq, src_bufs, bsum, bssq, tag=""):
        P.add("vector",
              (lambda a0, c0: (lambda e: e.reduce_sum(
                  out=stats[:, c0:c0 + 1], in_=a0, axis=AXX)))(tile_ap, col_sum),
              reads=list(src_bufs), writes=[bsum], tag="lnsum" + tag)
        dumpctr[0] ^= 1
        dmp, bdmp = (lnsc2, b_lnsc2) if dumpctr[0] else (lnsc, b_lnsc)
        P.add("scalar",
              (lambda a0, c0, d0: (lambda e: e.activation(
                  out=d0[:], in_=a0, func=AF.Square,
                  accum_out=stats[:, c0:c0 + 1])))(tile_ap, col_ssq, dmp),
              reads=list(src_bufs), writes=[bdmp, bssq], tag="lnssq" + tag)

    def ln_rowmath(c_sum, c_ssq, c_A, c_B, bsum, bssq, bA, bB, n, tag=""):
        c_t, bt = stat_col("tmp0")
        P.add("vector",
              (lambda: (lambda e: e.tensor_scalar_mul(
                  out=stats[:, c_t:c_t + n], in0=stats[:, c_sum:c_sum + n],
                  scalar1=1.0 / D)))(),
              reads=[bsum], writes=[bt], tag="rm_m" + tag)
        P.add("vector",
              (lambda: (lambda e: e.tensor_mul(
                  out=stats[:, c_A:c_A + n], in0=stats[:, c_t:c_t + n],
                  in1=stats[:, c_t:c_t + n])))(),
              reads=[bt], writes=[bA], tag="rm_m2" + tag)
        P.add("vector",
              (lambda: (lambda e: e.scalar_tensor_tensor(
                  out=stats[:, c_B:c_B + n], in0=stats[:, c_ssq:c_ssq + n],
                  scalar=1.0 / D, in1=stats[:, c_A:c_A + n],
                  op0=ALU.mult, op1=ALU.subtract)))(),
              reads=[bssq, bA], writes=[bB], tag="rm_var" + tag)
        P.add("scalar",
              (lambda: (lambda e: e.activation(
                  out=stats[:, c_B:c_B + n], in_=stats[:, c_B:c_B + n],
                  func=AF.Ln, bias=eps[:])))(),
              reads=[bB, b_eps], writes=[bB], tag="rm_ln" + tag)
        P.add("scalar",
              (lambda: (lambda e: e.activation(
                  out=stats[:, c_A:c_A + n], in_=stats[:, c_B:c_B + n],
                  func=AF.Exp, scale=-0.5)))(),
              reads=[bB], writes=[bA], tag="rm_exp" + tag)
        P.add("vector",
              (lambda: (lambda e: e.scalar_tensor_tensor(
                  out=stats[:, c_B:c_B + n], in0=stats[:, c_t:c_t + n],
                  scalar=-1.0, in1=stats[:, c_A:c_A + n],
                  op0=ALU.mult, op1=ALU.mult)))(),
              reads=[bt, bA], writes=[bB], tag="rm_B" + tag)

    for s in range(SEQS):
        sfx = f"_s{s}"
        arena.reset()
        o_ln1t, b_ln1t = arena.alloc("ln1t", TT * 1024 * 2, nbufs=TT)
        o_attnN, b_attnN = arena.alloc("attnN", NPAIR * 2048 * 2, nbufs=NPAIR * QC)
        o_qt, b_qt = arena.alloc("qt", 2048 * 2, nbufs=QC)
        o_kt0, b_kts = arena.alloc("kt", 2 * 2048 * 2, nbufs=2 * QC)
        o_v, b_v = arena.alloc("v", 16 * 130 * 2, nbufs=4)
        o_ptl, b_ptl = arena.alloc("ptl", 2 * 1024 * 2, nbufs=2)
        o_qkvw, b_qkvw = arena.alloc("qkvw", 2 * 3 * 1024 * 2, nbufs=6)
        o_owt, b_owt = arena.alloc("owt", 8192 * 2, nbufs=8)

        def ln1T_ap(f, c0, ncols, p0=0, np_=128):
            return AP(arena_bt, o_ln1t // 2 + 2048 * f + c0 + p0 * AB_ROW,
                      [[AB_ROW, np_], [1, ncols]])

        for base in (64, 129):
            P.add("vector",
                  (lambda b0, ov: (lambda e: e.memset(
                      AP(arena_bt, ov // 2 + b0, [[AB_ROW, 128], [130, 16], [1, 1]]),
                      1.0)))(base, o_v),
                  writes=list(b_v), tag="vones" + sfx)

        for g in range(4):
            src = dx.ap()[s].rearrange("(t p) c -> p t c", p=128)[:, 4 * g:4 * g + 4, :]
            dst = AP(stream, 4096 * g, [[TT * 1024, 128], [1024, 4], [1, 1024]])
            dma_to(dst, src, w=b_stream[4 * g:4 * g + 4], tag=f"ldx{g}" + sfx)

        cS, bS = stat_col("l1sum")
        cQ, bQ = stat_col("l1ssq")
        cA1, bA1 = stat_col("A1")
        cB1, bB1 = stat_col("B1")
        for t in range(TT):
            ln_stats(stream[:, 1024 * t:1024 * (t + 1)], cS + t, cQ + t,
                     [b_stream[t]], bS, bQ, tag=f"1_{t}" + sfx)
        ln_rowmath(cS, cQ, cA1, cB1, bS, bQ, bA1, bB1, TT, tag="1" + sfx)

        for t in range(TT):
            c = t % 4
            P.add("vector",
                  (lambda t_, c_: (lambda e: e.tensor_scalar(
                      out=castb[:, 1024 * c_:1024 * (c_ + 1)],
                      in0=stream[:, 1024 * t_:1024 * (t_ + 1)],
                      scalar1=stats[:, cA1 + t_:cA1 + t_ + 1],
                      scalar2=stats[:, cB1 + t_:cB1 + t_ + 1],
                      op0=ALU.mult, op1=ALU.add)))(t, c),
                  reads=[b_stream[t], bA1, bB1], writes=[b_castb[c]],
                  tag=f"ln1ap{t}" + sfx)
            pt, bpt = ps_t[t % 2], b_pt[t % 2]

            def mk_tr(c_, pt_):
                def f(e):
                    r = None
                    for fb in range(8):
                        r = e.transpose(pt_[:, 128 * fb:128 * fb + 128],
                                        castb[:, 1024 * c_ + 128 * fb:1024 * c_ + 128 * fb + 128],
                                        ident[:])
                    return r
                return f
            P.add("tensor", mk_tr(c, pt), reads=[b_castb[c], b_ident], writes=[bpt],
                  tag=f"ln1tr{t}" + sfx)
            dstap = AP(arena_bt, o_ln1t // 2 + 128 * t, [[AB_ROW, 128], [2048, 8], [1, 128]])
            P.add("scalar",
                  (lambda pt_, d0: (lambda e: e.copy(out=d0, in_=pt_[:, 0:1024])))(pt, dstap),
                  reads=[bpt], writes=[b_ln1t[t]], tag=f"ln1ev{t}" + sfx)

        def load_qkv(p):
            slot = p % 2
            for m in range(3):
                dst = ab(o_qkvw + (slot * 3 + m) * 2048, 1024)
                dma_to(dst, dwqkv.ap()[m, p], eng="gpsimd",
                       w=[b_qkvw[slot * 3 + m]], tag=f"ldqkv{p}_{m}" + sfx)

        load_qkv(0)
        load_qkv(1)

        def emit_norms(tp):
            kpar = tp % 2
            o_ktp = o_kt0 + kpar * 4096
            bkt = b_kts[kpar * 4: kpar * 4 + 4]
            sqs_ap = ab(o_ptl, 1024)
            for (src_o, sbufs, w_) in ((o_qt, b_qt, 0), (o_ktp, bkt, 1)):
                for h2 in range(2):
                    sl, sb2 = ps_s[h2], b_s[h2]
                    P.add("vector",
                          (lambda so, h_: (lambda e: e.tensor_mul(
                              out=sqs_ap, in0=ab(so + h_ * 2048, 1024),
                              in1=ab(so + h_ * 2048, 1024))))(src_o, h2),
                          reads=[sbufs[2 * h2], sbufs[2 * h2 + 1]], writes=[b_ptl[0]],
                          tag=f"sq{tp}" + sfx)
                    for cc in range(2):
                        P.add("tensor",
                              (lambda sl_, cc_: (lambda e: e.matmul(
                                  sl_[0:2, 512 * cc_:512 * cc_ + 512], nb128[:],
                                  ab(o_ptl + 1024 * cc_, 512),
                                  start=True, stop=True, skip_group_check=True)))(sl, cc),
                              reads=[b_ptl[0], b_nb], writes=[sb2[cc]], tag=f"nmm{tp}" + sfx)
                    P.add("vector",
                          (lambda sl_, col: (lambda e: e.reduce_max(
                              out=normrow[:, col:col + 1], in_=sl_[0:2, 0:1024],
                              axis=AXX)))(sl, 4 * w_ + h2),
                          reads=[sb2[0], sb2[1]], writes=[b_norm], tag=f"nrd{tp}" + sfx)
            P.add("vector", lambda e: e.tensor_max(out=normrow[:, 0:1],
                                                   in0=normrow[:, 0:1], in1=normrow[:, 1:2]),
                  reads=[b_norm], writes=[b_norm])
            P.add("vector", lambda e: e.tensor_max(out=normrow[:, 4:5],
                                                   in0=normrow[:, 4:5], in1=normrow[:, 5:6]),
                  reads=[b_norm], writes=[b_norm])
            P.add("scalar", lambda e: e.activation(out=normrow[:, 2:3], in_=normrow[:, 0:1],
                                                   func=AF.Ln),
                  reads=[b_norm], writes=[b_norm])
            P.add("scalar", lambda e: e.activation(out=normrow[:, 6:7], in_=normrow[:, 4:5],
                                                   func=AF.Ln),
                  reads=[b_norm], writes=[b_norm])
            P.add("vector", lambda e: e.tensor_add(out=normrow[:, 3:4], in0=normrow[:, 2:3],
                                                   in1=normrow[:, 6:7]),
                  reads=[b_norm], writes=[b_norm])
            P.add("scalar", lambda e: e.activation(out=normrow[:, 7:8], in_=normrow[:, 3:4],
                                                   func=AF.Exp, scale=0.5),
                  reads=[b_norm], writes=[b_norm])
            P.add("vector", lambda e: e.tensor_scalar_mul(out=normrow[0:1, 7:8],
                                                          in0=normrow[0:1, 7:8],
                                                          scalar1=-1.02),
                  reads=[b_norm], writes=[b_norm])
            P.add("tensor", lambda e: e.matmul(ps_a[:, 0:1], ones_f[:], normrow[0:1, 7:8],
                                               start=True, stop=True, skip_group_check=True),
                  reads=[b_norm, b_ones], writes=[b_pa], tag=f"Mbc{tp}" + sfx)
            P.add("vector",
                  (lambda kp: (lambda e: e.tensor_copy(
                      out=negM[:, kp:kp + 1], in_=ps_a[:, 0:1])))(kpar),
                  reads=[b_pa], writes=[b_negMs[kpar]])

        for p in range(NPAIR):
            slot = p % 2
            kpar = p % 2
            o_ktp = o_kt0 + kpar * 4096
            b_kt = b_kts[kpar * 4: kpar * 4 + 4]
            wq = ab(o_qkvw + (slot * 3 + 0) * 2048, 1024)
            wk = ab(o_qkvw + (slot * 3 + 1) * 2048, 1024)
            wv = ab(o_qkvw + (slot * 3 + 2) * 2048, 1024)
            bwq, bwk, bwv = (b_qkvw[slot * 3 + m] for m in range(3))

            if p == 0:
                for (wt, bw, dsto, dbufs, scale, nm) in (
                        (wq, bwq, o_qt, b_qt, 1.0 / 32, "q"),
                        (wk, bwk, o_ktp, b_kt, None, "k")):
                    for qc in range(QC):
                        sl, sb2 = ps_s[qc // 2], b_s[qc // 2]
                        half = qc % 2
                        pap = sl[:, 512 * half:512 * half + 512]

                        def mk_proj(wt_, qc_, pap_):
                            def f(e):
                                r = None
                                for fb in range(8):
                                    r = e.matmul(pap_, wt_[:, 128 * fb:128 * fb + 128],
                                                 ln1T_ap(fb, 512 * qc_, 512),
                                                 start=(fb == 0), stop=(fb == 7),
                                                 skip_group_check=True)
                                return r
                            return f
                        P.add("tensor", mk_proj(wt, qc, pap),
                              reads=[bw] + b_ln1t[4 * qc:4 * qc + 4], writes=[sb2[half]],
                              tag=f"prj{nm}{p}_{qc}" + sfx)
                        dstap = ab(dsto + 1024 * qc, 512)
                        if scale is None:
                            P.add("vector",
                                  (lambda pap_, d0: (lambda e: e.tensor_copy(
                                      out=d0, in_=pap_)))(pap, dstap),
                                  reads=[sb2[half]], writes=[dbufs[qc]],
                                  tag=f"ev{nm}{p}_{qc}" + sfx)
                        else:
                            P.add("vector",
                                  (lambda pap_, d0, sc_: (lambda e: e.tensor_scalar_mul(
                                      out=d0, in0=pap_, scalar1=sc_)))(pap, dstap, scale),
                                  reads=[sb2[half]], writes=[dbufs[qc]],
                                  tag=f"ev{nm}{p}_{qc}" + sfx)
                emit_norms(0)

            for g in range(4):
                sl, sb2 = ps_s[g % 2], b_s[g % 2]
                for j in range(4):
                    kt = 4 * g + j
                    half, sub = j // 2, j % 2
                    pap = sl[:, 512 * half + 128 * sub: 512 * half + 128 * sub + 128]

                    def mk_v(kt_, pap_, wv_):
                        def f(e):
                            r = None
                            for fb in range(8):
                                r = e.matmul(pap_, ln1T_ap(fb, 128 * kt_, 128),
                                             wv_[:, 128 * fb:128 * fb + 128],
                                             start=(fb == 0), stop=(fb == 7),
                                             skip_group_check=True)
                            return r
                        return f
                    P.add("tensor", mk_v(kt, pap, wv),
                          reads=[bwv, b_ln1t[kt]], writes=[sb2[half]],
                          tag=f"prjv{p}_{kt}" + sfx)
                for j in range(4):
                    kt = 4 * g + j
                    half, sub = j // 2, j % 2
                    srcap = AP(sl, 512 * half + 128 * sub, [[1024, 128], [64, 2], [1, 64]])
                    dstap = AP(arena_bt, o_v // 2 + 130 * kt, [[AB_ROW, 128], [65, 2], [1, 64]])
                    P.add("vector",
                          (lambda s0, d0: (lambda e: e.tensor_copy(out=d0, in_=s0)))(srcap, dstap),
                          reads=[sb2[half]], writes=[b_v[g]], tag=f"evv{p}_{kt}" + sfx)

            if p + 2 < NPAIR:
                load_qkv(p + 2)
            if p == 5:
                for fb in range(8):
                    dma_to(ab(o_owt + fb * 2048, 1024),
                           dowt.ap()[:, 1024 * fb:1024 * fb + 1024], eng="gpsimd",
                           w=[b_owt[fb]], tag=f"ldow{fb}" + sfx)

            def emit_av(qc, kt, nkt):
                off = max(0, 128 * (kt - 4 * qc))
                ncols = 512 - off
                P.add("tensor",
                      (lambda kt_, o_, n_, f0, l0: (lambda e: e.matmul(
                          ps_a[0:65, o_:512],
                          AP(arena_bt, o_v // 2 + 130 * kt_, [[AB_ROW, 128], [1, 65]]),
                          AP(arena_bt, (o_ptl + 2048 * (kt_ % 2)) // 2 + o_,
                             [[AB_ROW, 128], [1, n_]]),
                          start=f0, stop=l0,
                          skip_group_check=True)))(kt, off, ncols, kt == 0, kt == nkt - 1),
                      reads=[b_ptl[kt % 2], b_v[kt // 4]], writes=[b_pa],
                      tag=f"avA{p}_{qc}_{kt}" + sfx)
                P.add("tensor",
                      (lambda kt_, o_, n_, f0, l0: (lambda e: e.matmul(
                          ps_b[0:65, o_:512],
                          AP(arena_bt, o_v // 2 + 130 * kt_ + 65, [[AB_ROW, 128], [1, 65]]),
                          AP(arena_bt, (o_ptl + 2048 * (kt_ % 2)) // 2 + 512 + o_,
                             [[AB_ROW, 128], [1, n_]]),
                          start=f0, stop=l0,
                          skip_group_check=True)))(kt, off, ncols, kt == 0, kt == nkt - 1),
                      reads=[b_ptl[kt % 2], b_v[kt // 4]], writes=[b_pb],
                      tag=f"avB{p}_{qc}_{kt}" + sfx)

            def interleave_unit(qc, which):
                nslot = (p + 1) % 2
                if which == "q":
                    wt_n = ab(o_qkvw + (nslot * 3 + 0) * 2048, 1024)
                    bw_n = b_qkvw[nslot * 3 + 0]
                    dstap = ab(o_qt + 1024 * qc, 512)
                    dbuf = b_qt[qc]
                else:
                    wt_n = ab(o_qkvw + (nslot * 3 + 1) * 2048, 1024)
                    bw_n = b_qkvw[nslot * 3 + 1]
                    dstap = ab(o_kt0 + nslot * 4096 + 1024 * qc, 512)
                    dbuf = b_kts[nslot * 4 + qc]

                def mk_prj(wt_, qc_):
                    def f(e):
                        r = None
                        for fb in range(8):
                            r = e.matmul(ps_c[:, 0:512], wt_[:, 128 * fb:128 * fb + 128],
                                         ln1T_ap(fb, 512 * qc_, 512),
                                         start=(fb == 0), stop=(fb == 7),
                                         skip_group_check=True)
                        return r
                    return f
                P.add("tensor", mk_prj(wt_n, qc),
                      reads=[bw_n] + b_ln1t[4 * qc:4 * qc + 4], writes=[b_pt[1]],
                      tag=f"iprj{which}{p + 1}_{qc}" + sfx)
                if which == "q":
                    P.add("vector",
                          (lambda d0: (lambda e: e.tensor_scalar_mul(
                              out=d0, in0=ps_c[:, 0:512], scalar1=1.0 / 32)))(dstap),
                          reads=[b_pt[1]], writes=[dbuf], tag=f"iev{which}{p + 1}_{qc}" + sfx)
                else:
                    P.add("vector",
                          (lambda d0: (lambda e: e.tensor_copy(
                              out=d0, in_=ps_c[:, 0:512])))(dstap),
                          reads=[b_pt[1]], writes=[dbuf], tag=f"iev{which}{p + 1}_{qc}" + sfx)

            for qc in range(QC):
                nkt = 4 * qc + 4
                for kt in range(nkt):
                    off = max(0, 128 * (kt - 4 * qc))
                    ncols = 512 - off
                    sl, sb2 = ps_s[kt % 2], b_s[kt % 2]
                    bptl = b_ptl[kt % 2]
                    P.add("tensor",
                          (lambda kt_, qc_, sl_, o_, n_, ok_: (lambda e: e.matmul(
                              sl_[:, o_:512],
                              AP(arena_bt, ok_ // 2 + 128 * kt_, [[AB_ROW, 64], [1, 128]]),
                              AP(arena_bt, o_qt // 2 + 512 * qc_ + o_, [[AB_ROW, 64], [1, n_]]),
                              start=True, stop=True, tile_position=(0, 0),
                              skip_group_check=True)))(kt, qc, sl, off, ncols, o_ktp),
                          reads=[b_kt[kt // 4], b_qt[qc]], writes=[sb2[0]],
                          tag=f"scA{p}_{qc}_{kt}" + sfx)
                    P.add("tensor",
                          (lambda kt_, qc_, sl_, o_, n_, ok_: (lambda e: e.matmul(
                              sl_[:, 512 + o_:1024],
                              AP(arena_bt, ok_ // 2 + 128 * kt_ + 64 * AB_ROW,
                                 [[AB_ROW, 64], [1, 128]]),
                              AP(arena_bt, o_qt // 2 + 512 * qc_ + o_ + 64 * AB_ROW,
                                 [[AB_ROW, 64], [1, n_]]),
                              start=True, stop=True, tile_position=(64, 0),
                              skip_group_check=True)))(kt, qc, sl, off, ncols, o_ktp),
                          reads=[b_kt[kt // 4], b_qt[qc]], writes=[sb2[1]],
                          tag=f"scB{p}_{qc}_{kt}" + sfx)
                    P.add("scalar",
                          (lambda kt_, sl_, o_, n_, kp: (lambda e: e.activation(
                              out=AP(arena_bt, (o_ptl + 2048 * (kt_ % 2)) // 2 + o_,
                                     [[AB_ROW, 128], [512, 2], [1, n_]]),
                              in_=AP(sl_, o_, [[1024, 128], [512, 2], [1, n_]]),
                              func=AF.Exp, bias=negM[:, kp:kp + 1])))(kt, sl, off, ncols, kpar),
                          reads=[sb2[0], sb2[1], b_negMs[kpar]], writes=[bptl],
                          tag=f"exp{p}_{qc}_{kt}" + sfx)
                    if off or kt == 4 * qc:
                        P.add("vector",
                              (lambda kt_, o_, n_, mo_: (lambda e: e.tensor_mul(
                                  out=AP(arena_bt, (o_ptl + 2048 * (kt_ % 2)) // 2 + o_,
                                         [[AB_ROW, 128], [512, 2], [1, n_]]),
                                  in0=AP(arena_bt, (o_ptl + 2048 * (kt_ % 2)) // 2 + o_,
                                         [[AB_ROW, 128], [512, 2], [1, n_]]),
                                  in1=AP(masks, 512 * mo_ + o_,
                                         [[2048, 128], [0, 2], [1, n_]]))))(
                                             kt, off, ncols, kt - 4 * qc),
                              reads=[bptl, b_masks], writes=[bptl],
                              tag=f"msk{p}_{qc}_{kt}" + sfx)
                    if kt > 0:
                        emit_av(qc, kt - 1, nkt)
                    if kt == nkt // 2 and p + 1 < NPAIR:
                        interleave_unit(qc, "k")
                emit_av(qc, nkt - 1, nkt)
                aidx = p * QC + qc
                P.add("vector",
                      (lambda p_, qc_: (lambda e: e.tensor_copy(
                          out=AP(arena_bt, o_attnN // 2 + 2048 * p_ + 512 * qc_,
                                 [[AB_ROW, 64], [1, 512]]),
                          in_=ps_a[0:64, :])))(p, qc),
                      reads=[b_pa], writes=[b_attnN[aidx]], tag=f"eva{p}_{qc}" + sfx)
                P.add("vector",
                      (lambda p_, qc_: (lambda e: e.tensor_copy(
                          out=AP(arena_bt, o_attnN // 2 + 2048 * p_ + 512 * qc_ + 64 * AB_ROW,
                                 [[AB_ROW, 64], [1, 512]]),
                          in_=ps_b[0:64, :])))(p, qc),
                      reads=[b_pb], writes=[b_attnN[aidx]], tag=f"evb{p}_{qc}" + sfx)
                if qc < 3:
                    P.add("vector",
                          (lambda qc_: (lambda e: e.tensor_copy(
                              out=rowz[ZR(qc_):ZR(qc_) + 1, 0:512],
                              in_=ps_a[64:65, 0:512])))(qc),
                          reads=[b_pa], writes=[b_zq[qc]], tag=f"zra{p}_{qc}" + sfx)
                    P.add("vector",
                          (lambda qc_: (lambda e: e.tensor_copy(
                              out=rowz[ZR(qc_):ZR(qc_) + 1, 512:1024],
                              in_=ps_b[64:65, 0:512])))(qc),
                          reads=[b_pb], writes=[b_zq[qc]], tag=f"zrb{p}_{qc}" + sfx)
                if p + 1 < NPAIR:
                    interleave_unit(qc, "q")

            for qc in range(QC):
                zi = 0 if qc == 3 else qc
                if qc == 3:
                    P.add("vector", lambda e: e.tensor_copy(
                              out=rowz[0:1, 0:512], in_=ps_a[64:65, 0:512]),
                          reads=[b_pa], writes=[b_zq[0]], tag=f"zra{p}_3" + sfx)
                    P.add("vector", lambda e: e.tensor_copy(
                              out=rowz[0:1, 512:1024], in_=ps_b[64:65, 0:512]),
                          reads=[b_pb], writes=[b_zq[0]], tag=f"zrb{p}_3" + sfx)
                P.add("scalar",
                      (lambda qc_: (lambda e: e.activation(
                          out=rowz[ZR(qc_):ZR(qc_) + 1, :],
                          in_=rowz[ZR(qc_):ZR(qc_) + 1, :],
                          func=AF.Ln)))(qc),
                      reads=[b_zq[zi]], writes=[b_zq[zi]], tag=f"zln{p}_{qc}" + sfx)
                P.add("scalar",
                      (lambda qc_: (lambda e: e.activation(
                          out=rowzb[ZR(qc_):ZR(qc_) + 1, :],
                          in_=rowz[ZR(qc_):ZR(qc_) + 1, :],
                          func=AF.Exp, scale=-1.0)))(qc),
                      reads=[b_zq[zi]], writes=[b_zrecb], tag=f"zexp{p}_{qc}" + sfx)
                aidx = p * QC + qc
                P.add("tensor",
                      (lambda qc_: (lambda e: e.matmul(
                          ps_s[0][:, 0:512], selp[ZR(qc_):ZR(qc_) + 1, 0:128],
                          rowzb[ZR(qc_):ZR(qc_) + 1, 0:512],
                          start=True, stop=False, skip_group_check=True)))(qc),
                      reads=[b_zrecb, b_selp], writes=[b_s[0][0]], tag=f"zbcA{p}_{qc}" + sfx)
                P.add("tensor",
                      (lambda qc_: (lambda e: e.matmul(
                          ps_s[0][:, 0:512], selp[ZR(qc_):ZR(qc_) + 1, 128:256],
                          rowzb[ZR(qc_):ZR(qc_) + 1, 512:1024],
                          start=False, stop=True, skip_group_check=True)))(qc),
                      reads=[b_zrecb, b_selp], writes=[b_s[0][0]], tag=f"zbcB{p}_{qc}" + sfx)
                P.add("vector", lambda e: e.tensor_copy(out=zbc[:], in_=ps_s[0][:, 0:512]),
                      reads=[b_s[0][0]], writes=[b_zbc], tag=f"zbce{p}_{qc}" + sfx)
                P.add("vector",
                      (lambda p_, qc_: (lambda e: e.tensor_mul(
                          out=AP(arena_bt, o_attnN // 2 + 2048 * p_ + 512 * qc_,
                                 [[AB_ROW, 128], [1, 512]]),
                          in0=AP(arena_bt, o_attnN // 2 + 2048 * p_ + 512 * qc_,
                                 [[AB_ROW, 128], [1, 512]]),
                          in1=zbc[:])))(p, qc),
                      reads=[b_zbc, b_attnN[aidx]], writes=[b_attnN[aidx]],
                      tag=f"nrm{p}_{qc}" + sfx)

            if p + 1 < NPAIR:
                emit_norms(p + 1)

        cS2, bS2 = stat_col("l2sum")
        cQ2, bQ2 = stat_col("l2ssq")
        cA2, bA2 = stat_col("A2")
        cB2, bB2 = stat_col("B2")
        cSo, bSo = stat_col("losum", 2)
        cQo, bQo = stat_col("lossq", 2)
        cAo, bAo = stat_col("Ao", 2)
        cBo, bBo = stat_col("Bo", 2)
        for t in range(TT):
            sl, sb2 = ps_s[t % 2], b_s[t % 2]

            def mk_oproj(t_, sl_):
                def f(e):
                    r = None
                    for pp in range(NPAIR):
                        for h2 in range(2):
                            r = e.matmul(
                                sl_[:, 512 * h2:512 * h2 + 512],
                                AP(arena_bt, o_attnN // 2 + 2048 * pp + 128 * t_,
                                   [[AB_ROW, 128], [1, 128]]),
                                AP(arena_bt, o_owt // 2 + 1024 * pp + 512 * h2,
                                   [[AB_ROW, 128], [1, 512]]),
                                start=(pp == 0), stop=(pp == NPAIR - 1),
                                skip_group_check=True)
                    return r
                return f
            P.add("tensor", mk_oproj(t, sl),
                  reads=[b_attnN[pp * QC + t // 4] for pp in range(NPAIR)] + list(b_owt),
                  writes=[sb2[0], sb2[1]], tag=f"oprj{t}" + sfx)
            ln_stats(sl[:, 0:1024], cSo, cQo, [sb2[0], sb2[1]], bSo, bQo,
                     tag=f"o{t}" + sfx)
            ln_rowmath(cSo, cQo, cAo, cBo, bSo, bQo, bAo, bBo, 1, tag=f"o{t}" + sfx)
            P.add("scalar",
                  (lambda t_: (lambda e: e.activation(
                      out=stream[:, 1024 * t_:1024 * (t_ + 1)],
                      in_=stream[:, 1024 * t_:1024 * (t_ + 1)], func=AF.Identity,
                      bias=stats[:, cBo:cBo + 1])))(t),
                  reads=[bBo, b_stream[t]], writes=[b_stream[t]], tag=f"oresb{t}" + sfx)
            P.add("vector",
                  (lambda t_, sl_: (lambda e: e.scalar_tensor_tensor(
                      out=stream[:, 1024 * t_:1024 * (t_ + 1)],
                      in0=sl_[:, 0:1024], scalar=stats[:, cAo:cAo + 1],
                      in1=stream[:, 1024 * t_:1024 * (t_ + 1)],
                      op0=ALU.mult, op1=ALU.add)))(t, sl),
                  reads=[sb2[0], sb2[1], bAo, b_stream[t]], writes=[b_stream[t]],
                  tag=f"ores{t}" + sfx)
            ln_stats(stream[:, 1024 * t:1024 * (t + 1)], cS2 + t, cQ2 + t,
                     [b_stream[t]], bS2, bQ2, tag=f"2_{t}" + sfx)
            ln_rowmath(cS2 + t, cQ2 + t, cA2 + t, cB2 + t, bS2, bQ2, bA2, bB2, 1,
                       tag=f"2_{t}" + sfx)

        arena.reset()
        o_w1, b_w1 = arena.alloc("w1", 8192 * 2, nbufs=1)
        o_w2, b_w2 = arena.alloc("w2", 4096 * 2, nbufs=1)
        o_w3, b_w3 = arena.alloc("w3", 1024 * 2, nbufs=1)
        o_gw1, b_gw1 = arena.alloc("gw1", 4096 * 2, nbufs=1)
        o_gw2, b_gw2 = arena.alloc("gw2", 1024 * 2, nbufs=1)
        o_gw3, b_gw3 = arena.alloc("gw3", 4, nbufs=1)
        o_state, b_state = arena.alloc("state", 8 * 1024 * 4, nbufs=8)
        o_outT, b_outTh = arena.alloc("outT", 8192 * 2, nbufs=2)
        o_h1T, b_h1T = arena.alloc("h1T", 4096 * 2, nbufs=8)
        o_h2T, b_h2T = arena.alloc("h2T", 2048 * 2, nbufs=4)
        o_h3T, b_h3T = arena.alloc("h3T", 2048 * 2, nbufs=2)
        o_twt, b_twt = arena.alloc("twt", 2 * 1024 * 4, nbufs=8)

        dma_to(ab(o_w1, 8192), dw1.ap(), eng="gpsimd", w=[b_w1], tag="ldw1" + sfx)
        dma_to(ab(o_w2, 4096), dw2.ap(), eng="gpsimd", w=[b_w2], tag="ldw2" + sfx)
        dma_to(ab(o_w3, 1024), dw3.ap(), eng="gpsimd", w=[b_w3], tag="ldw3" + sfx)
        dma_to(ab(o_gw1, 4096), dgw1.ap(), eng="gpsimd", w=[b_gw1], tag="ldgw1" + sfx)
        dma_to(ab(o_gw2, 1024), dgw2.ap(), eng="gpsimd", w=[b_gw2], tag="ldgw2" + sfx)
        dma_to(ab(o_gw3, 2), dgw3.ap(), eng="gpsimd", w=[b_gw3], tag="ldgw3" + sfx)

        def emit_gcnet():
            P.add("tensor", lambda e: e.matmul(ps_s[0][0:1, 0:16], ones_f128c[:],
                                                stats[:, cB2:cB2 + 16],
                                                start=True, stop=True,
                                                skip_group_check=True),
                  reads=[bB2, b_ones], writes=[b_s[0][0]], tag="b2sum_mm" + sfx)
            P.add("vector", lambda e: e.reduce_sum(out=grow[0:1, 5:6], in_=ps_s[0][0:1, 0:16],
                                                   axis=AXX),
                  reads=[b_s[0][0]], writes=[b_grow], tag="b2sum_red" + sfx)
            P.add("vector", lambda e: e.tensor_scalar_mul(out=grow[0:1, 6:7],
                                                          in0=grow[0:1, 5:6], scalar1=1.0 / S),
                  reads=[b_grow], writes=[b_grow], tag="b2sum_sc" + sfx)
            for t in range(TT):
                for h2 in range(2):
                    psg, bpsg = ((ps_a, b_pa), (ps_b, b_pb))[h2]
                    P.add("tensor",
                          (lambda t_, h_, ps_: (lambda e: e.matmul(
                              ps_[0:1, :], stats[:, cA2 + t_:cA2 + t_ + 1],
                              stream[:, 1024 * t_ + 512 * h_:1024 * t_ + 512 * h_ + 512],
                              start=(t_ == 0), stop=(t_ == TT - 1),
                              skip_group_check=True)))(t, h2, psg),
                          reads=[bA2, b_stream[t]], writes=[bpsg], tag=f"gc{t}_{h2}" + sfx)

            P.add("scalar", lambda e: e.activation(out=gcrowb[0:1, 0:512], in_=ps_a[0:1, :],
                                                   func=AF.Identity, scale=1.0 / S,
                                                   bias=grow[0:1, 6:7]),
                  reads=[b_pa, b_grow], writes=[b_gc], tag="gcev0" + sfx)
            P.add("scalar", lambda e: e.activation(out=gcrowb[0:1, 512:1024], in_=ps_b[0:1, :],
                                                   func=AF.Identity, scale=1.0 / S,
                                                   bias=grow[0:1, 6:7]),
                  reads=[b_pb, b_grow], writes=[b_gc], tag="gcev1" + sfx)

            def mk_gctr():
                def f(e):
                    r = None
                    for fb in range(8):
                        r = e.transpose(ps_t[0][:, 2 * fb:2 * fb + 1],
                                        gcrowb[0:1, 128 * fb:128 * fb + 128], ident[0:1, 0:1])
                    return r
                return f
            P.add("tensor", mk_gctr(), reads=[b_gc, b_ident], writes=[b_pt[0]], tag="gctr" + sfx)
            P.add("vector", lambda e: e.tensor_copy(
                out=gcT[:], in_=AP(ps_t[0], 0, [[1024, 128], [2, 8], [1, 1]])),
                  reads=[b_pt[0]], writes=[b_gc], tag="gctev" + sfx)

            def mk_g1(o_):
                def f(e):
                    r = None
                    for fb in range(8):
                        r = e.matmul(ps_a[:, o_:o_ + 1],
                                     ab(o_gw1 + (o_ * 8 + fb) * 256, 128),
                                     gcT[:, fb:fb + 1], start=(fb == 0), stop=(fb == 7),
                                     skip_group_check=True)
                    return r
                return f
            for o in range(4):
                P.add("tensor", mk_g1(o), reads=[b_gw1, b_gc], writes=[b_pa],
                      tag=f"g1_{o}" + sfx)
            for o in range(4):
                P.add("scalar",
                      (lambda o_: (lambda e: e.activation(
                          out=hg1[:, o_:o_ + 1], in_=ps_a[:, o_:o_ + 1], func=AF.Gelu,
                          bias=gb1s[:, o_:o_ + 1])))(o),
                      reads=[b_pa, b_bias], writes=[b_gc], tag=f"g1g{o}" + sfx)

            def mk_g2(o_):
                def f(e):
                    r = None
                    for fb in range(4):
                        r = e.matmul(ps_b[:, o_:o_ + 1],
                                     ab(o_gw2 + (o_ * 4 + fb) * 256, 128),
                                     hg1[:, fb:fb + 1], start=(fb == 0), stop=(fb == 3),
                                     skip_group_check=True)
                    return r
                return f
            for o in range(2):
                P.add("tensor", mk_g2(o), reads=[b_gw2, b_gc], writes=[b_pb],
                      tag=f"g2_{o}" + sfx)
            for o in range(2):
                P.add("scalar",
                      (lambda o_: (lambda e: e.activation(
                          out=hg2[:, o_:o_ + 1], in_=ps_b[:, o_:o_ + 1], func=AF.Gelu,
                          bias=gb2s[:, o_:o_ + 1])))(o),
                      reads=[b_pb, b_bias], writes=[b_gc], tag=f"g2g{o}" + sfx)

            def mk_g3():
                def f(e):
                    r = None
                    for fb in range(2):
                        r = e.matmul(ps_a[0:1, 8:9], ab(o_gw3 + fb * 2, 1),
                                     hg2[:, fb:fb + 1], start=(fb == 0), stop=(fb == 1),
                                     skip_group_check=True)
                    return r
                return f
            P.add("tensor", mk_g3(), reads=[b_gw3, b_gc], writes=[b_pa], tag="g3" + sfx)
            P.add("scalar", lambda e: e.activation(out=grow[0:1, 0:1], in_=ps_a[0:1, 8:9],
                                                   func=AF.Tanh, scale=0.5,
                                                   bias=gb3h[0:1, 0:1]),
                  reads=[b_pa, b_bias], writes=[b_grow], tag="gf" + sfx)
            P.add("vector", lambda e: e.tensor_scalar(out=grow[0:1, 1:2], in0=grow[0:1, 0:1],
                                                      scalar1=0.35, scalar2=-0.15,
                                                      op0=ALU.mult, op1=ALU.add),
                  reads=[b_grow], writes=[b_grow])
            P.add("vector", lambda e: e.tensor_mul(out=grow[0:1, 2:3], in0=grow[0:1, 1:2],
                                                   in1=ars[0:1, 0:1]),
                  reads=[b_grow, b_ar], writes=[b_grow])
            P.add("vector", lambda e: e.tensor_scalar(out=grow[0:1, 3:4], in0=grow[0:1, 2:3],
                                                      scalar1=2.0, scalar2=0.1,
                                                      op0=ALU.mult, op1=ALU.add),
                  reads=[b_grow], writes=[b_grow], tag="s1" + sfx)
            P.add("vector", lambda e: e.tensor_scalar_mul(out=grow[0:1, 4:5], in0=ars[0:1, 0:1],
                                                          scalar1=0.3),
                  reads=[b_ar], writes=[b_grow], tag="s2" + sfx)
            P.add("vector", lambda e: e.tensor_add(out=grow[0:1, 3:4], in0=grow[0:1, 3:4],
                                                   in1=grow[0:1, 4:5]),
                  reads=[b_grow], writes=[b_grow], tag="s1p" + sfx)
            P.add("tensor", lambda e: e.matmul(ps_a[:, 16:18], ones_f[:], grow[0:1, 3:5],
                                               start=True, stop=True, skip_group_check=True),
                  reads=[b_grow, b_ones], writes=[b_pa], tag="s12bc" + sfx)
            P.add("vector", lambda e: e.tensor_copy(out=sc12[:], in_=ps_a[:, 16:18]),
                  reads=[b_pa], writes=[b_sc], tag="s12ev" + sfx)



        cSf, bSf = stat_col("lfsum", 8)
        cQf, bQf = stat_col("lfssq", 8)
        cAf, bAf = stat_col("Af", 8)
        cBf, bBf = stat_col("Bf", 8)
        def emit_st(ch, j):
            t = NJ * ch + j
            P.add("vector",
                  (lambda t_, j_: (lambda e: e.tensor_scalar(
                      out=af(o_state + j_ * 4096, 1024),
                      in0=stream[:, 1024 * t_:1024 * (t_ + 1)],
                      scalar1=stats[:, cA2 + t_:cA2 + t_ + 1],
                      scalar2=stats[:, cB2 + t_:cB2 + t_ + 1],
                      op0=ALU.mult, op1=ALU.add)))(t, j),
                  reads=[b_stream[t], bA2, bB2], writes=[b_state[j]],
                  tag=f"st{ch}_{j}" + sfx)

        for ch in range(2):
            if ch == 0:
                for j in range(NJ):
                    emit_st(0, j)

            def castrf(it, th):
                for j in range(4 * th, 4 * th + 4):
                    c4 = j % 4
                    P.add("vector",
                          (lambda j_, c_: (lambda e: e.tensor_copy(
                              out=castb[:, 1024 * c_:1024 * (c_ + 1)],
                              in_=af(o_state + j_ * 4096, 1024))))(j, c4),
                          reads=[b_state[j]], writes=[b_castb[c4]],
                          tag=f"cst{ch}_{it}_{j}" + sfx)
                    pt, bpt = ps_t[j % 2], b_pt[j % 2]

                    def mk_trf(c_, pt_):
                        def f(e):
                            r = None
                            for fb in range(8):
                                r = e.transpose(pt_[:, 128 * fb:128 * fb + 128],
                                                castb[:, 1024 * c_ + 128 * fb:
                                                      1024 * c_ + 128 * fb + 128],
                                                ident[:])
                            return r
                        return f
                    P.add("tensor", mk_trf(c4, pt), reads=[b_castb[c4], b_ident],
                          writes=[bpt], tag=f"trf{ch}_{it}_{j}" + sfx)
                    dstap = AP(arena_bt, o_outT // 2 + 128 * (j % 4) + 512 * th,
                               [[AB_ROW, 128], [1024, 8], [1, 128]])
                    P.add("scalar",
                          (lambda pt_, d0: (lambda e: e.activation(
                              out=d0, in_=pt_[:, 0:1024], func=AF.Identity)))(pt, dstap),
                          reads=[bpt], writes=[b_outTh[th]], tag=f"tev{ch}_{it}_{j}" + sfx)

            castrf(0, 0)
            if ch == 0:
                emit_gcnet()
            for it in range(NITER):
                for th in range(2):
                    for o in range(8):
                        sl, sb2 = ps_s[(o // 2) % 2], b_s[(o // 2) % 2]
                        half = o % 2
                        pap = sl[:, 512 * half:512 * half + 512]

                        def mk_w1(o_, th_, pap_):
                            def f(e):
                                r = None
                                for fb in range(8):
                                    r = e.matmul(pap_, ab(o_w1 + (o_ * 8 + fb) * 256, 128),
                                                 ab(o_outT + fb * 2048 + th_ * 1024, 512),
                                                 start=(fb == 0), stop=(fb == 7),
                                                 skip_group_check=True)
                                return r
                            return f
                        P.add("tensor", mk_w1(o, th, pap), reads=[b_w1, b_outTh[th]],
                              writes=[sb2[half]], tag=f"w1_{ch}_{it}_{th}_{o}" + sfx)
                        P.add("scalar",
                              (lambda o_, th_, pap_: (lambda e: e.activation(
                                  out=ab(o_h1T + o_ * 1024, 512), in_=pap_,
                                  func=AF.Gelu, bias=b1s[:, o_:o_ + 1])))(o, th, pap),
                              reads=[sb2[half], b_bias], writes=[b_h1T[o]],
                              tag=f"h1_{ch}_{it}_{th}_{o}" + sfx)
                    if th == 0:
                        castrf(it, 1)
                    elif it + 1 < NITER:
                        castrf(it + 1, 0)
                    for o in range(4):
                        sl, sb2 = ps_s[o % 2], b_s[o % 2]
                        pap = sl[:, 0:512]

                        def mk_w2(o_, th_, pap_):
                            def f(e):
                                r = None
                                for fb in range(8):
                                    r = e.matmul(pap_, ab(o_w2 + (o_ * 8 + fb) * 256, 128),
                                                 ab(o_h1T + fb * 1024, 512),
                                                 start=(fb == 0), stop=(fb == 7),
                                                 skip_group_check=True)
                                return r
                            return f
                        P.add("tensor", mk_w2(o, th, pap), reads=[b_w2] + list(b_h1T),
                              writes=[sb2[0]], tag=f"w2_{ch}_{it}_{th}_{o}" + sfx)
                        P.add("scalar",
                              (lambda o_, th_, pap_: (lambda e: e.activation(
                                  out=ab(o_h2T + o_ * 1024, 512), in_=pap_,
                                  func=AF.Gelu, bias=b2s[:, o_:o_ + 1])))(o, th, pap),
                              reads=[sb2[0], b_bias], writes=[b_h2T[o]],
                              tag=f"h2_{ch}_{it}_{th}_{o}" + sfx)
                    for o in range(2):
                        sl, sb2 = ps_s[o % 2], b_s[o % 2]
                        pap = sl[:, 512:1024]

                        def mk_w3(o_, th_, pap_):
                            def f(e):
                                r = None
                                for fb in range(4):
                                    r = e.matmul(pap_, ab(o_w3 + (o_ * 4 + fb) * 256, 128),
                                                 ab(o_h2T + fb * 1024, 512),
                                                 start=(fb == 0), stop=(fb == 3),
                                                 skip_group_check=True)
                                return r
                            return f
                        P.add("tensor", mk_w3(o, th, pap), reads=[b_w3] + list(b_h2T),
                              writes=[sb2[1]], tag=f"w3_{ch}_{it}_{th}_{o}" + sfx)
                        P.add("scalar",
                              (lambda o_, th_, pap_: (lambda e: e.activation(
                                  out=ab(o_h3T + o_ * 2048 + th_ * 1024, 512), in_=pap_,
                                  func=AF.Gelu, bias=b3s[:, o_:o_ + 1])))(o, th, pap),
                              reads=[sb2[1], b_bias], writes=[b_h3T[o]],
                              tag=f"h3_{ch}_{it}_{th}_{o}" + sfx)
                    for j in range(4 * th, 4 * th + 4):
                        def mk_tf(j_):
                            def f(e):
                                r = None
                                for fb in range(2):
                                    r = e.matmul(ps_a[:, 24 + j_:25 + j_],
                                                 AP(arena_bt,
                                                    o_h3T // 2 + fb * 1024 + 128 * (j_ % 4)
                                                    + (j_ // 4) * 512,
                                                    [[AB_ROW, 128], [1, 128]]),
                                                 w4t[:, fb:fb + 1], start=(fb == 0),
                                                 stop=(fb == 1), skip_group_check=True)
                                return r
                            return f
                        P.add("tensor", mk_tf(j), reads=list(b_h3T) + [b_w4],
                              writes=[b_pa], tag=f"tf{ch}_{it}_{j}" + sfx)
                    P.add("scalar",
                          (lambda th_: (lambda e: e.activation(
                              out=sigt[:, 4 * th_:4 * th_ + 4],
                              in_=ps_a[:, 24 + 4 * th_:28 + 4 * th_],
                              func=AF.Tanh, scale=0.5, bias=b4h[:])))(th),
                          reads=[b_pa, b_b4], writes=[b_sig], tag=f"sig{ch}_{it}_{th}" + sfx)
                    P.add("vector",
                          (lambda th_: (lambda e: e.tensor_scalar(
                              out=cpl[:, 4 * th_:4 * th_ + 4],
                              in0=sigt[:, 4 * th_:4 * th_ + 4],
                              scalar1=sc12[:, 1:2], scalar2=sc12[:, 0:1],
                              op0=ALU.mult, op1=ALU.add)))(th),
                          reads=[b_sig, b_sc], writes=[b_cpl], tag=f"cpl{ch}_{it}_{th}" + sfx)
                    P.add("vector",
                          (lambda th_: (lambda e: e.tensor_scalar_mul(
                              out=ncpl[:, 4 * th_:4 * th_ + 4],
                              in0=cpl[:, 4 * th_:4 * th_ + 4], scalar1=-1.0)))(th),
                          reads=[b_cpl], writes=[b_cpl], tag=f"ncpl{ch}_{it}_{th}" + sfx)
                    for j in range(4 * th, 4 * th + 4):
                        half = j % 2
                        for sv in range(2):
                            P.add("scalar",
                                  (lambda j_, s_, h_: (lambda e: e.tensor_copy(
                                      out=af(o_twt + h_ * 4096 + 1024 * s_, 256),
                                      in0=af(o_state + j_ * 4096 + 1024 * s_, 256))))(
                                          j, sv, half) if False else
                                  (lambda j_, s_, h_: (lambda e: e.activation(
                                      out=af(o_twt + h_ * 4096 + 1024 * s_, 256),
                                      in_=af(o_state + j_ * 4096 + 1024 * s_, 256),
                                      func=AF.Identity)))(j, sv, half),
                                  reads=[b_state[j]], writes=[b_twt[half * 4 + sv]],
                                  tag=f"twsv{ch}_{it}_{j}_{sv}" + sfx)
                        for (dc, s_src, neg, from_tmp) in (
                                (0, 2, False, False), (1, 3, True, False),
                                (2, 0, True, True), (3, 1, False, True)):
                            src_ap = (af(o_twt + (j % 2) * 4096 + 1024 * s_src, 256)
                                      if from_tmp else
                                      af(o_state + j * 4096 + 1024 * s_src, 256))
                            rbufs = ([b_twt[(j % 2) * 4 + s_src]] if from_tmp else [])                                 + [b_state[j], b_cpl]
                            P.add("vector",
                                  (lambda j_, d_, n_, sap: (lambda e: e.scalar_tensor_tensor(
                                      out=af(o_state + j_ * 4096 + 1024 * d_, 256),
                                      in0=sap, scalar=(ncpl if n_ else cpl)[:, j_:j_ + 1],
                                      in1=af(o_state + j_ * 4096 + 1024 * d_, 256),
                                      op0=ALU.mult, op1=ALU.add)))(j, dc, neg, src_ap),
                                  reads=rbufs, writes=[b_state[j]],
                                  tag=f"tw{ch}_{it}_{j}_{dc}" + sfx)

            for j in range(NJ):
                t = NJ * ch + j
                ln_stats(af(o_state + j * 4096, 1024), cSf + j, cQf + j,
                         [b_state[j]], bSf, bQf, tag=f"f{t}" + sfx)
            ln_rowmath(cSf, cQf, cAf, cBf, bSf, bQf, bAf, bBf, NJ, tag=f"f{ch}" + sfx)
            for j in range(NJ):
                t = NJ * ch + j
                P.add("scalar",
                      (lambda t_, j_: (lambda e: e.activation(
                          out=stream[:, 1024 * t_:1024 * (t_ + 1)],
                          in_=stream[:, 1024 * t_:1024 * (t_ + 1)], func=AF.Identity,
                          bias=stats[:, cBf + j_:cBf + j_ + 1])))(t, j),
                      reads=[bBf, b_stream[t]], writes=[b_stream[t]], tag=f"fapb{t}" + sfx)
                P.add("vector",
                      (lambda t_, j_: (lambda e: e.scalar_tensor_tensor(
                          out=stream[:, 1024 * t_:1024 * (t_ + 1)],
                          in0=af(o_state + j_ * 4096, 1024),
                          scalar=stats[:, cAf + j_:cAf + j_ + 1],
                          in1=stream[:, 1024 * t_:1024 * (t_ + 1)],
                          op0=ALU.mult, op1=ALU.add)))(t, j),
                      reads=[b_state[j], bAf, b_stream[t]], writes=[b_stream[t]],
                      tag=f"fres{t}" + sfx)
                outap = dout.ap()[s].rearrange("(t p) c -> p t c", p=128)[:, t:t + 1, :]
                P.add("sync",
                      (lambda t_, o0: (lambda e: e.dma_start(
                          out=o0, in_=stream[:, 1024 * t_:1024 * (t_ + 1)])))(t, outap),
                      reads=[b_stream[t]], writes=[b_stored[s * TT + t]], dma=True,
                      tag=f"stout{t}" + sfx)
                if ch == 0:
                    emit_st(1, j)

    P.add("sync", lambda e: e.nop(), reads=b_stored, tag="fin")

    P.prepare()

    @block.tensor
    def _(eng):
        P.emit_engine("tensor", eng, sems)

    @block.vector
    def _(eng):
        P.emit_engine("vector", eng, sems)

    @block.scalar
    def _(eng):
        P.emit_engine("scalar", eng, sems)

    @block.sync
    def _(eng):
        P.emit_engine("sync", eng, sems)

    @block.gpsimd
    def _(eng):
        P.emit_engine("gpsimd", eng, sems)

    es.close()
    return nc



def _host_inputs(inp):
    perm = np.array([(j % 64) * 16 + j // 64 for j in range(D)], dtype=np.int64)
    qw = np.ascontiguousarray(np.asarray(inp["L0_attn_qw"], np.float32))[:, perm]
    kw = np.ascontiguousarray(np.asarray(inp["L0_attn_kw"], np.float32))[:, perm]
    vw = np.ascontiguousarray(np.asarray(inp["L0_attn_vw"], np.float32))[:, perm]
    ow = np.ascontiguousarray(np.asarray(inp["L0_attn_ow"], np.float32))[perm, :]

    def tile_pair(W):
        A = W.reshape(8, 128, 8, 128)
        return np.ascontiguousarray(A.transpose(2, 1, 0, 3).reshape(8, 128, 1024))

    wqkv = np.stack([tile_pair(qw), tile_pair(kw), tile_pair(vw)])
    owt = np.ascontiguousarray(ow.reshape(8, 128, 1024).transpose(1, 0, 2).reshape(128, 8192))

    def tile_of(W, no, nf):
        A = np.asarray(W, np.float32).reshape(nf, 128, no, 128)
        return np.ascontiguousarray(A.transpose(1, 2, 0, 3).reshape(128, no * nf * 128))

    w1t = tile_of(inp["L0_ffn_cn_w1"], 8, 8)
    w2t = tile_of(inp["L0_ffn_cn_w2"], 4, 8)
    w3t = tile_of(inp["L0_ffn_cn_w3"], 2, 4)
    w4t = np.ascontiguousarray(np.asarray(inp["L0_ffn_cn_w4"], np.float32).reshape(2, 128).T)
    gw1t = tile_of(inp["L0_ffn_gc_w1"], 4, 8)
    gw2t = tile_of(inp["L0_ffn_gc_w2"], 2, 4)
    gw3t = np.ascontiguousarray(np.asarray(inp["L0_ffn_gc_w3"], np.float32).reshape(2, 128).T)

    def bias_t(b, no):
        return np.ascontiguousarray(np.asarray(b, np.float32).reshape(no, 128).T)

    f32 = np.float32
    ident = np.eye(128, dtype=f32).astype(md.bfloat16)
    m = np.zeros((128, 4, 512), f32)
    pk = np.arange(128)[:, None]
    fq = np.arange(512)[None, :]
    for i, off in enumerate((0, 128, 256, 384)):
        m[:, i, :] = (fq >= pk + off).astype(f32)
    masksv = np.ascontiguousarray(m.reshape(128, 2048)).astype(md.bfloat16)
    selpair = np.zeros((128, 256), f32)
    selpair[:, 0:64] = 1.0
    selpair[:, 192:256] = 1.0
    nb = np.zeros((128, 2), f32)
    nb[0:64, 0] = 1.0
    nb[64:128, 1] = 1.0

    return dict(
        wqkv=wqkv.astype(f32), owt=owt.astype(f32), w1t=w1t, w2t=w2t, w3t=w3t,
        w4t=w4t, gw1t=gw1t, gw2t=gw2t, gw3t=gw3t,
        b1=bias_t(inp["L0_ffn_cn_b1"], 8), b2=bias_t(inp["L0_ffn_cn_b2"], 4),
        b3=bias_t(inp["L0_ffn_cn_b3"], 2),
        b4=np.asarray(inp["L0_ffn_cn_b4"], f32).reshape(1, 1),
        gb1=bias_t(inp["L0_ffn_gc_b1"], 4), gb2=bias_t(inp["L0_ffn_gc_b2"], 2),
        gb3=np.asarray(inp["L0_ffn_gc_b3"], f32).reshape(1, 1),
        ar=np.asarray(inp["L0_ffn_adaptive_range"], f32).reshape(1, 1),
        ident=ident, masks=masksv, ones_f=np.ones((1, 128), f32),
        selp=selpair.astype(md.bfloat16), nb128=nb.astype(md.bfloat16),
        ones128=np.ones((128, 1), f32),
        onec=np.ones((128, 1), f32).astype(md.bfloat16),
    )


def kernel(**inputs):
    x = np.ascontiguousarray(np.asarray(inputs["L0_attn_x"], dtype=np.float32))
    common = _host_inputs(inputs)
    nc = build_nc()
    in_maps = []
    for c in range(NCORES):
        m = dict(common)
        m["x"] = np.ascontiguousarray(x[2 * c:2 * c + 2])
        in_maps.append(m)
    res = run_bass_kernel_spmd(nc, in_maps, core_ids=list(range(NCORES)))
    out = np.concatenate([np.asarray(r["out"], dtype=np.float32) for r in res.results],
                         axis=0)
    return out


# version 23
# speedup vs baseline: 1.1259x; 1.0156x over previous
import numpy as np
import ml_dtypes as md
import concourse.bass as bass
import concourse.mybir as mybir
from concourse import dve_ops
from concourse.bass_utils import run_bass_kernel_spmd

F32 = mybir.dt.float32
BF16 = mybir.dt.bfloat16
AF = mybir.ActivationFunctionType
ALU = mybir.AluOpType
AXX = mybir.AxisListType.X
AP = bass.AP

NCORES = 8
SEQS, S, D = 2, 2048, 1024
NPAIR, TT, QC = 8, 16, 4
NITER = 3
NJ, NTF = 8, 8
OPLIMIT = [None]


ENGS = ("tensor", "vector", "scalar", "sync", "gpsimd")
DRAIN_ENGS = frozenset(("vector", "scalar"))
SELFSEM_ENGS = frozenset(("sync", "gpsimd"))


class Buf:
    __slots__ = ("name", "w", "rs", "psum")

    def __init__(self, name="b", init_deps=(), psum=False):
        self.name, self.w, self.rs = name, None, list(init_deps)
        self.psum = psum


class Op:
    __slots__ = ("eng", "idx", "fn", "raw", "war", "inc", "flag", "semval", "tag", "skey")


class Prog:
    def __init__(self):
        self.ops = {e: [] for e in ENGS}

    def add(self, eng, fn, reads=(), writes=(), dma=False, tag=""):
        if OPLIMIT[0] is not None:
            self.total = getattr(self, "total", 0) + 1
            if self.total > OPLIMIT[0]:
                return ("dropped", -1)
        idx = len(self.ops[eng])
        tok = (eng, idx)
        op = Op()
        op.eng, op.idx, op.fn, op.tag = eng, idx, fn, tag
        op.inc = 16 if dma else 1
        op.flag = False
        op.semval = 0
        raw, war = set(), set()
        for b in reads:
            if b.w is not None:
                raw.add(b.w)
            if b.psum and eng in ("vector", "scalar"):
                for rtok in b.rs:
                    if rtok[0] != eng and rtok[0] in ("vector", "scalar"):
                        raw.add(rtok)
        for b in writes:
            if b.w is not None:
                raw.add(b.w)
            war.update(b.rs)
        raw.discard(tok)
        war.discard(tok)
        op.raw, op.war = raw, war
        self.ops[eng].append(op)
        for b in reads:
            b.rs.append(tok)
        for b in writes:
            b.w = tok
            b.rs = []
        return tok

    def prepare(self):
        for e in ENGS:
            for op in self.ops[e]:
                op.skey = (e + "_c") if (e == "gpsimd" and op.inc == 1) else e
                if op.inc == 16:
                    op.flag = True
                for (de, di) in list(op.raw) + list(op.war):
                    if de != e or de in SELFSEM_ENGS:
                        self.ops[de][di].flag = True
        for e in ENGS:
            cnt = {}
            for op in self.ops[e]:
                if op.flag:
                    cnt[op.skey] = cnt.get(op.skey, 0) + op.inc
                op.semval = cnt.get(op.skey, 0)
        self.collect_waitvals()

    def _op_needs(self, ename, op, last_drain):
        needs = {}
        drain_upto = -1
        for kind, deps in (("raw", op.raw), ("war", op.war)):
            for (de, di) in deps:
                if de == ename and de not in SELFSEM_ENGS:
                    if di > last_drain and ename in DRAIN_ENGS:
                        drain_upto = max(drain_upto, di)
                    continue
                dep = self.ops[de][di]
                needs[dep.skey] = max(needs.get(dep.skey, 0), dep.semval)
        return needs, drain_upto

    def collect_waitvals(self):
        self.waitvals = {}
        for ename in ENGS:
            seen = {}
            last_drain = -1
            for op in self.ops[ename]:
                needs, drain_upto = self._op_needs(ename, op, last_drain)
                if drain_upto >= 0:
                    last_drain = op.idx - 1
                for dk, v in needs.items():
                    if v > seen.get(dk, 0):
                        self.waitvals.setdefault(dk, set()).add(v)
                        seen[dk] = v

    def emit_engine(self, ename, eng, sems):
        seen = {}
        last_drain = -1
        qval = 0
        wv = self.waitvals.get(ename, ()) if ename in SELFSEM_ENGS else ()
        for op in self.ops[ename]:
            needs, drain_upto = self._op_needs(ename, op, last_drain)
            if drain_upto >= 0:
                eng.drain()
                last_drain = op.idx - 1
            if op.flag and op.inc == 16 and qval in wv and qval > seen.get(ename, 0):
                needs[ename] = max(needs.get(ename, 0), qval)
            for dk in sorted(needs):
                if needs[dk] > seen.get(dk, 0):
                    eng.wait_ge(sems[dk], needs[dk])
                    seen[dk] = needs[dk]
            inst = op.fn(eng)
            if op.flag:
                assert inst is not None, f"flagged op lacks inst: {ename}#{op.idx} {op.tag}"
                inst.then_inc(sems[op.skey], op.inc)
                if op.inc == 16:
                    qval = op.semval


class Arena:
    def __init__(self, nbytes):
        self.nbytes = nbytes
        self.active = []
        self.cur = 0
        self.peak = 0

    def reset(self):
        self.cur = 0

    def alloc(self, name, nbytes, nbufs=1):
        start = (self.cur + 63) & ~63
        end = start + nbytes
        assert end <= self.nbytes, f"arena overflow: {name} needs {end}"
        self.cur = end
        self.peak = max(self.peak, end)
        init, keep = [], []
        for (s0, e0, b) in self.active:
            if s0 < end and e0 > start:
                if b.w is not None:
                    init.append(b.w)
                init.extend(b.rs)
                if s0 < start:
                    keep.append((s0, start, b))
                if e0 > end:
                    keep.append((end, e0, b))
            else:
                keep.append((s0, e0, b))
        bufs = [Buf(f"{name}{i}", init) for i in range(nbufs)]
        for b in bufs:
            keep.append((start, end, b))
        self.active = keep
        return start, (bufs if nbufs > 1 else bufs[0])



def build_nc():
    nc = bass.Bass("TRN2", target_bir_lowering=False)

    dx = nc.dram_tensor("x", [SEQS, S, D], F32, kind="ExternalInput")
    dwqkv = nc.dram_tensor("wqkv", [3, NPAIR, 128, 1024], F32, kind="ExternalInput")
    dowt = nc.dram_tensor("owt", [128, 8192], F32, kind="ExternalInput")
    dw1 = nc.dram_tensor("w1t", [128, 8192], F32, kind="ExternalInput")
    dw2 = nc.dram_tensor("w2t", [128, 4096], F32, kind="ExternalInput")
    dw3 = nc.dram_tensor("w3t", [128, 1024], F32, kind="ExternalInput")
    dw4 = nc.dram_tensor("w4t", [128, 2], F32, kind="ExternalInput")
    dgw1 = nc.dram_tensor("gw1t", [128, 4096], F32, kind="ExternalInput")
    dgw2 = nc.dram_tensor("gw2t", [128, 1024], F32, kind="ExternalInput")
    dgw3 = nc.dram_tensor("gw3t", [128, 2], F32, kind="ExternalInput")
    db1 = nc.dram_tensor("b1", [128, 8], F32, kind="ExternalInput")
    db2 = nc.dram_tensor("b2", [128, 4], F32, kind="ExternalInput")
    db3 = nc.dram_tensor("b3", [128, 2], F32, kind="ExternalInput")
    db4 = nc.dram_tensor("b4", [1, 1], F32, kind="ExternalInput")
    dgb1 = nc.dram_tensor("gb1", [128, 4], F32, kind="ExternalInput")
    dgb2 = nc.dram_tensor("gb2", [128, 2], F32, kind="ExternalInput")
    dgb3 = nc.dram_tensor("gb3", [1, 1], F32, kind="ExternalInput")
    dar = nc.dram_tensor("ar", [1, 1], F32, kind="ExternalInput")
    dident = nc.dram_tensor("ident", [128, 128], BF16, kind="ExternalInput")
    dmasks = nc.dram_tensor("masks", [128, 2048], BF16, kind="ExternalInput")
    dones = nc.dram_tensor("ones_f", [1, 128], F32, kind="ExternalInput")
    dones128 = nc.dram_tensor("ones128", [128, 1], F32, kind="ExternalInput")
    dselp = nc.dram_tensor("selp", [128, 256], BF16, kind="ExternalInput")
    dnb = nc.dram_tensor("nb128", [128, 2], BF16, kind="ExternalInput")
    donec = nc.dram_tensor("onec", [128, 1], BF16, kind="ExternalInput")
    dout = nc.dram_tensor("out", [SEQS, S, D], F32, kind="ExternalOutput")

    from contextlib import ExitStack
    es = ExitStack()

    def sbt(name, shape, dt):
        return es.enter_context(nc.sbuf_tensor(name, shape, dt))

    stream = sbt("stream", [128, TT * 1024], F32)
    lnsc = sbt("lnsc", [128, 1024], F32)
    lnsc2 = sbt("lnsc2", [128, 1024], F32)
    castb = sbt("castb", [128, 4096], BF16)
    stats = sbt("stats", [128, 184], F32)
    masks = sbt("masks_s", [128, 2048], BF16)
    ident = sbt("ident_s", [128, 128], BF16)
    ones_f = sbt("ones_fs", [1, 128], F32)
    ones_f128c = sbt("ones_f128", [128, 1], F32)
    selp = sbt("selp_s", [128, 256], BF16)
    nb128 = sbt("nb128_s", [128, 2], BF16)
    onec = sbt("onec_s", [128, 1], BF16)
    eps = sbt("eps", [128, 1], F32)
    w4t = sbt("w4t_s", [128, 2], BF16)
    b1s = sbt("b1s", [128, 8], F32)
    b2s = sbt("b2s", [128, 4], F32)
    b3s = sbt("b3s", [128, 2], F32)
    b4bc = sbt("b4bc", [128, 1], F32)
    b4h = sbt("b4h", [128, 1], F32)
    gb3h = sbt("gb3h", [1, 1], F32)
    gb1s = sbt("gb1s", [128, 4], F32)
    gb2s = sbt("gb2s", [128, 2], F32)
    gb3s = sbt("gb3s", [1, 1], F32)
    ars = sbt("ars", [1, 1], F32)
    rowz = sbt("rowz", [128, 1024], F32)
    rowzb = sbt("rowzb", [128, 1024], BF16)
    zbc = sbt("zbc", [128, 512], BF16)
    negM = sbt("negM", [128, 2], F32)
    sc12 = sbt("sc12", [128, 2], F32)
    cpl = sbt("cpl", [128, 8], F32)
    ncpl = sbt("ncpl", [128, 8], F32)
    sigt = sbt("sigt", [128, 8], F32)
    gcrowb = sbt("gcrowb", [1, 1024], BF16)
    gcT = sbt("gcT", [128, 8], BF16)
    hg1 = sbt("hg1", [128, 4], BF16)
    hg2 = sbt("hg2", [128, 2], BF16)
    grow = sbt("grow", [1, 8], F32)
    normrow = sbt("normrow", [2, 8], F32)

    ARENA_F32 = 28688
    arena_t = sbt("arena", [128, ARENA_F32], F32)
    arena_bt = arena_t.bitcast(BF16)
    AB_ROW = 2 * ARENA_F32
    arena = Arena(ARENA_F32 * 4)

    def af(off_b, cols, p0=0, np_=128):
        assert off_b % 4 == 0
        return AP(arena_t, (off_b // 4) + p0 * ARENA_F32, [[ARENA_F32, np_], [1, cols]])

    def ab(off_b, cols, p0=0, np_=128):
        assert off_b % 2 == 0
        return AP(arena_bt, (off_b // 2) + p0 * AB_ROW, [[AB_ROW, np_], [1, cols]])

    ps_s = [es.enter_context(nc.psum_tensor(f"ps_s{i}", [128, 1024], F32)) for i in range(2)]
    ps_a = es.enter_context(nc.psum_tensor("ps_a", [128, 512], F32))
    ps_b = es.enter_context(nc.psum_tensor("ps_b", [128, 512], F32))
    ps_t0 = es.enter_context(nc.psum_tensor("ps_t0", [128, 1024], BF16))
    ps_c = es.enter_context(nc.psum_tensor("ps_c", [128, 512], F32))
    ps_t = [ps_t0, ps_c.bitcast(BF16)]

    sems = {e: es.enter_context(nc.semaphore(f"sem_{e}")) for e in ENGS}
    sems["gpsimd_c"] = es.enter_context(nc.semaphore("sem_gpsimd_c"))
    block = es.enter_context(nc.Block())

    P = Prog()

    def ZR(qc):
        return (32 * qc) if qc < 3 else 0
    b_s = [[Buf("s0a", psum=True), Buf("s0b", psum=True)],
           [Buf("s1a", psum=True), Buf("s1b", psum=True)]]
    b_pa, b_pb = Buf("pa", psum=True), Buf("pb", psum=True)
    b_pt = [Buf("pt0", psum=True), Buf("pt1", psum=True)]
    b_stream = [Buf(f"st{t}") for t in range(TT)]
    b_lnsc = Buf("lnsc")
    b_lnsc2 = Buf("lnsc2")
    dumpctr = [0]
    b_castb = [Buf(f"cb{i}") for i in range(4)]
    b_masks, b_ident, b_ones, b_selp, b_nb = Buf("mk"), Buf("id"), Buf("on"), Buf("sp"), Buf("nb")
    b_onec = Buf("onec")
    b_bias, b_eps, b_w4, b_ar, b_b4 = Buf("bi"), Buf("ep"), Buf("w4"), Buf("ar"), Buf("b4")
    b_zq = [Buf(f"zq{i}") for i in range(4)]
    b_zrecb = Buf("zrecb")
    b_zbc = Buf("zbc")
    b_negMs = [Buf("negM0"), Buf("negM1")]
    b_norm = Buf("norm")
    b_cpl, b_sig, b_sc = Buf("cpl"), Buf("sig"), Buf("sc")
    b_grow, b_gc = Buf("grow"), Buf("gc")
    b_stored = [Buf(f"so{i}") for i in range(SEQS * TT)]
    b_stats = {}
    SCOL = [0]

    def stat_col(name, width=16):
        if name not in b_stats:
            b_stats[name] = (SCOL[0], Buf(name))
            SCOL[0] += width
            assert SCOL[0] <= 184, name
        return b_stats[name]

    def dma_to(dst, src, eng="sync", w=(), r=(), tag=""):
        P.add(eng, (lambda d0, s0: (lambda e: e.dma_start(out=d0, in_=s0)))(dst, src),
              reads=list(r), writes=list(w), dma=True, tag=tag)

    dma_to(masks[:], dmasks.ap(), w=[b_masks], tag="ldmask")
    dma_to(ident[:], dident.ap(), w=[b_ident])
    dma_to(ones_f[:], dones.ap(), w=[b_ones])
    dma_to(ones_f128c[:], dones128.ap(), r=[b_ones], w=[b_ones])
    dma_to(selp[:], dselp.ap(), w=[b_selp])
    dma_to(nb128[:], dnb.ap(), w=[b_nb])
    dma_to(onec[:], donec.ap(), w=[b_onec])
    dma_to(b1s[:], db1.ap(), w=[b_bias])
    dma_to(b2s[:], db2.ap(), r=[b_bias], w=[b_bias])
    dma_to(b3s[:], db3.ap(), r=[b_bias], w=[b_bias])
    dma_to(gb1s[:], dgb1.ap(), r=[b_bias], w=[b_bias])
    dma_to(gb2s[:], dgb2.ap(), r=[b_bias], w=[b_bias])
    dma_to(gb3s[:], dgb3.ap(), r=[b_bias], w=[b_bias])
    dma_to(ars[:], dar.ap(), w=[b_ar])
    dma_to(grow[0:1, 0:1], db4.ap(), w=[b_grow], tag="ldb4")
    P.add("vector", lambda e: e.memset(eps[:], 1e-5), writes=[b_eps])
    dma_to(w4t[:], dw4.ap(), eng="gpsimd", w=[b_w4])
    P.add("tensor", lambda e: e.matmul(ps_a[:, 0:1], ones_f[:], grow[0:1, 0:1],
                                       start=True, stop=True, skip_group_check=True),
          reads=[b_ones, b_grow], writes=[b_pa], tag="b4bc")
    P.add("vector", lambda e: e.tensor_copy(out=b4bc[:], in_=ps_a[:, 0:1]),
          reads=[b_pa], writes=[b_b4])
    P.add("vector", lambda e: e.tensor_scalar_mul(out=b4h[:], in0=b4bc[:], scalar1=0.5),
          reads=[b_b4], writes=[b_b4])
    P.add("vector", lambda e: e.tensor_scalar_mul(out=gb3h[0:1, :], in0=gb3s[0:1, :],
                                                  scalar1=0.5),
          reads=[b_bias], writes=[b_bias])

    def ln_stats(tile_ap, col_sum, col_ssq, src_bufs, bsum, bssq, tag=""):
        P.add("vector",
              (lambda a0, c0: (lambda e: e.reduce_sum(
                  out=stats[:, c0:c0 + 1], in_=a0, axis=AXX)))(tile_ap, col_sum),
              reads=list(src_bufs), writes=[bsum], tag="lnsum" + tag)
        dumpctr[0] ^= 1
        dmp, bdmp = (lnsc2, b_lnsc2) if dumpctr[0] else (lnsc, b_lnsc)
        P.add("scalar",
              (lambda a0, c0, d0: (lambda e: e.activation(
                  out=d0[:], in_=a0, func=AF.Square,
                  accum_out=stats[:, c0:c0 + 1])))(tile_ap, col_ssq, dmp),
              reads=list(src_bufs), writes=[bdmp, bssq], tag="lnssq" + tag)

    def ln_rowmath(c_sum, c_ssq, c_A, c_B, bsum, bssq, bA, bB, n, tag="", extra_wA=()):
        c_t, bt = stat_col("tmp0")
        P.add("vector",
              (lambda: (lambda e: e.tensor_scalar_mul(
                  out=stats[:, c_t:c_t + n], in0=stats[:, c_sum:c_sum + n],
                  scalar1=1.0 / D)))(),
              reads=[bsum], writes=[bt], tag="rm_m" + tag)
        P.add("vector",
              (lambda: (lambda e: e.tensor_mul(
                  out=stats[:, c_A:c_A + n], in0=stats[:, c_t:c_t + n],
                  in1=stats[:, c_t:c_t + n])))(),
              reads=[bt], writes=[bA], tag="rm_m2" + tag)
        P.add("vector",
              (lambda: (lambda e: e.scalar_tensor_tensor(
                  out=stats[:, c_B:c_B + n], in0=stats[:, c_ssq:c_ssq + n],
                  scalar=1.0 / D, in1=stats[:, c_A:c_A + n],
                  op0=ALU.mult, op1=ALU.subtract)))(),
              reads=[bssq, bA], writes=[bB], tag="rm_var" + tag)
        P.add("scalar",
              (lambda: (lambda e: e.activation(
                  out=stats[:, c_B:c_B + n], in_=stats[:, c_B:c_B + n],
                  func=AF.Ln, bias=eps[:])))(),
              reads=[bB, b_eps], writes=[bB], tag="rm_ln" + tag)
        P.add("scalar",
              (lambda: (lambda e: e.activation(
                  out=stats[:, c_A:c_A + n], in_=stats[:, c_B:c_B + n],
                  func=AF.Exp, scale=-0.5)))(),
              reads=[bB], writes=[bA] + list(extra_wA), tag="rm_exp" + tag)
        P.add("vector",
              (lambda: (lambda e: e.scalar_tensor_tensor(
                  out=stats[:, c_B:c_B + n], in0=stats[:, c_t:c_t + n],
                  scalar=-1.0, in1=stats[:, c_A:c_A + n],
                  op0=ALU.mult, op1=ALU.mult)))(),
              reads=[bt, bA], writes=[bB], tag="rm_B" + tag)

    for s in range(SEQS):
        sfx = f"_s{s}"
        arena.reset()
        o_ln1t, b_ln1t = arena.alloc("ln1t", TT * 1024 * 2, nbufs=TT)
        o_attnN, b_attnN = arena.alloc("attnN", NPAIR * 2048 * 2, nbufs=NPAIR * QC)
        o_qt, b_qt = arena.alloc("qt", 2048 * 2, nbufs=QC)
        o_kt0, b_kts = arena.alloc("kt", 2 * 2048 * 2, nbufs=2 * QC)
        o_v, b_v = arena.alloc("v", 16 * 130 * 2, nbufs=4)
        o_ptl, b_ptl = arena.alloc("ptl", 2 * 1024 * 2, nbufs=2)
        o_qkvw, b_qkvw = arena.alloc("qkvw", 2 * 3 * 1024 * 2, nbufs=6)
        o_owt, b_owt = arena.alloc("owt", 8192 * 2, nbufs=8)

        def ln1T_ap(f, c0, ncols, p0=0, np_=128):
            return AP(arena_bt, o_ln1t // 2 + 2048 * f + c0 + p0 * AB_ROW,
                      [[AB_ROW, np_], [1, ncols]])

        for base in (64, 129):
            P.add("vector",
                  (lambda b0, ov: (lambda e: e.memset(
                      AP(arena_bt, ov // 2 + b0, [[AB_ROW, 128], [130, 16], [1, 1]]),
                      1.0)))(base, o_v),
                  writes=list(b_v), tag="vones" + sfx)

        for g in range(4):
            src = dx.ap()[s].rearrange("(t p) c -> p t c", p=128)[:, 4 * g:4 * g + 4, :]
            dst = AP(stream, 4096 * g, [[TT * 1024, 128], [1024, 4], [1, 1024]])
            dma_to(dst, src, w=b_stream[4 * g:4 * g + 4], tag=f"ldx{g}" + sfx)

        cS, bS = stat_col("l1sum")
        cQ, bQ = stat_col("l1ssq")
        cA1, bA1 = stat_col("A1")
        cB1, bB1 = stat_col("B1")
        for t in range(TT):
            ln_stats(stream[:, 1024 * t:1024 * (t + 1)], cS + t, cQ + t,
                     [b_stream[t]], bS, bQ, tag=f"1_{t}" + sfx)
        ln_rowmath(cS, cQ, cA1, cB1, bS, bQ, bA1, bB1, TT, tag="1" + sfx)

        for t in range(TT):
            c = t % 4
            P.add("vector",
                  (lambda t_, c_: (lambda e: e.tensor_scalar(
                      out=castb[:, 1024 * c_:1024 * (c_ + 1)],
                      in0=stream[:, 1024 * t_:1024 * (t_ + 1)],
                      scalar1=stats[:, cA1 + t_:cA1 + t_ + 1],
                      scalar2=stats[:, cB1 + t_:cB1 + t_ + 1],
                      op0=ALU.mult, op1=ALU.add)))(t, c),
                  reads=[b_stream[t], bA1, bB1], writes=[b_castb[c]],
                  tag=f"ln1ap{t}" + sfx)
            pt, bpt = ps_t[t % 2], b_pt[t % 2]

            def mk_tr(c_, pt_):
                def f(e):
                    r = None
                    for fb in range(8):
                        r = e.transpose(pt_[:, 128 * fb:128 * fb + 128],
                                        castb[:, 1024 * c_ + 128 * fb:1024 * c_ + 128 * fb + 128],
                                        ident[:])
                    return r
                return f
            P.add("tensor", mk_tr(c, pt), reads=[b_castb[c], b_ident], writes=[bpt],
                  tag=f"ln1tr{t}" + sfx)
            dstap = AP(arena_bt, o_ln1t // 2 + 128 * t, [[AB_ROW, 128], [2048, 8], [1, 128]])
            P.add("scalar",
                  (lambda pt_, d0: (lambda e: e.copy(out=d0, in_=pt_[:, 0:1024])))(pt, dstap),
                  reads=[bpt], writes=[b_ln1t[t]], tag=f"ln1ev{t}" + sfx)

        def load_qkv(p):
            slot = p % 2
            for m in range(3):
                dst = ab(o_qkvw + (slot * 3 + m) * 2048, 1024)
                dma_to(dst, dwqkv.ap()[m, p], eng="gpsimd",
                       w=[b_qkvw[slot * 3 + m]], tag=f"ldqkv{p}_{m}" + sfx)

        load_qkv(0)
        load_qkv(1)

        def emit_norms(tp):
            kpar = tp % 2
            o_ktp = o_kt0 + kpar * 4096
            bkt = b_kts[kpar * 4: kpar * 4 + 4]
            sqs_ap = ab(o_ptl, 1024)
            for (src_o, sbufs, w_) in ((o_qt, b_qt, 0), (o_ktp, bkt, 1)):
                for h2 in range(2):
                    sl, sb2 = ps_s[h2], b_s[h2]
                    P.add("vector",
                          (lambda so, h_: (lambda e: e.tensor_mul(
                              out=sqs_ap, in0=ab(so + h_ * 2048, 1024),
                              in1=ab(so + h_ * 2048, 1024))))(src_o, h2),
                          reads=[sbufs[2 * h2], sbufs[2 * h2 + 1]], writes=[b_ptl[0]],
                          tag=f"sq{tp}" + sfx)
                    for cc in range(2):
                        P.add("tensor",
                              (lambda sl_, cc_: (lambda e: e.matmul(
                                  sl_[0:2, 512 * cc_:512 * cc_ + 512], nb128[:],
                                  ab(o_ptl + 1024 * cc_, 512),
                                  start=True, stop=True, skip_group_check=True)))(sl, cc),
                              reads=[b_ptl[0], b_nb], writes=[sb2[cc]], tag=f"nmm{tp}" + sfx)
                    P.add("vector",
                          (lambda sl_, col: (lambda e: e.reduce_max(
                              out=normrow[:, col:col + 1], in_=sl_[0:2, 0:1024],
                              axis=AXX)))(sl, 4 * w_ + h2),
                          reads=[sb2[0], sb2[1]], writes=[b_norm], tag=f"nrd{tp}" + sfx)
            P.add("vector", lambda e: e.tensor_max(out=normrow[:, 0:1],
                                                   in0=normrow[:, 0:1], in1=normrow[:, 1:2]),
                  reads=[b_norm], writes=[b_norm])
            P.add("vector", lambda e: e.tensor_max(out=normrow[:, 4:5],
                                                   in0=normrow[:, 4:5], in1=normrow[:, 5:6]),
                  reads=[b_norm], writes=[b_norm])
            P.add("scalar", lambda e: e.activation(out=normrow[:, 2:3], in_=normrow[:, 0:1],
                                                   func=AF.Ln),
                  reads=[b_norm], writes=[b_norm])
            P.add("scalar", lambda e: e.activation(out=normrow[:, 6:7], in_=normrow[:, 4:5],
                                                   func=AF.Ln),
                  reads=[b_norm], writes=[b_norm])
            P.add("vector", lambda e: e.tensor_add(out=normrow[:, 3:4], in0=normrow[:, 2:3],
                                                   in1=normrow[:, 6:7]),
                  reads=[b_norm], writes=[b_norm])
            P.add("scalar", lambda e: e.activation(out=normrow[:, 7:8], in_=normrow[:, 3:4],
                                                   func=AF.Exp, scale=0.5),
                  reads=[b_norm], writes=[b_norm])
            P.add("vector", lambda e: e.tensor_scalar_mul(out=normrow[0:1, 7:8],
                                                          in0=normrow[0:1, 7:8],
                                                          scalar1=-1.02),
                  reads=[b_norm], writes=[b_norm])
            P.add("tensor", lambda e: e.matmul(ps_a[:, 0:1], ones_f[:], normrow[0:1, 7:8],
                                               start=True, stop=True, skip_group_check=True),
                  reads=[b_norm, b_ones], writes=[b_pa], tag=f"Mbc{tp}" + sfx)
            P.add("vector",
                  (lambda kp: (lambda e: e.tensor_copy(
                      out=negM[:, kp:kp + 1], in_=ps_a[:, 0:1])))(kpar),
                  reads=[b_pa], writes=[b_negMs[kpar]])

        for p in range(NPAIR):
            slot = p % 2
            kpar = p % 2
            o_ktp = o_kt0 + kpar * 4096
            b_kt = b_kts[kpar * 4: kpar * 4 + 4]
            wq = ab(o_qkvw + (slot * 3 + 0) * 2048, 1024)
            wk = ab(o_qkvw + (slot * 3 + 1) * 2048, 1024)
            wv = ab(o_qkvw + (slot * 3 + 2) * 2048, 1024)
            bwq, bwk, bwv = (b_qkvw[slot * 3 + m] for m in range(3))

            if p == 0:
                for (wt, bw, dsto, dbufs, scale, nm) in (
                        (wq, bwq, o_qt, b_qt, 1.0 / 32, "q"),
                        (wk, bwk, o_ktp, b_kt, None, "k")):
                    for qc in range(QC):
                        sl, sb2 = ps_s[qc // 2], b_s[qc // 2]
                        half = qc % 2
                        pap = sl[:, 512 * half:512 * half + 512]

                        def mk_proj(wt_, qc_, pap_):
                            def f(e):
                                r = None
                                for fb in range(8):
                                    r = e.matmul(pap_, wt_[:, 128 * fb:128 * fb + 128],
                                                 ln1T_ap(fb, 512 * qc_, 512),
                                                 start=(fb == 0), stop=(fb == 7),
                                                 skip_group_check=True)
                                return r
                            return f
                        P.add("tensor", mk_proj(wt, qc, pap),
                              reads=[bw] + b_ln1t[4 * qc:4 * qc + 4], writes=[sb2[half]],
                              tag=f"prj{nm}{p}_{qc}" + sfx)
                        dstap = ab(dsto + 1024 * qc, 512)
                        if scale is None:
                            P.add("vector",
                                  (lambda pap_, d0: (lambda e: e.tensor_copy(
                                      out=d0, in_=pap_)))(pap, dstap),
                                  reads=[sb2[half]], writes=[dbufs[qc]],
                                  tag=f"ev{nm}{p}_{qc}" + sfx)
                        else:
                            P.add("vector",
                                  (lambda pap_, d0, sc_: (lambda e: e.tensor_scalar_mul(
                                      out=d0, in0=pap_, scalar1=sc_)))(pap, dstap, scale),
                                  reads=[sb2[half]], writes=[dbufs[qc]],
                                  tag=f"ev{nm}{p}_{qc}" + sfx)
                emit_norms(0)

            for g in range(4):
                sl, sb2 = ps_s[g % 2], b_s[g % 2]
                for j in range(4):
                    kt = 4 * g + j
                    half, sub = j // 2, j % 2
                    pap = sl[:, 512 * half + 128 * sub: 512 * half + 128 * sub + 128]

                    def mk_v(kt_, pap_, wv_):
                        def f(e):
                            r = None
                            for fb in range(8):
                                r = e.matmul(pap_, ln1T_ap(fb, 128 * kt_, 128),
                                             wv_[:, 128 * fb:128 * fb + 128],
                                             start=(fb == 0), stop=(fb == 7),
                                             skip_group_check=True)
                            return r
                        return f
                    P.add("tensor", mk_v(kt, pap, wv),
                          reads=[bwv, b_ln1t[kt]], writes=[sb2[half]],
                          tag=f"prjv{p}_{kt}" + sfx)
                for j in range(4):
                    kt = 4 * g + j
                    half, sub = j // 2, j % 2
                    srcap = AP(sl, 512 * half + 128 * sub, [[1024, 128], [64, 2], [1, 64]])
                    dstap = AP(arena_bt, o_v // 2 + 130 * kt, [[AB_ROW, 128], [65, 2], [1, 64]])
                    P.add("vector",
                          (lambda s0, d0: (lambda e: e.tensor_copy(out=d0, in_=s0)))(srcap, dstap),
                          reads=[sb2[half]], writes=[b_v[g]], tag=f"evv{p}_{kt}" + sfx)

            if p + 2 < NPAIR:
                load_qkv(p + 2)
            if p == 5:
                for fb in range(8):
                    dma_to(ab(o_owt + fb * 2048, 1024),
                           dowt.ap()[:, 1024 * fb:1024 * fb + 1024], eng="gpsimd",
                           w=[b_owt[fb]], tag=f"ldow{fb}" + sfx)

            def emit_av(qc, kt, nkt):
                off = max(0, 128 * (kt - 4 * qc))
                ncols = 512 - off
                P.add("tensor",
                      (lambda kt_, o_, n_, f0, l0: (lambda e: e.matmul(
                          ps_a[0:65, o_:512],
                          AP(arena_bt, o_v // 2 + 130 * kt_, [[AB_ROW, 128], [1, 65]]),
                          AP(arena_bt, (o_ptl + 2048 * (kt_ % 2)) // 2 + o_,
                             [[AB_ROW, 128], [1, n_]]),
                          start=f0, stop=l0,
                          skip_group_check=True)))(kt, off, ncols, kt == 0, kt == nkt - 1),
                      reads=[b_ptl[kt % 2], b_v[kt // 4]], writes=[b_pa],
                      tag=f"avA{p}_{qc}_{kt}" + sfx)
                P.add("tensor",
                      (lambda kt_, o_, n_, f0, l0: (lambda e: e.matmul(
                          ps_b[0:65, o_:512],
                          AP(arena_bt, o_v // 2 + 130 * kt_ + 65, [[AB_ROW, 128], [1, 65]]),
                          AP(arena_bt, (o_ptl + 2048 * (kt_ % 2)) // 2 + 512 + o_,
                             [[AB_ROW, 128], [1, n_]]),
                          start=f0, stop=l0,
                          skip_group_check=True)))(kt, off, ncols, kt == 0, kt == nkt - 1),
                      reads=[b_ptl[kt % 2], b_v[kt // 4]], writes=[b_pb],
                      tag=f"avB{p}_{qc}_{kt}" + sfx)

            def interleave_unit(qc, which):
                nslot = (p + 1) % 2
                if which == "q":
                    wt_n = ab(o_qkvw + (nslot * 3 + 0) * 2048, 1024)
                    bw_n = b_qkvw[nslot * 3 + 0]
                    dstap = ab(o_qt + 1024 * qc, 512)
                    dbuf = b_qt[qc]
                else:
                    wt_n = ab(o_qkvw + (nslot * 3 + 1) * 2048, 1024)
                    bw_n = b_qkvw[nslot * 3 + 1]
                    dstap = ab(o_kt0 + nslot * 4096 + 1024 * qc, 512)
                    dbuf = b_kts[nslot * 4 + qc]

                def mk_prj(wt_, qc_):
                    def f(e):
                        r = None
                        for fb in range(8):
                            r = e.matmul(ps_c[:, 0:512], wt_[:, 128 * fb:128 * fb + 128],
                                         ln1T_ap(fb, 512 * qc_, 512),
                                         start=(fb == 0), stop=(fb == 7),
                                         skip_group_check=True)
                        return r
                    return f
                P.add("tensor", mk_prj(wt_n, qc),
                      reads=[bw_n] + b_ln1t[4 * qc:4 * qc + 4], writes=[b_pt[1]],
                      tag=f"iprj{which}{p + 1}_{qc}" + sfx)
                if which == "q":
                    P.add("vector",
                          (lambda d0: (lambda e: e.tensor_scalar_mul(
                              out=d0, in0=ps_c[:, 0:512], scalar1=1.0 / 32)))(dstap),
                          reads=[b_pt[1]], writes=[dbuf], tag=f"iev{which}{p + 1}_{qc}" + sfx)
                else:
                    P.add("vector",
                          (lambda d0: (lambda e: e.tensor_copy(
                              out=d0, in_=ps_c[:, 0:512])))(dstap),
                          reads=[b_pt[1]], writes=[dbuf], tag=f"iev{which}{p + 1}_{qc}" + sfx)

            for qc in range(QC):
                nkt = 4 * qc + 4
                for kt in range(nkt):
                    off = max(0, 128 * (kt - 4 * qc))
                    ncols = 512 - off
                    sl, sb2 = ps_s[kt % 2], b_s[kt % 2]
                    bptl = b_ptl[kt % 2]
                    P.add("tensor",
                          (lambda kt_, qc_, sl_, o_, n_, ok_: (lambda e: e.matmul(
                              sl_[:, o_:512],
                              AP(arena_bt, ok_ // 2 + 128 * kt_, [[AB_ROW, 64], [1, 128]]),
                              AP(arena_bt, o_qt // 2 + 512 * qc_ + o_, [[AB_ROW, 64], [1, n_]]),
                              start=True, stop=True, tile_position=(0, 0),
                              skip_group_check=True)))(kt, qc, sl, off, ncols, o_ktp),
                          reads=[b_kt[kt // 4], b_qt[qc]], writes=[sb2[0]],
                          tag=f"scA{p}_{qc}_{kt}" + sfx)
                    P.add("tensor",
                          (lambda kt_, qc_, sl_, o_, n_, ok_: (lambda e: e.matmul(
                              sl_[:, 512 + o_:1024],
                              AP(arena_bt, ok_ // 2 + 128 * kt_ + 64 * AB_ROW,
                                 [[AB_ROW, 64], [1, 128]]),
                              AP(arena_bt, o_qt // 2 + 512 * qc_ + o_ + 64 * AB_ROW,
                                 [[AB_ROW, 64], [1, n_]]),
                              start=True, stop=True, tile_position=(64, 0),
                              skip_group_check=True)))(kt, qc, sl, off, ncols, o_ktp),
                          reads=[b_kt[kt // 4], b_qt[qc]], writes=[sb2[1]],
                          tag=f"scB{p}_{qc}_{kt}" + sfx)
                    P.add("scalar",
                          (lambda kt_, sl_, o_, n_, kp: (lambda e: e.activation(
                              out=AP(arena_bt, (o_ptl + 2048 * (kt_ % 2)) // 2 + o_,
                                     [[AB_ROW, 128], [512, 2], [1, n_]]),
                              in_=AP(sl_, o_, [[1024, 128], [512, 2], [1, n_]]),
                              func=AF.Exp, bias=negM[:, kp:kp + 1])))(kt, sl, off, ncols, kpar),
                          reads=[sb2[0], sb2[1], b_negMs[kpar]], writes=[bptl],
                          tag=f"exp{p}_{qc}_{kt}" + sfx)
                    if off or kt == 4 * qc:
                        P.add("vector",
                              (lambda kt_, o_, n_, mo_: (lambda e: e.tensor_mul(
                                  out=AP(arena_bt, (o_ptl + 2048 * (kt_ % 2)) // 2 + o_,
                                         [[AB_ROW, 128], [512, 2], [1, n_]]),
                                  in0=AP(arena_bt, (o_ptl + 2048 * (kt_ % 2)) // 2 + o_,
                                         [[AB_ROW, 128], [512, 2], [1, n_]]),
                                  in1=AP(masks, 512 * mo_ + o_,
                                         [[2048, 128], [0, 2], [1, n_]]))))(
                                             kt, off, ncols, kt - 4 * qc),
                              reads=[bptl, b_masks], writes=[bptl],
                              tag=f"msk{p}_{qc}_{kt}" + sfx)
                    if kt > 0:
                        emit_av(qc, kt - 1, nkt)
                    if kt == nkt // 2 and p + 1 < NPAIR:
                        interleave_unit(qc, "k")
                emit_av(qc, nkt - 1, nkt)
                aidx = p * QC + qc
                P.add("vector",
                      (lambda p_, qc_: (lambda e: e.tensor_copy(
                          out=AP(arena_bt, o_attnN // 2 + 2048 * p_ + 512 * qc_,
                                 [[AB_ROW, 64], [1, 512]]),
                          in_=ps_a[0:64, :])))(p, qc),
                      reads=[b_pa], writes=[b_attnN[aidx]], tag=f"eva{p}_{qc}" + sfx)
                P.add("vector",
                      (lambda p_, qc_: (lambda e: e.tensor_copy(
                          out=AP(arena_bt, o_attnN // 2 + 2048 * p_ + 512 * qc_ + 64 * AB_ROW,
                                 [[AB_ROW, 64], [1, 512]]),
                          in_=ps_b[0:64, :])))(p, qc),
                      reads=[b_pb], writes=[b_attnN[aidx]], tag=f"evb{p}_{qc}" + sfx)
                if qc < 3:
                    P.add("vector",
                          (lambda qc_: (lambda e: e.tensor_copy(
                              out=rowz[ZR(qc_):ZR(qc_) + 1, 0:512],
                              in_=ps_a[64:65, 0:512])))(qc),
                          reads=[b_pa], writes=[b_zq[qc]], tag=f"zra{p}_{qc}" + sfx)
                    P.add("vector",
                          (lambda qc_: (lambda e: e.tensor_copy(
                              out=rowz[ZR(qc_):ZR(qc_) + 1, 512:1024],
                              in_=ps_b[64:65, 0:512])))(qc),
                          reads=[b_pb], writes=[b_zq[qc]], tag=f"zrb{p}_{qc}" + sfx)
                if p + 1 < NPAIR:
                    interleave_unit(qc, "q")

            for qc in range(QC):
                zi = 0 if qc == 3 else qc
                if qc == 3:
                    P.add("vector", lambda e: e.tensor_copy(
                              out=rowz[0:1, 0:512], in_=ps_a[64:65, 0:512]),
                          reads=[b_pa], writes=[b_zq[0]], tag=f"zra{p}_3" + sfx)
                    P.add("vector", lambda e: e.tensor_copy(
                              out=rowz[0:1, 512:1024], in_=ps_b[64:65, 0:512]),
                          reads=[b_pb], writes=[b_zq[0]], tag=f"zrb{p}_3" + sfx)
                P.add("scalar",
                      (lambda qc_: (lambda e: e.activation(
                          out=rowz[ZR(qc_):ZR(qc_) + 1, :],
                          in_=rowz[ZR(qc_):ZR(qc_) + 1, :],
                          func=AF.Ln)))(qc),
                      reads=[b_zq[zi]], writes=[b_zq[zi]], tag=f"zln{p}_{qc}" + sfx)
                P.add("scalar",
                      (lambda qc_: (lambda e: e.activation(
                          out=rowzb[ZR(qc_):ZR(qc_) + 1, :],
                          in_=rowz[ZR(qc_):ZR(qc_) + 1, :],
                          func=AF.Exp, scale=-1.0)))(qc),
                      reads=[b_zq[zi]], writes=[b_zrecb], tag=f"zexp{p}_{qc}" + sfx)
                aidx = p * QC + qc
                P.add("tensor",
                      (lambda qc_: (lambda e: e.matmul(
                          ps_s[0][:, 0:512], selp[ZR(qc_):ZR(qc_) + 1, 0:128],
                          rowzb[ZR(qc_):ZR(qc_) + 1, 0:512],
                          start=True, stop=False, skip_group_check=True)))(qc),
                      reads=[b_zrecb, b_selp], writes=[b_s[0][0]], tag=f"zbcA{p}_{qc}" + sfx)
                P.add("tensor",
                      (lambda qc_: (lambda e: e.matmul(
                          ps_s[0][:, 0:512], selp[ZR(qc_):ZR(qc_) + 1, 128:256],
                          rowzb[ZR(qc_):ZR(qc_) + 1, 512:1024],
                          start=False, stop=True, skip_group_check=True)))(qc),
                      reads=[b_zrecb, b_selp], writes=[b_s[0][0]], tag=f"zbcB{p}_{qc}" + sfx)
                P.add("vector", lambda e: e.tensor_copy(out=zbc[:], in_=ps_s[0][:, 0:512]),
                      reads=[b_s[0][0]], writes=[b_zbc], tag=f"zbce{p}_{qc}" + sfx)
                P.add("vector",
                      (lambda p_, qc_: (lambda e: e.tensor_mul(
                          out=AP(arena_bt, o_attnN // 2 + 2048 * p_ + 512 * qc_,
                                 [[AB_ROW, 128], [1, 512]]),
                          in0=AP(arena_bt, o_attnN // 2 + 2048 * p_ + 512 * qc_,
                                 [[AB_ROW, 128], [1, 512]]),
                          in1=zbc[:])))(p, qc),
                      reads=[b_zbc, b_attnN[aidx]], writes=[b_attnN[aidx]],
                      tag=f"nrm{p}_{qc}" + sfx)

            if p + 1 < NPAIR:
                emit_norms(p + 1)

        cS2, bS2 = stat_col("l2sum")
        cQ2, bQ2 = stat_col("l2ssq")
        cA2, bA2 = stat_col("A2")
        cB2, bB2 = stat_col("B2")
        cSo, bSo = stat_col("losum", 2)
        cQo, bQo = stat_col("lossq", 2)
        cAo, bAo = stat_col("Ao", 2)
        cBo, bBo = stat_col("Bo", 2)
        bA2t = [Buf(f"A2t{t}") for t in range(TT)]

        def emit_gc_mm(t):
            for h2 in range(2):
                psg, bpsg = ((ps_a, b_pa), (ps_b, b_pb))[h2]
                P.add("tensor",
                      (lambda t_, h_, ps_: (lambda e: e.matmul(
                          ps_[0:1, :], stats[:, cA2 + t_:cA2 + t_ + 1],
                          stream[:, 1024 * t_ + 512 * h_:1024 * t_ + 512 * h_ + 512],
                          start=(t_ == 0), stop=(t_ == TT - 1),
                          skip_group_check=True)))(t, h2, psg),
                      reads=[bA2t[t], b_stream[t]], writes=[bpsg], tag=f"gc{t}_{h2}" + sfx)

        for t in range(TT):
            sl, sb2 = ps_s[t % 2], b_s[t % 2]

            def mk_oproj(t_, sl_):
                def f(e):
                    r = None
                    for pp in range(NPAIR):
                        for h2 in range(2):
                            r = e.matmul(
                                sl_[:, 512 * h2:512 * h2 + 512],
                                AP(arena_bt, o_attnN // 2 + 2048 * pp + 128 * t_,
                                   [[AB_ROW, 128], [1, 128]]),
                                AP(arena_bt, o_owt // 2 + 1024 * pp + 512 * h2,
                                   [[AB_ROW, 128], [1, 512]]),
                                start=(pp == 0), stop=(pp == NPAIR - 1),
                                skip_group_check=True)
                    return r
                return f
            P.add("tensor", mk_oproj(t, sl),
                  reads=[b_attnN[pp * QC + t // 4] for pp in range(NPAIR)] + list(b_owt),
                  writes=[sb2[0], sb2[1]], tag=f"oprj{t}" + sfx)
            if t >= 3:
                emit_gc_mm(t - 3)
            ln_stats(sl[:, 0:1024], cSo, cQo, [sb2[0], sb2[1]], bSo, bQo,
                     tag=f"o{t}" + sfx)
            ln_rowmath(cSo, cQo, cAo, cBo, bSo, bQo, bAo, bBo, 1, tag=f"o{t}" + sfx)
            P.add("scalar",
                  (lambda t_: (lambda e: e.activation(
                      out=stream[:, 1024 * t_:1024 * (t_ + 1)],
                      in_=stream[:, 1024 * t_:1024 * (t_ + 1)], func=AF.Identity,
                      bias=stats[:, cBo:cBo + 1])))(t),
                  reads=[bBo, b_stream[t]], writes=[b_stream[t]], tag=f"oresb{t}" + sfx)
            P.add("vector",
                  (lambda t_, sl_: (lambda e: e.scalar_tensor_tensor(
                      out=stream[:, 1024 * t_:1024 * (t_ + 1)],
                      in0=sl_[:, 0:1024], scalar=stats[:, cAo:cAo + 1],
                      in1=stream[:, 1024 * t_:1024 * (t_ + 1)],
                      op0=ALU.mult, op1=ALU.add)))(t, sl),
                  reads=[sb2[0], sb2[1], bAo, b_stream[t]], writes=[b_stream[t]],
                  tag=f"ores{t}" + sfx)
            ln_stats(stream[:, 1024 * t:1024 * (t + 1)], cS2 + t, cQ2 + t,
                     [b_stream[t]], bS2, bQ2, tag=f"2_{t}" + sfx)
            ln_rowmath(cS2 + t, cQ2 + t, cA2 + t, cB2 + t, bS2, bQ2, bA2, bB2, 1,
                       tag=f"2_{t}" + sfx, extra_wA=[bA2t[t]])
        for t in range(TT - 3, TT):
            emit_gc_mm(t)

        arena.reset()
        o_w1, b_w1 = arena.alloc("w1", 8192 * 2, nbufs=1)
        o_w2, b_w2 = arena.alloc("w2", 4096 * 2, nbufs=1)
        o_w3, b_w3 = arena.alloc("w3", 1024 * 2, nbufs=1)
        o_gw1, b_gw1 = arena.alloc("gw1", 4096 * 2, nbufs=1)
        o_gw2, b_gw2 = arena.alloc("gw2", 1024 * 2, nbufs=1)
        o_gw3, b_gw3 = arena.alloc("gw3", 4, nbufs=1)
        o_state, b_state = arena.alloc("state", 8 * 1024 * 4, nbufs=8)
        o_outT, b_outTh = arena.alloc("outT", 8192 * 2, nbufs=2)
        o_h1T, b_h1T = arena.alloc("h1T", 4096 * 2, nbufs=8)
        o_h2T, b_h2T = arena.alloc("h2T", 2048 * 2, nbufs=4)
        o_h3T, b_h3T = arena.alloc("h3T", 2048 * 2, nbufs=2)
        o_twt, b_twt = arena.alloc("twt", 2 * 1024 * 4, nbufs=8)

        dma_to(ab(o_w1, 8192), dw1.ap(), eng="gpsimd", w=[b_w1], tag="ldw1" + sfx)
        dma_to(ab(o_w2, 4096), dw2.ap(), eng="gpsimd", w=[b_w2], tag="ldw2" + sfx)
        dma_to(ab(o_w3, 1024), dw3.ap(), eng="gpsimd", w=[b_w3], tag="ldw3" + sfx)
        dma_to(ab(o_gw1, 4096), dgw1.ap(), eng="gpsimd", w=[b_gw1], tag="ldgw1" + sfx)
        dma_to(ab(o_gw2, 1024), dgw2.ap(), eng="gpsimd", w=[b_gw2], tag="ldgw2" + sfx)
        dma_to(ab(o_gw3, 2), dgw3.ap(), eng="gpsimd", w=[b_gw3], tag="ldgw3" + sfx)

        def emit_gcnet():
            P.add("tensor", lambda e: e.matmul(ps_s[0][0:1, 0:16], ones_f128c[:],
                                                stats[:, cB2:cB2 + 16],
                                                start=True, stop=True,
                                                skip_group_check=True),
                  reads=[bB2, b_ones], writes=[b_s[0][0]], tag="b2sum_mm" + sfx)
            P.add("vector", lambda e: e.reduce_sum(out=grow[0:1, 5:6], in_=ps_s[0][0:1, 0:16],
                                                   axis=AXX),
                  reads=[b_s[0][0]], writes=[b_grow], tag="b2sum_red" + sfx)
            P.add("vector", lambda e: e.tensor_scalar_mul(out=grow[0:1, 6:7],
                                                          in0=grow[0:1, 5:6], scalar1=1.0 / S),
                  reads=[b_grow], writes=[b_grow], tag="b2sum_sc" + sfx)
            P.add("scalar", lambda e: e.activation(out=gcrowb[0:1, 0:512], in_=ps_a[0:1, :],
                                                   func=AF.Identity, scale=1.0 / S,
                                                   bias=grow[0:1, 6:7]),
                  reads=[b_pa, b_grow], writes=[b_gc], tag="gcev0" + sfx)
            P.add("scalar", lambda e: e.activation(out=gcrowb[0:1, 512:1024], in_=ps_b[0:1, :],
                                                   func=AF.Identity, scale=1.0 / S,
                                                   bias=grow[0:1, 6:7]),
                  reads=[b_pb, b_grow], writes=[b_gc], tag="gcev1" + sfx)

            def mk_gctr():
                def f(e):
                    r = None
                    for fb in range(8):
                        r = e.transpose(ps_t[0][:, 2 * fb:2 * fb + 1],
                                        gcrowb[0:1, 128 * fb:128 * fb + 128], ident[0:1, 0:1])
                    return r
                return f
            P.add("tensor", mk_gctr(), reads=[b_gc, b_ident], writes=[b_pt[0]], tag="gctr" + sfx)
            P.add("vector", lambda e: e.tensor_copy(
                out=gcT[:], in_=AP(ps_t[0], 0, [[1024, 128], [2, 8], [1, 1]])),
                  reads=[b_pt[0]], writes=[b_gc], tag="gctev" + sfx)

            def mk_g1(o_):
                def f(e):
                    r = None
                    for fb in range(8):
                        r = e.matmul(ps_a[:, o_:o_ + 1],
                                     ab(o_gw1 + (o_ * 8 + fb) * 256, 128),
                                     gcT[:, fb:fb + 1], start=(fb == 0), stop=(fb == 7),
                                     skip_group_check=True)
                    return r
                return f
            for o in range(4):
                P.add("tensor", mk_g1(o), reads=[b_gw1, b_gc], writes=[b_pa],
                      tag=f"g1_{o}" + sfx)
            for o in range(4):
                P.add("scalar",
                      (lambda o_: (lambda e: e.activation(
                          out=hg1[:, o_:o_ + 1], in_=ps_a[:, o_:o_ + 1], func=AF.Gelu,
                          bias=gb1s[:, o_:o_ + 1])))(o),
                      reads=[b_pa, b_bias], writes=[b_gc], tag=f"g1g{o}" + sfx)

            def mk_g2(o_):
                def f(e):
                    r = None
                    for fb in range(4):
                        r = e.matmul(ps_b[:, o_:o_ + 1],
                                     ab(o_gw2 + (o_ * 4 + fb) * 256, 128),
                                     hg1[:, fb:fb + 1], start=(fb == 0), stop=(fb == 3),
                                     skip_group_check=True)
                    return r
                return f
            for o in range(2):
                P.add("tensor", mk_g2(o), reads=[b_gw2, b_gc], writes=[b_pb],
                      tag=f"g2_{o}" + sfx)
            for o in range(2):
                P.add("scalar",
                      (lambda o_: (lambda e: e.activation(
                          out=hg2[:, o_:o_ + 1], in_=ps_b[:, o_:o_ + 1], func=AF.Gelu,
                          bias=gb2s[:, o_:o_ + 1])))(o),
                      reads=[b_pb, b_bias], writes=[b_gc], tag=f"g2g{o}" + sfx)

            def mk_g3():
                def f(e):
                    r = None
                    for fb in range(2):
                        r = e.matmul(ps_a[0:1, 8:9], ab(o_gw3 + fb * 2, 1),
                                     hg2[:, fb:fb + 1], start=(fb == 0), stop=(fb == 1),
                                     skip_group_check=True)
                    return r
                return f
            P.add("tensor", mk_g3(), reads=[b_gw3, b_gc], writes=[b_pa], tag="g3" + sfx)
            P.add("scalar", lambda e: e.activation(out=grow[0:1, 0:1], in_=ps_a[0:1, 8:9],
                                                   func=AF.Tanh, scale=0.5,
                                                   bias=gb3h[0:1, 0:1]),
                  reads=[b_pa, b_bias], writes=[b_grow], tag="gf" + sfx)
            P.add("vector", lambda e: e.tensor_scalar(out=grow[0:1, 1:2], in0=grow[0:1, 0:1],
                                                      scalar1=0.35, scalar2=-0.15,
                                                      op0=ALU.mult, op1=ALU.add),
                  reads=[b_grow], writes=[b_grow])
            P.add("vector", lambda e: e.tensor_mul(out=grow[0:1, 2:3], in0=grow[0:1, 1:2],
                                                   in1=ars[0:1, 0:1]),
                  reads=[b_grow, b_ar], writes=[b_grow])
            P.add("vector", lambda e: e.tensor_scalar(out=grow[0:1, 3:4], in0=grow[0:1, 2:3],
                                                      scalar1=2.0, scalar2=0.1,
                                                      op0=ALU.mult, op1=ALU.add),
                  reads=[b_grow], writes=[b_grow], tag="s1" + sfx)
            P.add("vector", lambda e: e.tensor_scalar_mul(out=grow[0:1, 4:5], in0=ars[0:1, 0:1],
                                                          scalar1=0.3),
                  reads=[b_ar], writes=[b_grow], tag="s2" + sfx)
            P.add("vector", lambda e: e.tensor_add(out=grow[0:1, 3:4], in0=grow[0:1, 3:4],
                                                   in1=grow[0:1, 4:5]),
                  reads=[b_grow], writes=[b_grow], tag="s1p" + sfx)
            P.add("tensor", lambda e: e.matmul(ps_a[:, 16:18], ones_f[:], grow[0:1, 3:5],
                                               start=True, stop=True, skip_group_check=True),
                  reads=[b_grow, b_ones], writes=[b_pa], tag="s12bc" + sfx)
            P.add("vector", lambda e: e.tensor_copy(out=sc12[:], in_=ps_a[:, 16:18]),
                  reads=[b_pa], writes=[b_sc], tag="s12ev" + sfx)



        cSf, bSf = stat_col("lfsum", 8)
        cQf, bQf = stat_col("lfssq", 8)
        cAf, bAf = stat_col("Af", 8)
        cBf, bBf = stat_col("Bf", 8)
        def emit_st(ch, j):
            t = NJ * ch + j
            P.add("vector",
                  (lambda t_, j_: (lambda e: e.tensor_scalar(
                      out=af(o_state + j_ * 4096, 1024),
                      in0=stream[:, 1024 * t_:1024 * (t_ + 1)],
                      scalar1=stats[:, cA2 + t_:cA2 + t_ + 1],
                      scalar2=stats[:, cB2 + t_:cB2 + t_ + 1],
                      op0=ALU.mult, op1=ALU.add)))(t, j),
                  reads=[b_stream[t], bA2, bB2], writes=[b_state[j]],
                  tag=f"st{ch}_{j}" + sfx)

        def castrf(ch, it, th, js=None):
            for j in (range(4 * th, 4 * th + 4) if js is None else js):
                c4 = j % 4
                P.add("vector",
                      (lambda j_, c_: (lambda e: e.tensor_copy(
                          out=castb[:, 1024 * c_:1024 * (c_ + 1)],
                          in_=af(o_state + j_ * 4096, 1024))))(j, c4),
                      reads=[b_state[j]], writes=[b_castb[c4]],
                      tag=f"cst{ch}_{it}_{j}" + sfx)
                pt, bpt = ps_t[j % 2], b_pt[j % 2]

                def mk_trf(c_, pt_):
                    def f(e):
                        r = None
                        for fb in range(8):
                            r = e.transpose(pt_[:, 128 * fb:128 * fb + 128],
                                            castb[:, 1024 * c_ + 128 * fb:
                                                  1024 * c_ + 128 * fb + 128],
                                            ident[:])
                        return r
                    return f
                P.add("tensor", mk_trf(c4, pt), reads=[b_castb[c4], b_ident],
                      writes=[bpt], tag=f"trf{ch}_{it}_{j}" + sfx)
                dstap = AP(arena_bt, o_outT // 2 + 128 * (j % 4) + 512 * th,
                           [[AB_ROW, 128], [1024, 8], [1, 128]])
                P.add("scalar",
                      (lambda pt_, d0: (lambda e: e.activation(
                          out=d0, in_=pt_[:, 0:1024], func=AF.Identity)))(pt, dstap),
                      reads=[bpt], writes=[b_outTh[th]], tag=f"tev{ch}_{it}_{j}" + sfx)

        for ch in range(2):
            if ch == 0:
                for j in range(NJ):
                    emit_st(0, j)
                castrf(0, 0, 0)
                emit_gcnet()
            for it in range(NITER):
                for th in range(2):
                    for o in range(8):
                        sl, sb2 = ps_s[(o // 2) % 2], b_s[(o // 2) % 2]
                        half = o % 2
                        pap = sl[:, 512 * half:512 * half + 512]

                        def mk_w1(o_, th_, pap_):
                            def f(e):
                                r = None
                                for fb in range(8):
                                    r = e.matmul(pap_, ab(o_w1 + (o_ * 8 + fb) * 256, 128),
                                                 ab(o_outT + fb * 2048 + th_ * 1024, 512),
                                                 start=(fb == 0), stop=(fb == 7),
                                                 skip_group_check=True)
                                return r
                            return f
                        P.add("tensor", mk_w1(o, th, pap), reads=[b_w1, b_outTh[th]],
                              writes=[sb2[half]], tag=f"w1_{ch}_{it}_{th}_{o}" + sfx)
                        P.add("scalar",
                              (lambda o_, th_, pap_: (lambda e: e.activation(
                                  out=ab(o_h1T + o_ * 1024, 512), in_=pap_,
                                  func=AF.Gelu, bias=b1s[:, o_:o_ + 1])))(o, th, pap),
                              reads=[sb2[half], b_bias], writes=[b_h1T[o]],
                              tag=f"h1_{ch}_{it}_{th}_{o}" + sfx)
                    if th == 0:
                        castrf(ch, it, 1)
                    elif it + 1 < NITER:
                        castrf(ch, it + 1, 0)
                    for o in range(4):
                        sl, sb2 = ps_s[o % 2], b_s[o % 2]
                        pap = sl[:, 0:512]

                        def mk_w2(o_, th_, pap_):
                            def f(e):
                                r = None
                                for fb in range(8):
                                    r = e.matmul(pap_, ab(o_w2 + (o_ * 8 + fb) * 256, 128),
                                                 ab(o_h1T + fb * 1024, 512),
                                                 start=(fb == 0), stop=(fb == 7),
                                                 skip_group_check=True)
                                return r
                            return f
                        P.add("tensor", mk_w2(o, th, pap), reads=[b_w2] + list(b_h1T),
                              writes=[sb2[0]], tag=f"w2_{ch}_{it}_{th}_{o}" + sfx)
                        P.add("scalar",
                              (lambda o_, th_, pap_: (lambda e: e.activation(
                                  out=ab(o_h2T + o_ * 1024, 512), in_=pap_,
                                  func=AF.Gelu, bias=b2s[:, o_:o_ + 1])))(o, th, pap),
                              reads=[sb2[0], b_bias], writes=[b_h2T[o]],
                              tag=f"h2_{ch}_{it}_{th}_{o}" + sfx)
                    for o in range(2):
                        sl, sb2 = ps_s[o % 2], b_s[o % 2]
                        pap = sl[:, 512:1024]

                        def mk_w3(o_, th_, pap_):
                            def f(e):
                                r = None
                                for fb in range(4):
                                    r = e.matmul(pap_, ab(o_w3 + (o_ * 4 + fb) * 256, 128),
                                                 ab(o_h2T + fb * 1024, 512),
                                                 start=(fb == 0), stop=(fb == 3),
                                                 skip_group_check=True)
                                return r
                            return f
                        P.add("tensor", mk_w3(o, th, pap), reads=[b_w3] + list(b_h2T),
                              writes=[sb2[1]], tag=f"w3_{ch}_{it}_{th}_{o}" + sfx)
                        P.add("scalar",
                              (lambda o_, th_, pap_: (lambda e: e.activation(
                                  out=ab(o_h3T + o_ * 2048 + th_ * 1024, 512), in_=pap_,
                                  func=AF.Gelu, bias=b3s[:, o_:o_ + 1])))(o, th, pap),
                              reads=[sb2[1], b_bias], writes=[b_h3T[o]],
                              tag=f"h3_{ch}_{it}_{th}_{o}" + sfx)
                    for j in range(4 * th, 4 * th + 4):
                        def mk_tf(j_):
                            def f(e):
                                r = None
                                for fb in range(2):
                                    r = e.matmul(ps_a[:, 24 + j_:25 + j_],
                                                 AP(arena_bt,
                                                    o_h3T // 2 + fb * 1024 + 128 * (j_ % 4)
                                                    + (j_ // 4) * 512,
                                                    [[AB_ROW, 128], [1, 128]]),
                                                 w4t[:, fb:fb + 1], start=(fb == 0),
                                                 stop=(fb == 1), skip_group_check=True)
                                return r
                            return f
                        P.add("tensor", mk_tf(j), reads=list(b_h3T) + [b_w4],
                              writes=[b_pa], tag=f"tf{ch}_{it}_{j}" + sfx)
                    P.add("scalar",
                          (lambda th_: (lambda e: e.activation(
                              out=sigt[:, 4 * th_:4 * th_ + 4],
                              in_=ps_a[:, 24 + 4 * th_:28 + 4 * th_],
                              func=AF.Tanh, scale=0.5, bias=b4h[:])))(th),
                          reads=[b_pa, b_b4], writes=[b_sig], tag=f"sig{ch}_{it}_{th}" + sfx)
                    P.add("vector",
                          (lambda th_: (lambda e: e.tensor_scalar(
                              out=cpl[:, 4 * th_:4 * th_ + 4],
                              in0=sigt[:, 4 * th_:4 * th_ + 4],
                              scalar1=sc12[:, 1:2], scalar2=sc12[:, 0:1],
                              op0=ALU.mult, op1=ALU.add)))(th),
                          reads=[b_sig, b_sc], writes=[b_cpl], tag=f"cpl{ch}_{it}_{th}" + sfx)
                    P.add("vector",
                          (lambda th_: (lambda e: e.tensor_scalar_mul(
                              out=ncpl[:, 4 * th_:4 * th_ + 4],
                              in0=cpl[:, 4 * th_:4 * th_ + 4], scalar1=-1.0)))(th),
                          reads=[b_cpl], writes=[b_cpl], tag=f"ncpl{ch}_{it}_{th}" + sfx)
                    for j in range(4 * th, 4 * th + 4):
                        half = j % 2
                        for sv in range(2):
                            P.add("scalar",
                                  (lambda j_, s_, h_: (lambda e: e.tensor_copy(
                                      out=af(o_twt + h_ * 4096 + 1024 * s_, 256),
                                      in0=af(o_state + j_ * 4096 + 1024 * s_, 256))))(
                                          j, sv, half) if False else
                                  (lambda j_, s_, h_: (lambda e: e.activation(
                                      out=af(o_twt + h_ * 4096 + 1024 * s_, 256),
                                      in_=af(o_state + j_ * 4096 + 1024 * s_, 256),
                                      func=AF.Identity)))(j, sv, half),
                                  reads=[b_state[j]], writes=[b_twt[half * 4 + sv]],
                                  tag=f"twsv{ch}_{it}_{j}_{sv}" + sfx)
                        for (dc, s_src, neg, from_tmp) in (
                                (0, 2, False, False), (1, 3, True, False),
                                (2, 0, True, True), (3, 1, False, True)):
                            src_ap = (af(o_twt + (j % 2) * 4096 + 1024 * s_src, 256)
                                      if from_tmp else
                                      af(o_state + j * 4096 + 1024 * s_src, 256))
                            rbufs = ([b_twt[(j % 2) * 4 + s_src]] if from_tmp else [])                                 + [b_state[j], b_cpl]
                            P.add("vector",
                                  (lambda j_, d_, n_, sap: (lambda e: e.scalar_tensor_tensor(
                                      out=af(o_state + j_ * 4096 + 1024 * d_, 256),
                                      in0=sap, scalar=(ncpl if n_ else cpl)[:, j_:j_ + 1],
                                      in1=af(o_state + j_ * 4096 + 1024 * d_, 256),
                                      op0=ALU.mult, op1=ALU.add)))(j, dc, neg, src_ap),
                                  reads=rbufs, writes=[b_state[j]],
                                  tag=f"tw{ch}_{it}_{j}_{dc}" + sfx)

            for j in range(NJ):
                t = NJ * ch + j
                ln_stats(af(o_state + j * 4096, 1024), cSf + j, cQf + j,
                         [b_state[j]], bSf, bQf, tag=f"f{t}" + sfx)
            ln_rowmath(cSf, cQf, cAf, cBf, bSf, bQf, bAf, bBf, NJ, tag=f"f{ch}" + sfx)
            for j in range(NJ):
                t = NJ * ch + j
                P.add("scalar",
                      (lambda t_, j_: (lambda e: e.activation(
                          out=stream[:, 1024 * t_:1024 * (t_ + 1)],
                          in_=stream[:, 1024 * t_:1024 * (t_ + 1)], func=AF.Identity,
                          bias=stats[:, cBf + j_:cBf + j_ + 1])))(t, j),
                      reads=[bBf, b_stream[t]], writes=[b_stream[t]], tag=f"fapb{t}" + sfx)
                P.add("vector",
                      (lambda t_, j_: (lambda e: e.scalar_tensor_tensor(
                          out=stream[:, 1024 * t_:1024 * (t_ + 1)],
                          in0=af(o_state + j_ * 4096, 1024),
                          scalar=stats[:, cAf + j_:cAf + j_ + 1],
                          in1=stream[:, 1024 * t_:1024 * (t_ + 1)],
                          op0=ALU.mult, op1=ALU.add)))(t, j),
                      reads=[b_state[j], bAf, b_stream[t]], writes=[b_stream[t]],
                      tag=f"fres{t}" + sfx)
                outap = dout.ap()[s].rearrange("(t p) c -> p t c", p=128)[:, t:t + 1, :]
                P.add("sync",
                      (lambda t_, o0: (lambda e: e.dma_start(
                          out=o0, in_=stream[:, 1024 * t_:1024 * (t_ + 1)])))(t, outap),
                      reads=[b_stream[t]], writes=[b_stored[s * TT + t]], dma=True,
                      tag=f"stout{t}" + sfx)
                if ch == 0:
                    emit_st(1, j)
                    if j < 4:
                        castrf(1, 0, 0, js=[j])

    P.add("sync", lambda e: e.nop(), reads=b_stored, tag="fin")

    P.prepare()

    @block.tensor
    def _(eng):
        P.emit_engine("tensor", eng, sems)

    @block.vector
    def _(eng):
        P.emit_engine("vector", eng, sems)

    @block.scalar
    def _(eng):
        P.emit_engine("scalar", eng, sems)

    @block.sync
    def _(eng):
        P.emit_engine("sync", eng, sems)

    @block.gpsimd
    def _(eng):
        P.emit_engine("gpsimd", eng, sems)

    es.close()
    return nc



def _host_inputs(inp):
    perm = np.array([(j % 64) * 16 + j // 64 for j in range(D)], dtype=np.int64)
    qw = np.ascontiguousarray(np.asarray(inp["L0_attn_qw"], np.float32))[:, perm]
    kw = np.ascontiguousarray(np.asarray(inp["L0_attn_kw"], np.float32))[:, perm]
    vw = np.ascontiguousarray(np.asarray(inp["L0_attn_vw"], np.float32))[:, perm]
    ow = np.ascontiguousarray(np.asarray(inp["L0_attn_ow"], np.float32))[perm, :]

    def tile_pair(W):
        A = W.reshape(8, 128, 8, 128)
        return np.ascontiguousarray(A.transpose(2, 1, 0, 3).reshape(8, 128, 1024))

    wqkv = np.stack([tile_pair(qw), tile_pair(kw), tile_pair(vw)])
    owt = np.ascontiguousarray(ow.reshape(8, 128, 1024).transpose(1, 0, 2).reshape(128, 8192))

    def tile_of(W, no, nf):
        A = np.asarray(W, np.float32).reshape(nf, 128, no, 128)
        return np.ascontiguousarray(A.transpose(1, 2, 0, 3).reshape(128, no * nf * 128))

    w1t = tile_of(inp["L0_ffn_cn_w1"], 8, 8)
    w2t = tile_of(inp["L0_ffn_cn_w2"], 4, 8)
    w3t = tile_of(inp["L0_ffn_cn_w3"], 2, 4)
    w4t = np.ascontiguousarray(np.asarray(inp["L0_ffn_cn_w4"], np.float32).reshape(2, 128).T)
    gw1t = tile_of(inp["L0_ffn_gc_w1"], 4, 8)
    gw2t = tile_of(inp["L0_ffn_gc_w2"], 2, 4)
    gw3t = np.ascontiguousarray(np.asarray(inp["L0_ffn_gc_w3"], np.float32).reshape(2, 128).T)

    def bias_t(b, no):
        return np.ascontiguousarray(np.asarray(b, np.float32).reshape(no, 128).T)

    f32 = np.float32
    ident = np.eye(128, dtype=f32).astype(md.bfloat16)
    m = np.zeros((128, 4, 512), f32)
    pk = np.arange(128)[:, None]
    fq = np.arange(512)[None, :]
    for i, off in enumerate((0, 128, 256, 384)):
        m[:, i, :] = (fq >= pk + off).astype(f32)
    masksv = np.ascontiguousarray(m.reshape(128, 2048)).astype(md.bfloat16)
    selpair = np.zeros((128, 256), f32)
    selpair[:, 0:64] = 1.0
    selpair[:, 192:256] = 1.0
    nb = np.zeros((128, 2), f32)
    nb[0:64, 0] = 1.0
    nb[64:128, 1] = 1.0

    return dict(
        wqkv=wqkv.astype(f32), owt=owt.astype(f32), w1t=w1t, w2t=w2t, w3t=w3t,
        w4t=w4t, gw1t=gw1t, gw2t=gw2t, gw3t=gw3t,
        b1=bias_t(inp["L0_ffn_cn_b1"], 8), b2=bias_t(inp["L0_ffn_cn_b2"], 4),
        b3=bias_t(inp["L0_ffn_cn_b3"], 2),
        b4=np.asarray(inp["L0_ffn_cn_b4"], f32).reshape(1, 1),
        gb1=bias_t(inp["L0_ffn_gc_b1"], 4), gb2=bias_t(inp["L0_ffn_gc_b2"], 2),
        gb3=np.asarray(inp["L0_ffn_gc_b3"], f32).reshape(1, 1),
        ar=np.asarray(inp["L0_ffn_adaptive_range"], f32).reshape(1, 1),
        ident=ident, masks=masksv, ones_f=np.ones((1, 128), f32),
        selp=selpair.astype(md.bfloat16), nb128=nb.astype(md.bfloat16),
        ones128=np.ones((128, 1), f32),
        onec=np.ones((128, 1), f32).astype(md.bfloat16),
    )


def kernel(**inputs):
    x = np.ascontiguousarray(np.asarray(inputs["L0_attn_x"], dtype=np.float32))
    common = _host_inputs(inputs)
    nc = build_nc()
    in_maps = []
    for c in range(NCORES):
        m = dict(common)
        m["x"] = np.ascontiguousarray(x[2 * c:2 * c + 2])
        in_maps.append(m)
    res = run_bass_kernel_spmd(nc, in_maps, core_ids=list(range(NCORES)))
    out = np.concatenate([np.asarray(r["out"], dtype=np.float32) for r in res.results],
                         axis=0)
    return out


# version 24
# speedup vs baseline: 1.1812x; 1.0491x over previous
import numpy as np
import ml_dtypes as md
import concourse.bass as bass
import concourse.mybir as mybir
from concourse import dve_ops
from concourse.bass_utils import run_bass_kernel_spmd

F32 = mybir.dt.float32
BF16 = mybir.dt.bfloat16
AF = mybir.ActivationFunctionType
ALU = mybir.AluOpType
AXX = mybir.AxisListType.X
AP = bass.AP

NCORES = 8
SEQS, S, D = 2, 2048, 1024
NPAIR, TT, QC = 8, 16, 4
NITER = 3
NJ, NTF = 8, 8
OPLIMIT = [None]


ENGS = ("tensor", "vector", "scalar", "sync", "gpsimd")
DRAIN_ENGS = frozenset(("vector", "scalar"))
SELFSEM_ENGS = frozenset(("sync", "gpsimd"))


class Buf:
    __slots__ = ("name", "w", "rs", "psum")

    def __init__(self, name="b", init_deps=(), psum=False):
        self.name, self.w, self.rs = name, None, list(init_deps)
        self.psum = psum


class Op:
    __slots__ = ("eng", "idx", "fn", "raw", "war", "inc", "flag", "semval", "tag", "skey")


class Prog:
    def __init__(self):
        self.ops = {e: [] for e in ENGS}

    def add(self, eng, fn, reads=(), writes=(), dma=False, tag=""):
        if OPLIMIT[0] is not None:
            self.total = getattr(self, "total", 0) + 1
            if self.total > OPLIMIT[0]:
                return ("dropped", -1)
        idx = len(self.ops[eng])
        tok = (eng, idx)
        op = Op()
        op.eng, op.idx, op.fn, op.tag = eng, idx, fn, tag
        op.inc = 16 if dma else 1
        op.flag = False
        op.semval = 0
        raw, war = set(), set()
        for b in reads:
            if b.w is not None:
                raw.add(b.w)
            if b.psum and eng in ("vector", "scalar"):
                for rtok in b.rs:
                    if rtok[0] != eng and rtok[0] in ("vector", "scalar"):
                        raw.add(rtok)
        for b in writes:
            if b.w is not None:
                raw.add(b.w)
            war.update(b.rs)
        raw.discard(tok)
        war.discard(tok)
        op.raw, op.war = raw, war
        self.ops[eng].append(op)
        for b in reads:
            b.rs.append(tok)
        for b in writes:
            b.w = tok
            b.rs = []
        return tok

    def prepare(self):
        for e in ENGS:
            for op in self.ops[e]:
                op.skey = (e + "_c") if (e == "gpsimd" and op.inc == 1) else e
                if op.inc == 16:
                    op.flag = True
                for (de, di) in list(op.raw) + list(op.war):
                    if de != e or de in SELFSEM_ENGS:
                        self.ops[de][di].flag = True
        for e in ENGS:
            cnt = {}
            for op in self.ops[e]:
                if op.flag:
                    cnt[op.skey] = cnt.get(op.skey, 0) + op.inc
                op.semval = cnt.get(op.skey, 0)
        self.collect_waitvals()

    def _op_needs(self, ename, op, last_drain):
        needs = {}
        drain_upto = -1
        for kind, deps in (("raw", op.raw), ("war", op.war)):
            for (de, di) in deps:
                if de == ename and de not in SELFSEM_ENGS:
                    if di > last_drain and ename in DRAIN_ENGS:
                        drain_upto = max(drain_upto, di)
                    continue
                dep = self.ops[de][di]
                needs[dep.skey] = max(needs.get(dep.skey, 0), dep.semval)
        return needs, drain_upto

    def collect_waitvals(self):
        self.waitvals = {}
        for ename in ENGS:
            seen = {}
            last_drain = -1
            for op in self.ops[ename]:
                needs, drain_upto = self._op_needs(ename, op, last_drain)
                if drain_upto >= 0:
                    last_drain = op.idx - 1
                for dk, v in needs.items():
                    if v > seen.get(dk, 0):
                        self.waitvals.setdefault(dk, set()).add(v)
                        seen[dk] = v

    def emit_engine(self, ename, eng, sems):
        seen = {}
        last_drain = -1
        qval = 0
        wv = self.waitvals.get(ename, ()) if ename in SELFSEM_ENGS else ()
        for op in self.ops[ename]:
            needs, drain_upto = self._op_needs(ename, op, last_drain)
            if drain_upto >= 0:
                eng.drain()
                last_drain = op.idx - 1
            if op.flag and op.inc == 16 and qval in wv and qval > seen.get(ename, 0):
                needs[ename] = max(needs.get(ename, 0), qval)
            for dk in sorted(needs):
                if needs[dk] > seen.get(dk, 0):
                    eng.wait_ge(sems[dk], needs[dk])
                    seen[dk] = needs[dk]
            inst = op.fn(eng)
            if op.flag:
                assert inst is not None, f"flagged op lacks inst: {ename}#{op.idx} {op.tag}"
                inst.then_inc(sems[op.skey], op.inc)
                if op.inc == 16:
                    qval = op.semval


class Arena:
    def __init__(self, nbytes):
        self.nbytes = nbytes
        self.active = []
        self.cur = 0
        self.peak = 0

    def reset(self):
        self.cur = 0

    def alloc(self, name, nbytes, nbufs=1):
        start = (self.cur + 63) & ~63
        end = start + nbytes
        assert end <= self.nbytes, f"arena overflow: {name} needs {end}"
        self.cur = end
        self.peak = max(self.peak, end)
        init, keep = [], []
        for (s0, e0, b) in self.active:
            if s0 < end and e0 > start:
                if b.w is not None:
                    init.append(b.w)
                init.extend(b.rs)
                if s0 < start:
                    keep.append((s0, start, b))
                if e0 > end:
                    keep.append((end, e0, b))
            else:
                keep.append((s0, e0, b))
        bufs = [Buf(f"{name}{i}", init) for i in range(nbufs)]
        for b in bufs:
            keep.append((start, end, b))
        self.active = keep
        return start, (bufs if nbufs > 1 else bufs[0])



def build_nc():
    nc = bass.Bass("TRN2", target_bir_lowering=False)

    dx = nc.dram_tensor("x", [SEQS, S, D], F32, kind="ExternalInput")
    dwqkv = nc.dram_tensor("wqkv", [3, NPAIR, 128, 1024], F32, kind="ExternalInput")
    dowt = nc.dram_tensor("owt", [128, 8192], F32, kind="ExternalInput")
    dw1 = nc.dram_tensor("w1t", [128, 8192], F32, kind="ExternalInput")
    dw2 = nc.dram_tensor("w2t", [128, 4096], F32, kind="ExternalInput")
    dw3 = nc.dram_tensor("w3t", [128, 1024], F32, kind="ExternalInput")
    dw4 = nc.dram_tensor("w4t", [128, 2], F32, kind="ExternalInput")
    dgw1 = nc.dram_tensor("gw1t", [128, 4096], F32, kind="ExternalInput")
    dgw2 = nc.dram_tensor("gw2t", [128, 1024], F32, kind="ExternalInput")
    dgw3 = nc.dram_tensor("gw3t", [128, 2], F32, kind="ExternalInput")
    db1 = nc.dram_tensor("b1", [128, 8], F32, kind="ExternalInput")
    db2 = nc.dram_tensor("b2", [128, 4], F32, kind="ExternalInput")
    db3 = nc.dram_tensor("b3", [128, 2], F32, kind="ExternalInput")
    db4 = nc.dram_tensor("b4", [1, 1], F32, kind="ExternalInput")
    dgb1 = nc.dram_tensor("gb1", [128, 4], F32, kind="ExternalInput")
    dgb2 = nc.dram_tensor("gb2", [128, 2], F32, kind="ExternalInput")
    dgb3 = nc.dram_tensor("gb3", [1, 1], F32, kind="ExternalInput")
    dar = nc.dram_tensor("ar", [1, 1], F32, kind="ExternalInput")
    dident = nc.dram_tensor("ident", [128, 128], BF16, kind="ExternalInput")
    dmasks = nc.dram_tensor("masks", [128, 2048], BF16, kind="ExternalInput")
    dones = nc.dram_tensor("ones_f", [1, 128], F32, kind="ExternalInput")
    dones128 = nc.dram_tensor("ones128", [128, 1], F32, kind="ExternalInput")
    dselp = nc.dram_tensor("selp", [128, 256], BF16, kind="ExternalInput")
    dnb = nc.dram_tensor("nb128", [128, 2], BF16, kind="ExternalInput")
    donec = nc.dram_tensor("onec", [128, 1], BF16, kind="ExternalInput")
    dout = nc.dram_tensor("out", [SEQS, S, D], F32, kind="ExternalOutput")

    from contextlib import ExitStack
    es = ExitStack()

    def sbt(name, shape, dt):
        return es.enter_context(nc.sbuf_tensor(name, shape, dt))

    stream = sbt("stream", [128, TT * 1024], F32)
    lnsc = sbt("lnsc", [128, 1024], F32)
    lnsc2 = sbt("lnsc2", [128, 1024], F32)
    castb = sbt("castb", [128, 4096], BF16)
    stats = sbt("stats", [128, 184], F32)
    masks = sbt("masks_s", [128, 2048], BF16)
    ident = sbt("ident_s", [128, 128], BF16)
    ones_f = sbt("ones_fs", [1, 128], F32)
    ones_f128c = sbt("ones_f128", [128, 1], F32)
    selp = sbt("selp_s", [128, 256], BF16)
    nb128 = sbt("nb128_s", [128, 2], BF16)
    onec = sbt("onec_s", [128, 1], BF16)
    eps = sbt("eps", [128, 1], F32)
    w4t = sbt("w4t_s", [128, 2], BF16)
    b1s = sbt("b1s", [128, 8], F32)
    b2s = sbt("b2s", [128, 4], F32)
    b3s = sbt("b3s", [128, 2], F32)
    b4bc = sbt("b4bc", [128, 1], F32)
    b4h = sbt("b4h", [128, 1], F32)
    gb3h = sbt("gb3h", [1, 1], F32)
    gb1s = sbt("gb1s", [128, 4], F32)
    gb2s = sbt("gb2s", [128, 2], F32)
    gb3s = sbt("gb3s", [1, 1], F32)
    ars = sbt("ars", [1, 1], F32)
    rowz = sbt("rowz", [128, 1024], F32)
    rowzb = sbt("rowzb", [128, 1024], BF16)
    zbc = sbt("zbc", [128, 512], BF16)
    negM = sbt("negM", [128, 2], F32)
    sc12 = sbt("sc12", [128, 2], F32)
    cpl = sbt("cpl", [128, 8], F32)
    ncpl = sbt("ncpl", [128, 8], F32)
    sigt = sbt("sigt", [128, 8], F32)
    gcrowb = sbt("gcrowb", [1, 1024], BF16)
    gcT = sbt("gcT", [128, 8], BF16)
    hg1 = sbt("hg1", [128, 4], BF16)
    hg2 = sbt("hg2", [128, 2], BF16)
    grow = sbt("grow", [1, 8], F32)
    normrow = sbt("normrow", [2, 8], F32)

    ARENA_F32 = 28688
    arena_t = sbt("arena", [128, ARENA_F32], F32)
    arena_bt = arena_t.bitcast(BF16)
    AB_ROW = 2 * ARENA_F32
    arena = Arena(ARENA_F32 * 4)

    def af(off_b, cols, p0=0, np_=128):
        assert off_b % 4 == 0
        return AP(arena_t, (off_b // 4) + p0 * ARENA_F32, [[ARENA_F32, np_], [1, cols]])

    def ab(off_b, cols, p0=0, np_=128):
        assert off_b % 2 == 0
        return AP(arena_bt, (off_b // 2) + p0 * AB_ROW, [[AB_ROW, np_], [1, cols]])

    ps_s = [es.enter_context(nc.psum_tensor(f"ps_s{i}", [128, 1024], F32)) for i in range(2)]
    ps_a = es.enter_context(nc.psum_tensor("ps_a", [128, 512], F32))
    ps_b = es.enter_context(nc.psum_tensor("ps_b", [128, 512], F32))
    ps_t0 = es.enter_context(nc.psum_tensor("ps_t0", [128, 1024], BF16))
    ps_c = es.enter_context(nc.psum_tensor("ps_c", [128, 512], F32))
    ps_t = [ps_t0, ps_c.bitcast(BF16)]

    sems = {e: es.enter_context(nc.semaphore(f"sem_{e}")) for e in ENGS}
    sems["gpsimd_c"] = es.enter_context(nc.semaphore("sem_gpsimd_c"))
    block = es.enter_context(nc.Block())

    P = Prog()

    def ZR(qc):
        return (32 * qc) if qc < 3 else 0
    b_s = [[Buf("s0a", psum=True), Buf("s0b", psum=True)],
           [Buf("s1a", psum=True), Buf("s1b", psum=True)]]
    b_pa, b_pb = Buf("pa", psum=True), Buf("pb", psum=True)
    b_pt = [Buf("pt0", psum=True), Buf("pt1", psum=True)]
    b_stream = [Buf(f"st{t}") for t in range(TT)]
    b_lnsc = Buf("lnsc")
    b_lnsc2 = Buf("lnsc2")
    dumpctr = [0]
    b_castb = [Buf(f"cb{i}") for i in range(4)]
    b_masks, b_ident, b_ones, b_selp, b_nb = Buf("mk"), Buf("id"), Buf("on"), Buf("sp"), Buf("nb")
    b_onec = Buf("onec")
    b_bias, b_eps, b_w4, b_ar, b_b4 = Buf("bi"), Buf("ep"), Buf("w4"), Buf("ar"), Buf("b4")
    b_zq = [Buf(f"zq{i}") for i in range(4)]
    b_zrecb = Buf("zrecb")
    b_zbc = Buf("zbc")
    b_negMs = [Buf("negM0"), Buf("negM1")]
    b_norm = Buf("norm")
    b_cpl, b_sig, b_sc = Buf("cpl"), Buf("sig"), Buf("sc")
    b_grow, b_gc = Buf("grow"), Buf("gc")
    b_stored = [Buf(f"so{i}") for i in range(SEQS * TT)]
    b_stats = {}
    SCOL = [0]

    def stat_col(name, width=16):
        if name not in b_stats:
            b_stats[name] = (SCOL[0], Buf(name))
            SCOL[0] += width
            assert SCOL[0] <= 184, name
        return b_stats[name]

    def dma_to(dst, src, eng="sync", w=(), r=(), tag=""):
        P.add(eng, (lambda d0, s0: (lambda e: e.dma_start(out=d0, in_=s0)))(dst, src),
              reads=list(r), writes=list(w), dma=True, tag=tag)

    dma_to(masks[:], dmasks.ap(), w=[b_masks], tag="ldmask")
    dma_to(ident[:], dident.ap(), w=[b_ident])
    dma_to(ones_f[:], dones.ap(), w=[b_ones])
    dma_to(ones_f128c[:], dones128.ap(), r=[b_ones], w=[b_ones])
    dma_to(selp[:], dselp.ap(), w=[b_selp])
    dma_to(nb128[:], dnb.ap(), w=[b_nb])
    dma_to(onec[:], donec.ap(), w=[b_onec])
    dma_to(b1s[:], db1.ap(), w=[b_bias])
    dma_to(b2s[:], db2.ap(), r=[b_bias], w=[b_bias])
    dma_to(b3s[:], db3.ap(), r=[b_bias], w=[b_bias])
    dma_to(gb1s[:], dgb1.ap(), r=[b_bias], w=[b_bias])
    dma_to(gb2s[:], dgb2.ap(), r=[b_bias], w=[b_bias])
    dma_to(gb3s[:], dgb3.ap(), r=[b_bias], w=[b_bias])
    dma_to(ars[:], dar.ap(), w=[b_ar])
    dma_to(grow[0:1, 0:1], db4.ap(), w=[b_grow], tag="ldb4")
    P.add("vector", lambda e: e.memset(eps[:], 1e-5), writes=[b_eps])
    dma_to(w4t[:], dw4.ap(), eng="gpsimd", w=[b_w4])
    P.add("tensor", lambda e: e.matmul(ps_a[:, 0:1], ones_f[:], grow[0:1, 0:1],
                                       start=True, stop=True, skip_group_check=True),
          reads=[b_ones, b_grow], writes=[b_pa], tag="b4bc")
    P.add("vector", lambda e: e.tensor_copy(out=b4bc[:], in_=ps_a[:, 0:1]),
          reads=[b_pa], writes=[b_b4])
    P.add("vector", lambda e: e.tensor_scalar_mul(out=b4h[:], in0=b4bc[:], scalar1=0.5),
          reads=[b_b4], writes=[b_b4])
    P.add("vector", lambda e: e.tensor_scalar_mul(out=gb3h[0:1, :], in0=gb3s[0:1, :],
                                                  scalar1=0.5),
          reads=[b_bias], writes=[b_bias])

    def ln_stats(tile_ap, col_sum, col_ssq, src_bufs, bsum, bssq, tag=""):
        P.add("vector",
              (lambda a0, c0: (lambda e: e.reduce_sum(
                  out=stats[:, c0:c0 + 1], in_=a0, axis=AXX)))(tile_ap, col_sum),
              reads=list(src_bufs), writes=[bsum], tag="lnsum" + tag)
        dumpctr[0] ^= 1
        dmp, bdmp = (lnsc2, b_lnsc2) if dumpctr[0] else (lnsc, b_lnsc)
        P.add("scalar",
              (lambda a0, c0, d0: (lambda e: e.activation(
                  out=d0[:], in_=a0, func=AF.Square,
                  accum_out=stats[:, c0:c0 + 1])))(tile_ap, col_ssq, dmp),
              reads=list(src_bufs), writes=[bdmp, bssq], tag="lnssq" + tag)

    def ln_rowmath(c_sum, c_ssq, c_A, c_B, bsum, bssq, bA, bB, n, tag="", extra_wA=()):
        c_t, bt = stat_col("tmp0")
        P.add("vector",
              (lambda: (lambda e: e.tensor_scalar_mul(
                  out=stats[:, c_t:c_t + n], in0=stats[:, c_sum:c_sum + n],
                  scalar1=1.0 / D)))(),
              reads=[bsum], writes=[bt], tag="rm_m" + tag)
        P.add("vector",
              (lambda: (lambda e: e.tensor_mul(
                  out=stats[:, c_A:c_A + n], in0=stats[:, c_t:c_t + n],
                  in1=stats[:, c_t:c_t + n])))(),
              reads=[bt], writes=[bA], tag="rm_m2" + tag)
        P.add("vector",
              (lambda: (lambda e: e.scalar_tensor_tensor(
                  out=stats[:, c_B:c_B + n], in0=stats[:, c_ssq:c_ssq + n],
                  scalar=1.0 / D, in1=stats[:, c_A:c_A + n],
                  op0=ALU.mult, op1=ALU.subtract)))(),
              reads=[bssq, bA], writes=[bB], tag="rm_var" + tag)
        P.add("scalar",
              (lambda: (lambda e: e.activation(
                  out=stats[:, c_B:c_B + n], in_=stats[:, c_B:c_B + n],
                  func=AF.Ln, bias=eps[:])))(),
              reads=[bB, b_eps], writes=[bB], tag="rm_ln" + tag)
        P.add("scalar",
              (lambda: (lambda e: e.activation(
                  out=stats[:, c_A:c_A + n], in_=stats[:, c_B:c_B + n],
                  func=AF.Exp, scale=-0.5)))(),
              reads=[bB], writes=[bA] + list(extra_wA), tag="rm_exp" + tag)
        P.add("vector",
              (lambda: (lambda e: e.scalar_tensor_tensor(
                  out=stats[:, c_B:c_B + n], in0=stats[:, c_t:c_t + n],
                  scalar=-1.0, in1=stats[:, c_A:c_A + n],
                  op0=ALU.mult, op1=ALU.mult)))(),
              reads=[bt, bA], writes=[bB], tag="rm_B" + tag)

    for s in range(SEQS):
        sfx = f"_s{s}"
        arena.reset()
        o_ln1t, b_ln1t = arena.alloc("ln1t", TT * 1024 * 2, nbufs=TT)
        o_attnN, b_attnN = arena.alloc("attnN", NPAIR * 2048 * 2, nbufs=NPAIR * QC)
        o_qt, b_qt = arena.alloc("qt", 2048 * 2, nbufs=QC)
        o_kt0, b_kts = arena.alloc("kt", 2 * 2048 * 2, nbufs=2 * QC)
        o_v, b_v = arena.alloc("v", 16 * 130 * 2, nbufs=4)
        o_ptl, b_ptl = arena.alloc("ptl", 2 * 1024 * 2, nbufs=2)
        o_qkvw, b_qkvw = arena.alloc("qkvw", 2 * 3 * 1024 * 2, nbufs=6)
        o_owt, b_owt = arena.alloc("owt", 8192 * 2, nbufs=8)

        def ln1T_ap(f, c0, ncols, p0=0, np_=128):
            return AP(arena_bt, o_ln1t // 2 + 2048 * f + c0 + p0 * AB_ROW,
                      [[AB_ROW, np_], [1, ncols]])

        for base in (64, 129):
            P.add("vector",
                  (lambda b0, ov: (lambda e: e.memset(
                      AP(arena_bt, ov // 2 + b0, [[AB_ROW, 128], [130, 16], [1, 1]]),
                      1.0)))(base, o_v),
                  writes=list(b_v), tag="vones" + sfx)

        for g in range(4):
            src = dx.ap()[s].rearrange("(t p) c -> p t c", p=128)[:, 4 * g:4 * g + 4, :]
            dst = AP(stream, 4096 * g, [[TT * 1024, 128], [1024, 4], [1, 1024]])
            dma_to(dst, src, w=b_stream[4 * g:4 * g + 4], tag=f"ldx{g}" + sfx)

        cS, bS = stat_col("l1sum")
        cQ, bQ = stat_col("l1ssq")
        cA1, bA1 = stat_col("A1")
        cB1, bB1 = stat_col("B1")
        for t in range(TT):
            ln_stats(stream[:, 1024 * t:1024 * (t + 1)], cS + t, cQ + t,
                     [b_stream[t]], bS, bQ, tag=f"1_{t}" + sfx)
        ln_rowmath(cS, cQ, cA1, cB1, bS, bQ, bA1, bB1, TT, tag="1" + sfx)

        for t in range(TT):
            c = t % 4
            P.add("vector",
                  (lambda t_, c_: (lambda e: e.tensor_scalar(
                      out=castb[:, 1024 * c_:1024 * (c_ + 1)],
                      in0=stream[:, 1024 * t_:1024 * (t_ + 1)],
                      scalar1=stats[:, cA1 + t_:cA1 + t_ + 1],
                      scalar2=stats[:, cB1 + t_:cB1 + t_ + 1],
                      op0=ALU.mult, op1=ALU.add)))(t, c),
                  reads=[b_stream[t], bA1, bB1], writes=[b_castb[c]],
                  tag=f"ln1ap{t}" + sfx)
            pt, bpt = ps_t[t % 2], b_pt[t % 2]

            def mk_tr(c_, pt_):
                def f(e):
                    r = None
                    for fb in range(8):
                        r = e.transpose(pt_[:, 128 * fb:128 * fb + 128],
                                        castb[:, 1024 * c_ + 128 * fb:1024 * c_ + 128 * fb + 128],
                                        ident[:])
                    return r
                return f
            P.add("tensor", mk_tr(c, pt), reads=[b_castb[c], b_ident], writes=[bpt],
                  tag=f"ln1tr{t}" + sfx)
            dstap = AP(arena_bt, o_ln1t // 2 + 128 * t, [[AB_ROW, 128], [2048, 8], [1, 128]])
            P.add("scalar",
                  (lambda pt_, d0: (lambda e: e.copy(out=d0, in_=pt_[:, 0:1024])))(pt, dstap),
                  reads=[bpt], writes=[b_ln1t[t]], tag=f"ln1ev{t}" + sfx)

        def load_qkv(p):
            slot = p % 2
            for m in range(3):
                dst = ab(o_qkvw + (slot * 3 + m) * 2048, 1024)
                dma_to(dst, dwqkv.ap()[m, p], eng="gpsimd",
                       w=[b_qkvw[slot * 3 + m]], tag=f"ldqkv{p}_{m}" + sfx)

        load_qkv(0)
        load_qkv(1)

        def emit_norms(tp):
            kpar = tp % 2
            o_ktp = o_kt0 + kpar * 4096
            bkt = b_kts[kpar * 4: kpar * 4 + 4]
            sqs_ap = ab(o_ptl, 1024)
            for (src_o, sbufs, w_) in ((o_qt, b_qt, 0), (o_ktp, bkt, 1)):
                for h2 in range(2):
                    sl, sb2 = ps_s[h2], b_s[h2]
                    P.add("vector",
                          (lambda so, h_: (lambda e: e.tensor_mul(
                              out=sqs_ap, in0=ab(so + h_ * 2048, 1024),
                              in1=ab(so + h_ * 2048, 1024))))(src_o, h2),
                          reads=[sbufs[2 * h2], sbufs[2 * h2 + 1]], writes=[b_ptl[0]],
                          tag=f"sq{tp}" + sfx)
                    for cc in range(2):
                        P.add("tensor",
                              (lambda sl_, cc_: (lambda e: e.matmul(
                                  sl_[0:2, 512 * cc_:512 * cc_ + 512], nb128[:],
                                  ab(o_ptl + 1024 * cc_, 512),
                                  start=True, stop=True, skip_group_check=True)))(sl, cc),
                              reads=[b_ptl[0], b_nb], writes=[sb2[cc]], tag=f"nmm{tp}" + sfx)
                    P.add("vector",
                          (lambda sl_, col: (lambda e: e.reduce_max(
                              out=normrow[:, col:col + 1], in_=sl_[0:2, 0:1024],
                              axis=AXX)))(sl, 4 * w_ + h2),
                          reads=[sb2[0], sb2[1]], writes=[b_norm], tag=f"nrd{tp}" + sfx)
            P.add("vector", lambda e: e.tensor_max(out=normrow[:, 0:1],
                                                   in0=normrow[:, 0:1], in1=normrow[:, 1:2]),
                  reads=[b_norm], writes=[b_norm])
            P.add("vector", lambda e: e.tensor_max(out=normrow[:, 4:5],
                                                   in0=normrow[:, 4:5], in1=normrow[:, 5:6]),
                  reads=[b_norm], writes=[b_norm])
            P.add("scalar", lambda e: e.activation(out=normrow[:, 2:3], in_=normrow[:, 0:1],
                                                   func=AF.Ln),
                  reads=[b_norm], writes=[b_norm])
            P.add("scalar", lambda e: e.activation(out=normrow[:, 6:7], in_=normrow[:, 4:5],
                                                   func=AF.Ln),
                  reads=[b_norm], writes=[b_norm])
            P.add("vector", lambda e: e.tensor_add(out=normrow[:, 3:4], in0=normrow[:, 2:3],
                                                   in1=normrow[:, 6:7]),
                  reads=[b_norm], writes=[b_norm])
            P.add("scalar", lambda e: e.activation(out=normrow[:, 7:8], in_=normrow[:, 3:4],
                                                   func=AF.Exp, scale=0.5),
                  reads=[b_norm], writes=[b_norm])
            P.add("vector", lambda e: e.tensor_scalar_mul(out=normrow[0:1, 7:8],
                                                          in0=normrow[0:1, 7:8],
                                                          scalar1=-1.02),
                  reads=[b_norm], writes=[b_norm])
            P.add("tensor", lambda e: e.matmul(ps_a[:, 0:1], ones_f[:], normrow[0:1, 7:8],
                                               start=True, stop=True, skip_group_check=True),
                  reads=[b_norm, b_ones], writes=[b_pa], tag=f"Mbc{tp}" + sfx)
            P.add("vector",
                  (lambda kp: (lambda e: e.tensor_copy(
                      out=negM[:, kp:kp + 1], in_=ps_a[:, 0:1])))(kpar),
                  reads=[b_pa], writes=[b_negMs[kpar]])

        for p in range(NPAIR):
            slot = p % 2
            kpar = p % 2
            o_ktp = o_kt0 + kpar * 4096
            b_kt = b_kts[kpar * 4: kpar * 4 + 4]
            wq = ab(o_qkvw + (slot * 3 + 0) * 2048, 1024)
            wk = ab(o_qkvw + (slot * 3 + 1) * 2048, 1024)
            wv = ab(o_qkvw + (slot * 3 + 2) * 2048, 1024)
            bwq, bwk, bwv = (b_qkvw[slot * 3 + m] for m in range(3))

            if p == 0:
                for (wt, bw, dsto, dbufs, scale, nm) in (
                        (wq, bwq, o_qt, b_qt, 1.0 / 32, "q"),
                        (wk, bwk, o_ktp, b_kt, None, "k")):
                    for qc in range(QC):
                        sl, sb2 = ps_s[qc // 2], b_s[qc // 2]
                        half = qc % 2
                        pap = sl[:, 512 * half:512 * half + 512]

                        def mk_proj(wt_, qc_, pap_):
                            def f(e):
                                r = None
                                for fb in range(8):
                                    r = e.matmul(pap_, wt_[:, 128 * fb:128 * fb + 128],
                                                 ln1T_ap(fb, 512 * qc_, 512),
                                                 start=(fb == 0), stop=(fb == 7),
                                                 skip_group_check=True)
                                return r
                            return f
                        P.add("tensor", mk_proj(wt, qc, pap),
                              reads=[bw] + b_ln1t[4 * qc:4 * qc + 4], writes=[sb2[half]],
                              tag=f"prj{nm}{p}_{qc}" + sfx)
                        dstap = ab(dsto + 1024 * qc, 512)
                        if scale is None:
                            P.add("vector",
                                  (lambda pap_, d0: (lambda e: e.tensor_copy(
                                      out=d0, in_=pap_)))(pap, dstap),
                                  reads=[sb2[half]], writes=[dbufs[qc]],
                                  tag=f"ev{nm}{p}_{qc}" + sfx)
                        else:
                            P.add("vector",
                                  (lambda pap_, d0, sc_: (lambda e: e.tensor_scalar_mul(
                                      out=d0, in0=pap_, scalar1=sc_)))(pap, dstap, scale),
                                  reads=[sb2[half]], writes=[dbufs[qc]],
                                  tag=f"ev{nm}{p}_{qc}" + sfx)
                emit_norms(0)

            for g in range(4):
                sl, sb2 = ps_s[g % 2], b_s[g % 2]
                for j in range(4):
                    kt = 4 * g + j
                    half, sub = j // 2, j % 2
                    pap = sl[:, 512 * half + 128 * sub: 512 * half + 128 * sub + 128]

                    def mk_v(kt_, pap_, wv_):
                        def f(e):
                            r = None
                            for fb in range(8):
                                r = e.matmul(pap_, ln1T_ap(fb, 128 * kt_, 128),
                                             wv_[:, 128 * fb:128 * fb + 128],
                                             start=(fb == 0), stop=(fb == 7),
                                             skip_group_check=True)
                            return r
                        return f
                    P.add("tensor", mk_v(kt, pap, wv),
                          reads=[bwv, b_ln1t[kt]], writes=[sb2[half]],
                          tag=f"prjv{p}_{kt}" + sfx)
                for j in range(4):
                    kt = 4 * g + j
                    half, sub = j // 2, j % 2
                    srcap = AP(sl, 512 * half + 128 * sub, [[1024, 128], [64, 2], [1, 64]])
                    dstap = AP(arena_bt, o_v // 2 + 130 * kt, [[AB_ROW, 128], [65, 2], [1, 64]])
                    P.add("vector",
                          (lambda s0, d0: (lambda e: e.tensor_copy(out=d0, in_=s0)))(srcap, dstap),
                          reads=[sb2[half]], writes=[b_v[g]], tag=f"evv{p}_{kt}" + sfx)

            if p + 2 < NPAIR:
                load_qkv(p + 2)
            if p == 5:
                for fb in range(8):
                    dma_to(ab(o_owt + fb * 2048, 1024),
                           dowt.ap()[:, 1024 * fb:1024 * fb + 1024], eng="gpsimd",
                           w=[b_owt[fb]], tag=f"ldow{fb}" + sfx)

            def ptl_slot(kt):
                sl3 = kt % 3
                if sl3 < 2:
                    return arena_bt, (o_ptl + 2048 * sl3) // 2, AB_ROW, b_ptl[sl3]
                return lnsc2.bitcast(BF16), 0, 2048, b_lnsc2

            def emit_av(qc, kt, nkt):
                off = max(0, 128 * (kt - 4 * qc))
                ncols = 512 - off
                ptns, pbase, prow, pbuf = ptl_slot(kt)
                P.add("tensor",
                      (lambda kt_, o_, n_, f0, l0, tn_, pb_, pr_: (lambda e: e.matmul(
                          ps_a[0:65, o_:512],
                          AP(arena_bt, o_v // 2 + 130 * kt_, [[AB_ROW, 128], [1, 65]]),
                          AP(tn_, pb_ + o_,
                             [[pr_, 128], [1, n_]]),
                          start=f0, stop=l0,
                          skip_group_check=True)))(kt, off, ncols, kt == 0, kt == nkt - 1,
                                                   ptns, pbase, prow),
                      reads=[pbuf, b_v[kt // 4]], writes=[b_pa],
                      tag=f"avA{p}_{qc}_{kt}" + sfx)
                P.add("tensor",
                      (lambda kt_, o_, n_, f0, l0, tn_, pb_, pr_: (lambda e: e.matmul(
                          ps_b[0:65, o_:512],
                          AP(arena_bt, o_v // 2 + 130 * kt_ + 65, [[AB_ROW, 128], [1, 65]]),
                          AP(tn_, pb_ + 512 + o_,
                             [[pr_, 128], [1, n_]]),
                          start=f0, stop=l0,
                          skip_group_check=True)))(kt, off, ncols, kt == 0, kt == nkt - 1,
                                                   ptns, pbase, prow),
                      reads=[pbuf, b_v[kt // 4]], writes=[b_pb],
                      tag=f"avB{p}_{qc}_{kt}" + sfx)

            def interleave_unit(qc, which):
                nslot = (p + 1) % 2
                if which == "q":
                    wt_n = ab(o_qkvw + (nslot * 3 + 0) * 2048, 1024)
                    bw_n = b_qkvw[nslot * 3 + 0]
                    dstap = ab(o_qt + 1024 * qc, 512)
                    dbuf = b_qt[qc]
                else:
                    wt_n = ab(o_qkvw + (nslot * 3 + 1) * 2048, 1024)
                    bw_n = b_qkvw[nslot * 3 + 1]
                    dstap = ab(o_kt0 + nslot * 4096 + 1024 * qc, 512)
                    dbuf = b_kts[nslot * 4 + qc]

                def mk_prj(wt_, qc_):
                    def f(e):
                        r = None
                        for fb in range(8):
                            r = e.matmul(ps_c[:, 0:512], wt_[:, 128 * fb:128 * fb + 128],
                                         ln1T_ap(fb, 512 * qc_, 512),
                                         start=(fb == 0), stop=(fb == 7),
                                         skip_group_check=True)
                        return r
                    return f
                P.add("tensor", mk_prj(wt_n, qc),
                      reads=[bw_n] + b_ln1t[4 * qc:4 * qc + 4], writes=[b_pt[1]],
                      tag=f"iprj{which}{p + 1}_{qc}" + sfx)
                if which == "q":
                    P.add("vector",
                          (lambda d0: (lambda e: e.tensor_scalar_mul(
                              out=d0, in0=ps_c[:, 0:512], scalar1=1.0 / 32)))(dstap),
                          reads=[b_pt[1]], writes=[dbuf], tag=f"iev{which}{p + 1}_{qc}" + sfx)
                else:
                    P.add("vector",
                          (lambda d0: (lambda e: e.tensor_copy(
                              out=d0, in_=ps_c[:, 0:512])))(dstap),
                          reads=[b_pt[1]], writes=[dbuf], tag=f"iev{which}{p + 1}_{qc}" + sfx)

            for qc in range(QC):
                nkt = 4 * qc + 4
                for kt in range(nkt):
                    off = max(0, 128 * (kt - 4 * qc))
                    ncols = 512 - off
                    sl, sb2 = ps_s[kt % 2], b_s[kt % 2]
                    etns, ebase, erow, bptl = ptl_slot(kt)
                    P.add("tensor",
                          (lambda kt_, qc_, sl_, o_, n_, ok_: (lambda e: e.matmul(
                              sl_[:, o_:512],
                              AP(arena_bt, ok_ // 2 + 128 * kt_, [[AB_ROW, 64], [1, 128]]),
                              AP(arena_bt, o_qt // 2 + 512 * qc_ + o_, [[AB_ROW, 64], [1, n_]]),
                              start=True, stop=True, tile_position=(0, 0),
                              skip_group_check=True)))(kt, qc, sl, off, ncols, o_ktp),
                          reads=[b_kt[kt // 4], b_qt[qc]], writes=[sb2[0]],
                          tag=f"scA{p}_{qc}_{kt}" + sfx)
                    P.add("tensor",
                          (lambda kt_, qc_, sl_, o_, n_, ok_: (lambda e: e.matmul(
                              sl_[:, 512 + o_:1024],
                              AP(arena_bt, ok_ // 2 + 128 * kt_ + 64 * AB_ROW,
                                 [[AB_ROW, 64], [1, 128]]),
                              AP(arena_bt, o_qt // 2 + 512 * qc_ + o_ + 64 * AB_ROW,
                                 [[AB_ROW, 64], [1, n_]]),
                              start=True, stop=True, tile_position=(64, 0),
                              skip_group_check=True)))(kt, qc, sl, off, ncols, o_ktp),
                          reads=[b_kt[kt // 4], b_qt[qc]], writes=[sb2[1]],
                          tag=f"scB{p}_{qc}_{kt}" + sfx)
                    P.add("scalar",
                          (lambda kt_, sl_, o_, n_, kp, tn_, pb_, pr_: (lambda e: e.activation(
                              out=AP(tn_, pb_ + o_,
                                     [[pr_, 128], [512, 2], [1, n_]]),
                              in_=AP(sl_, o_, [[1024, 128], [512, 2], [1, n_]]),
                              func=AF.Exp, bias=negM[:, kp:kp + 1])))(kt, sl, off, ncols, kpar,
                                                                     etns, ebase, erow),
                          reads=[sb2[0], sb2[1], b_negMs[kpar]], writes=[bptl],
                          tag=f"exp{p}_{qc}_{kt}" + sfx)
                    if off or kt == 4 * qc:
                        P.add("vector",
                              (lambda kt_, o_, n_, mo_, tn_, pb_, pr_: (lambda e: e.tensor_mul(
                                  out=AP(tn_, pb_ + o_,
                                         [[pr_, 128], [512, 2], [1, n_]]),
                                  in0=AP(tn_, pb_ + o_,
                                         [[pr_, 128], [512, 2], [1, n_]]),
                                  in1=AP(masks, 512 * mo_ + o_,
                                         [[2048, 128], [0, 2], [1, n_]]))))(
                                             kt, off, ncols, kt - 4 * qc,
                                             etns, ebase, erow),
                              reads=[bptl, b_masks], writes=[bptl],
                              tag=f"msk{p}_{qc}_{kt}" + sfx)
                    if kt > 1:
                        emit_av(qc, kt - 2, nkt)
                    if kt == nkt // 2 and p + 1 < NPAIR:
                        interleave_unit(qc, "k")
                emit_av(qc, nkt - 2, nkt)
                emit_av(qc, nkt - 1, nkt)
                aidx = p * QC + qc
                P.add("vector",
                      (lambda p_, qc_: (lambda e: e.tensor_copy(
                          out=AP(arena_bt, o_attnN // 2 + 2048 * p_ + 512 * qc_,
                                 [[AB_ROW, 64], [1, 512]]),
                          in_=ps_a[0:64, :])))(p, qc),
                      reads=[b_pa], writes=[b_attnN[aidx]], tag=f"eva{p}_{qc}" + sfx)
                P.add("vector",
                      (lambda p_, qc_: (lambda e: e.tensor_copy(
                          out=AP(arena_bt, o_attnN // 2 + 2048 * p_ + 512 * qc_ + 64 * AB_ROW,
                                 [[AB_ROW, 64], [1, 512]]),
                          in_=ps_b[0:64, :])))(p, qc),
                      reads=[b_pb], writes=[b_attnN[aidx]], tag=f"evb{p}_{qc}" + sfx)
                if qc < 3:
                    P.add("vector",
                          (lambda qc_: (lambda e: e.tensor_copy(
                              out=rowz[ZR(qc_):ZR(qc_) + 1, 0:512],
                              in_=ps_a[64:65, 0:512])))(qc),
                          reads=[b_pa], writes=[b_zq[qc]], tag=f"zra{p}_{qc}" + sfx)
                    P.add("vector",
                          (lambda qc_: (lambda e: e.tensor_copy(
                              out=rowz[ZR(qc_):ZR(qc_) + 1, 512:1024],
                              in_=ps_b[64:65, 0:512])))(qc),
                          reads=[b_pb], writes=[b_zq[qc]], tag=f"zrb{p}_{qc}" + sfx)
                if p + 1 < NPAIR:
                    interleave_unit(qc, "q")

            for qc in range(QC):
                zi = 0 if qc == 3 else qc
                if qc == 3:
                    P.add("vector", lambda e: e.tensor_copy(
                              out=rowz[0:1, 0:512], in_=ps_a[64:65, 0:512]),
                          reads=[b_pa], writes=[b_zq[0]], tag=f"zra{p}_3" + sfx)
                    P.add("vector", lambda e: e.tensor_copy(
                              out=rowz[0:1, 512:1024], in_=ps_b[64:65, 0:512]),
                          reads=[b_pb], writes=[b_zq[0]], tag=f"zrb{p}_3" + sfx)
                P.add("scalar",
                      (lambda qc_: (lambda e: e.activation(
                          out=rowz[ZR(qc_):ZR(qc_) + 1, :],
                          in_=rowz[ZR(qc_):ZR(qc_) + 1, :],
                          func=AF.Ln)))(qc),
                      reads=[b_zq[zi]], writes=[b_zq[zi]], tag=f"zln{p}_{qc}" + sfx)
                P.add("scalar",
                      (lambda qc_: (lambda e: e.activation(
                          out=rowzb[ZR(qc_):ZR(qc_) + 1, :],
                          in_=rowz[ZR(qc_):ZR(qc_) + 1, :],
                          func=AF.Exp, scale=-1.0)))(qc),
                      reads=[b_zq[zi]], writes=[b_zrecb], tag=f"zexp{p}_{qc}" + sfx)
                aidx = p * QC + qc
                P.add("tensor",
                      (lambda qc_: (lambda e: e.matmul(
                          ps_s[0][:, 0:512], selp[ZR(qc_):ZR(qc_) + 1, 0:128],
                          rowzb[ZR(qc_):ZR(qc_) + 1, 0:512],
                          start=True, stop=False, skip_group_check=True)))(qc),
                      reads=[b_zrecb, b_selp], writes=[b_s[0][0]], tag=f"zbcA{p}_{qc}" + sfx)
                P.add("tensor",
                      (lambda qc_: (lambda e: e.matmul(
                          ps_s[0][:, 0:512], selp[ZR(qc_):ZR(qc_) + 1, 128:256],
                          rowzb[ZR(qc_):ZR(qc_) + 1, 512:1024],
                          start=False, stop=True, skip_group_check=True)))(qc),
                      reads=[b_zrecb, b_selp], writes=[b_s[0][0]], tag=f"zbcB{p}_{qc}" + sfx)
                P.add("vector", lambda e: e.tensor_copy(out=zbc[:], in_=ps_s[0][:, 0:512]),
                      reads=[b_s[0][0]], writes=[b_zbc], tag=f"zbce{p}_{qc}" + sfx)
                P.add("vector",
                      (lambda p_, qc_: (lambda e: e.tensor_mul(
                          out=AP(arena_bt, o_attnN // 2 + 2048 * p_ + 512 * qc_,
                                 [[AB_ROW, 128], [1, 512]]),
                          in0=AP(arena_bt, o_attnN // 2 + 2048 * p_ + 512 * qc_,
                                 [[AB_ROW, 128], [1, 512]]),
                          in1=zbc[:])))(p, qc),
                      reads=[b_zbc, b_attnN[aidx]], writes=[b_attnN[aidx]],
                      tag=f"nrm{p}_{qc}" + sfx)

            if p + 1 < NPAIR:
                emit_norms(p + 1)

        cS2, bS2 = stat_col("l2sum")
        cQ2, bQ2 = stat_col("l2ssq")
        cA2, bA2 = stat_col("A2")
        cB2, bB2 = stat_col("B2")
        cSo, bSo = stat_col("losum", 2)
        cQo, bQo = stat_col("lossq", 2)
        cAo, bAo = stat_col("Ao", 2)
        cBo, bBo = stat_col("Bo", 2)
        bA2t = [Buf(f"A2t{t}") for t in range(TT)]

        def emit_gc_mm(t):
            for h2 in range(2):
                psg, bpsg = ((ps_a, b_pa), (ps_b, b_pb))[h2]
                P.add("tensor",
                      (lambda t_, h_, ps_: (lambda e: e.matmul(
                          ps_[0:1, :], stats[:, cA2 + t_:cA2 + t_ + 1],
                          stream[:, 1024 * t_ + 512 * h_:1024 * t_ + 512 * h_ + 512],
                          start=(t_ == 0), stop=(t_ == TT - 1),
                          skip_group_check=True)))(t, h2, psg),
                      reads=[bA2t[t], b_stream[t]], writes=[bpsg], tag=f"gc{t}_{h2}" + sfx)

        for t in range(TT):
            sl, sb2 = ps_s[t % 2], b_s[t % 2]

            def mk_oproj(t_, sl_):
                def f(e):
                    r = None
                    for pp in range(NPAIR):
                        for h2 in range(2):
                            r = e.matmul(
                                sl_[:, 512 * h2:512 * h2 + 512],
                                AP(arena_bt, o_attnN // 2 + 2048 * pp + 128 * t_,
                                   [[AB_ROW, 128], [1, 128]]),
                                AP(arena_bt, o_owt // 2 + 1024 * pp + 512 * h2,
                                   [[AB_ROW, 128], [1, 512]]),
                                start=(pp == 0), stop=(pp == NPAIR - 1),
                                skip_group_check=True)
                    return r
                return f
            P.add("tensor", mk_oproj(t, sl),
                  reads=[b_attnN[pp * QC + t // 4] for pp in range(NPAIR)] + list(b_owt),
                  writes=[sb2[0], sb2[1]], tag=f"oprj{t}" + sfx)
            if t >= 3:
                emit_gc_mm(t - 3)
            ln_stats(sl[:, 0:1024], cSo, cQo, [sb2[0], sb2[1]], bSo, bQo,
                     tag=f"o{t}" + sfx)
            ln_rowmath(cSo, cQo, cAo, cBo, bSo, bQo, bAo, bBo, 1, tag=f"o{t}" + sfx)
            P.add("scalar",
                  (lambda t_: (lambda e: e.activation(
                      out=stream[:, 1024 * t_:1024 * (t_ + 1)],
                      in_=stream[:, 1024 * t_:1024 * (t_ + 1)], func=AF.Identity,
                      bias=stats[:, cBo:cBo + 1])))(t),
                  reads=[bBo, b_stream[t]], writes=[b_stream[t]], tag=f"oresb{t}" + sfx)
            P.add("vector",
                  (lambda t_, sl_: (lambda e: e.scalar_tensor_tensor(
                      out=stream[:, 1024 * t_:1024 * (t_ + 1)],
                      in0=sl_[:, 0:1024], scalar=stats[:, cAo:cAo + 1],
                      in1=stream[:, 1024 * t_:1024 * (t_ + 1)],
                      op0=ALU.mult, op1=ALU.add)))(t, sl),
                  reads=[sb2[0], sb2[1], bAo, b_stream[t]], writes=[b_stream[t]],
                  tag=f"ores{t}" + sfx)
            ln_stats(stream[:, 1024 * t:1024 * (t + 1)], cS2 + t, cQ2 + t,
                     [b_stream[t]], bS2, bQ2, tag=f"2_{t}" + sfx)
            ln_rowmath(cS2 + t, cQ2 + t, cA2 + t, cB2 + t, bS2, bQ2, bA2, bB2, 1,
                       tag=f"2_{t}" + sfx, extra_wA=[bA2t[t]])
        for t in range(TT - 3, TT):
            emit_gc_mm(t)

        arena.reset()
        o_w1, b_w1 = arena.alloc("w1", 8192 * 2, nbufs=1)
        o_w2, b_w2 = arena.alloc("w2", 4096 * 2, nbufs=1)
        o_w3, b_w3 = arena.alloc("w3", 1024 * 2, nbufs=1)
        o_gw1, b_gw1 = arena.alloc("gw1", 4096 * 2, nbufs=1)
        o_gw2, b_gw2 = arena.alloc("gw2", 1024 * 2, nbufs=1)
        o_gw3, b_gw3 = arena.alloc("gw3", 4, nbufs=1)
        o_state, b_state = arena.alloc("state", 8 * 1024 * 4, nbufs=8)
        o_outT, b_outTh = arena.alloc("outT", 8192 * 2, nbufs=2)
        o_h1T, b_h1T = arena.alloc("h1T", 4096 * 2, nbufs=8)
        o_h2T, b_h2T = arena.alloc("h2T", 2048 * 2, nbufs=4)
        o_h3T, b_h3T = arena.alloc("h3T", 2048 * 2, nbufs=2)
        o_twt, b_twt = arena.alloc("twt", 2 * 1024 * 4, nbufs=8)

        dma_to(ab(o_w1, 8192), dw1.ap(), eng="gpsimd", w=[b_w1], tag="ldw1" + sfx)
        dma_to(ab(o_w2, 4096), dw2.ap(), eng="gpsimd", w=[b_w2], tag="ldw2" + sfx)
        dma_to(ab(o_w3, 1024), dw3.ap(), eng="gpsimd", w=[b_w3], tag="ldw3" + sfx)
        dma_to(ab(o_gw1, 4096), dgw1.ap(), eng="gpsimd", w=[b_gw1], tag="ldgw1" + sfx)
        dma_to(ab(o_gw2, 1024), dgw2.ap(), eng="gpsimd", w=[b_gw2], tag="ldgw2" + sfx)
        dma_to(ab(o_gw3, 2), dgw3.ap(), eng="gpsimd", w=[b_gw3], tag="ldgw3" + sfx)

        def emit_gcnet():
            P.add("tensor", lambda e: e.matmul(ps_s[0][0:1, 0:16], ones_f128c[:],
                                                stats[:, cB2:cB2 + 16],
                                                start=True, stop=True,
                                                skip_group_check=True),
                  reads=[bB2, b_ones], writes=[b_s[0][0]], tag="b2sum_mm" + sfx)
            P.add("vector", lambda e: e.reduce_sum(out=grow[0:1, 5:6], in_=ps_s[0][0:1, 0:16],
                                                   axis=AXX),
                  reads=[b_s[0][0]], writes=[b_grow], tag="b2sum_red" + sfx)
            P.add("vector", lambda e: e.tensor_scalar_mul(out=grow[0:1, 6:7],
                                                          in0=grow[0:1, 5:6], scalar1=1.0 / S),
                  reads=[b_grow], writes=[b_grow], tag="b2sum_sc" + sfx)
            P.add("scalar", lambda e: e.activation(out=gcrowb[0:1, 0:512], in_=ps_a[0:1, :],
                                                   func=AF.Identity, scale=1.0 / S,
                                                   bias=grow[0:1, 6:7]),
                  reads=[b_pa, b_grow], writes=[b_gc], tag="gcev0" + sfx)
            P.add("scalar", lambda e: e.activation(out=gcrowb[0:1, 512:1024], in_=ps_b[0:1, :],
                                                   func=AF.Identity, scale=1.0 / S,
                                                   bias=grow[0:1, 6:7]),
                  reads=[b_pb, b_grow], writes=[b_gc], tag="gcev1" + sfx)

            def mk_gctr():
                def f(e):
                    r = None
                    for fb in range(8):
                        r = e.transpose(ps_t[0][:, 2 * fb:2 * fb + 1],
                                        gcrowb[0:1, 128 * fb:128 * fb + 128], ident[0:1, 0:1])
                    return r
                return f
            P.add("tensor", mk_gctr(), reads=[b_gc, b_ident], writes=[b_pt[0]], tag="gctr" + sfx)
            P.add("vector", lambda e: e.tensor_copy(
                out=gcT[:], in_=AP(ps_t[0], 0, [[1024, 128], [2, 8], [1, 1]])),
                  reads=[b_pt[0]], writes=[b_gc], tag="gctev" + sfx)

            def mk_g1(o_):
                def f(e):
                    r = None
                    for fb in range(8):
                        r = e.matmul(ps_a[:, o_:o_ + 1],
                                     ab(o_gw1 + (o_ * 8 + fb) * 256, 128),
                                     gcT[:, fb:fb + 1], start=(fb == 0), stop=(fb == 7),
                                     skip_group_check=True)
                    return r
                return f
            for o in range(4):
                P.add("tensor", mk_g1(o), reads=[b_gw1, b_gc], writes=[b_pa],
                      tag=f"g1_{o}" + sfx)
            for o in range(4):
                P.add("scalar",
                      (lambda o_: (lambda e: e.activation(
                          out=hg1[:, o_:o_ + 1], in_=ps_a[:, o_:o_ + 1], func=AF.Gelu,
                          bias=gb1s[:, o_:o_ + 1])))(o),
                      reads=[b_pa, b_bias], writes=[b_gc], tag=f"g1g{o}" + sfx)

            def mk_g2(o_):
                def f(e):
                    r = None
                    for fb in range(4):
                        r = e.matmul(ps_b[:, o_:o_ + 1],
                                     ab(o_gw2 + (o_ * 4 + fb) * 256, 128),
                                     hg1[:, fb:fb + 1], start=(fb == 0), stop=(fb == 3),
                                     skip_group_check=True)
                    return r
                return f
            for o in range(2):
                P.add("tensor", mk_g2(o), reads=[b_gw2, b_gc], writes=[b_pb],
                      tag=f"g2_{o}" + sfx)
            for o in range(2):
                P.add("scalar",
                      (lambda o_: (lambda e: e.activation(
                          out=hg2[:, o_:o_ + 1], in_=ps_b[:, o_:o_ + 1], func=AF.Gelu,
                          bias=gb2s[:, o_:o_ + 1])))(o),
                      reads=[b_pb, b_bias], writes=[b_gc], tag=f"g2g{o}" + sfx)

            def mk_g3():
                def f(e):
                    r = None
                    for fb in range(2):
                        r = e.matmul(ps_a[0:1, 8:9], ab(o_gw3 + fb * 2, 1),
                                     hg2[:, fb:fb + 1], start=(fb == 0), stop=(fb == 1),
                                     skip_group_check=True)
                    return r
                return f
            P.add("tensor", mk_g3(), reads=[b_gw3, b_gc], writes=[b_pa], tag="g3" + sfx)
            P.add("scalar", lambda e: e.activation(out=grow[0:1, 0:1], in_=ps_a[0:1, 8:9],
                                                   func=AF.Tanh, scale=0.5,
                                                   bias=gb3h[0:1, 0:1]),
                  reads=[b_pa, b_bias], writes=[b_grow], tag="gf" + sfx)
            P.add("vector", lambda e: e.tensor_scalar(out=grow[0:1, 1:2], in0=grow[0:1, 0:1],
                                                      scalar1=0.35, scalar2=-0.15,
                                                      op0=ALU.mult, op1=ALU.add),
                  reads=[b_grow], writes=[b_grow])
            P.add("vector", lambda e: e.tensor_mul(out=grow[0:1, 2:3], in0=grow[0:1, 1:2],
                                                   in1=ars[0:1, 0:1]),
                  reads=[b_grow, b_ar], writes=[b_grow])
            P.add("vector", lambda e: e.tensor_scalar(out=grow[0:1, 3:4], in0=grow[0:1, 2:3],
                                                      scalar1=2.0, scalar2=0.1,
                                                      op0=ALU.mult, op1=ALU.add),
                  reads=[b_grow], writes=[b_grow], tag="s1" + sfx)
            P.add("vector", lambda e: e.tensor_scalar_mul(out=grow[0:1, 4:5], in0=ars[0:1, 0:1],
                                                          scalar1=0.3),
                  reads=[b_ar], writes=[b_grow], tag="s2" + sfx)
            P.add("vector", lambda e: e.tensor_add(out=grow[0:1, 3:4], in0=grow[0:1, 3:4],
                                                   in1=grow[0:1, 4:5]),
                  reads=[b_grow], writes=[b_grow], tag="s1p" + sfx)
            P.add("tensor", lambda e: e.matmul(ps_a[:, 16:18], ones_f[:], grow[0:1, 3:5],
                                               start=True, stop=True, skip_group_check=True),
                  reads=[b_grow, b_ones], writes=[b_pa], tag="s12bc" + sfx)
            P.add("vector", lambda e: e.tensor_copy(out=sc12[:], in_=ps_a[:, 16:18]),
                  reads=[b_pa], writes=[b_sc], tag="s12ev" + sfx)



        cSf, bSf = stat_col("lfsum", 8)
        cQf, bQf = stat_col("lfssq", 8)
        cAf, bAf = stat_col("Af", 8)
        cBf, bBf = stat_col("Bf", 8)
        def emit_st(ch, j):
            t = NJ * ch + j
            P.add("vector",
                  (lambda t_, j_: (lambda e: e.tensor_scalar(
                      out=af(o_state + j_ * 4096, 1024),
                      in0=stream[:, 1024 * t_:1024 * (t_ + 1)],
                      scalar1=stats[:, cA2 + t_:cA2 + t_ + 1],
                      scalar2=stats[:, cB2 + t_:cB2 + t_ + 1],
                      op0=ALU.mult, op1=ALU.add)))(t, j),
                  reads=[b_stream[t], bA2, bB2], writes=[b_state[j]],
                  tag=f"st{ch}_{j}" + sfx)

        def castrf(ch, it, th, js=None):
            for j in (range(4 * th, 4 * th + 4) if js is None else js):
                c4 = j % 4
                P.add("vector",
                      (lambda j_, c_: (lambda e: e.tensor_copy(
                          out=castb[:, 1024 * c_:1024 * (c_ + 1)],
                          in_=af(o_state + j_ * 4096, 1024))))(j, c4),
                      reads=[b_state[j]], writes=[b_castb[c4]],
                      tag=f"cst{ch}_{it}_{j}" + sfx)
                pt, bpt = ps_t[j % 2], b_pt[j % 2]

                def mk_trf(c_, pt_):
                    def f(e):
                        r = None
                        for fb in range(8):
                            r = e.transpose(pt_[:, 128 * fb:128 * fb + 128],
                                            castb[:, 1024 * c_ + 128 * fb:
                                                  1024 * c_ + 128 * fb + 128],
                                            ident[:])
                        return r
                    return f
                P.add("tensor", mk_trf(c4, pt), reads=[b_castb[c4], b_ident],
                      writes=[bpt], tag=f"trf{ch}_{it}_{j}" + sfx)
                dstap = AP(arena_bt, o_outT // 2 + 128 * (j % 4) + 512 * th,
                           [[AB_ROW, 128], [1024, 8], [1, 128]])
                P.add("scalar",
                      (lambda pt_, d0: (lambda e: e.activation(
                          out=d0, in_=pt_[:, 0:1024], func=AF.Identity)))(pt, dstap),
                      reads=[bpt], writes=[b_outTh[th]], tag=f"tev{ch}_{it}_{j}" + sfx)

        for ch in range(2):
            if ch == 0:
                for j in range(NJ):
                    emit_st(0, j)
                castrf(0, 0, 0)
                emit_gcnet()
            for it in range(NITER):
                for th in range(2):
                    for o in range(8):
                        sl, sb2 = ps_s[(o // 2) % 2], b_s[(o // 2) % 2]
                        half = o % 2
                        pap = sl[:, 512 * half:512 * half + 512]

                        def mk_w1(o_, th_, pap_):
                            def f(e):
                                r = None
                                for fb in range(8):
                                    r = e.matmul(pap_, ab(o_w1 + (o_ * 8 + fb) * 256, 128),
                                                 ab(o_outT + fb * 2048 + th_ * 1024, 512),
                                                 start=(fb == 0), stop=(fb == 7),
                                                 skip_group_check=True)
                                return r
                            return f
                        P.add("tensor", mk_w1(o, th, pap), reads=[b_w1, b_outTh[th]],
                              writes=[sb2[half]], tag=f"w1_{ch}_{it}_{th}_{o}" + sfx)
                        P.add("scalar",
                              (lambda o_, th_, pap_: (lambda e: e.activation(
                                  out=ab(o_h1T + o_ * 1024, 512), in_=pap_,
                                  func=AF.Gelu, bias=b1s[:, o_:o_ + 1])))(o, th, pap),
                              reads=[sb2[half], b_bias], writes=[b_h1T[o]],
                              tag=f"h1_{ch}_{it}_{th}_{o}" + sfx)
                    if th == 0:
                        castrf(ch, it, 1)
                    elif it + 1 < NITER:
                        castrf(ch, it + 1, 0)
                    for o in range(4):
                        sl, sb2 = ps_s[o % 2], b_s[o % 2]
                        pap = sl[:, 0:512]

                        def mk_w2(o_, th_, pap_):
                            def f(e):
                                r = None
                                for fb in range(8):
                                    r = e.matmul(pap_, ab(o_w2 + (o_ * 8 + fb) * 256, 128),
                                                 ab(o_h1T + fb * 1024, 512),
                                                 start=(fb == 0), stop=(fb == 7),
                                                 skip_group_check=True)
                                return r
                            return f
                        P.add("tensor", mk_w2(o, th, pap), reads=[b_w2] + list(b_h1T),
                              writes=[sb2[0]], tag=f"w2_{ch}_{it}_{th}_{o}" + sfx)
                        P.add("scalar",
                              (lambda o_, th_, pap_: (lambda e: e.activation(
                                  out=ab(o_h2T + o_ * 1024, 512), in_=pap_,
                                  func=AF.Gelu, bias=b2s[:, o_:o_ + 1])))(o, th, pap),
                              reads=[sb2[0], b_bias], writes=[b_h2T[o]],
                              tag=f"h2_{ch}_{it}_{th}_{o}" + sfx)
                    for o in range(2):
                        sl, sb2 = ps_s[o % 2], b_s[o % 2]
                        pap = sl[:, 512:1024]

                        def mk_w3(o_, th_, pap_):
                            def f(e):
                                r = None
                                for fb in range(4):
                                    r = e.matmul(pap_, ab(o_w3 + (o_ * 4 + fb) * 256, 128),
                                                 ab(o_h2T + fb * 1024, 512),
                                                 start=(fb == 0), stop=(fb == 3),
                                                 skip_group_check=True)
                                return r
                            return f
                        P.add("tensor", mk_w3(o, th, pap), reads=[b_w3] + list(b_h2T),
                              writes=[sb2[1]], tag=f"w3_{ch}_{it}_{th}_{o}" + sfx)
                        P.add("scalar",
                              (lambda o_, th_, pap_: (lambda e: e.activation(
                                  out=ab(o_h3T + o_ * 2048 + th_ * 1024, 512), in_=pap_,
                                  func=AF.Gelu, bias=b3s[:, o_:o_ + 1])))(o, th, pap),
                              reads=[sb2[1], b_bias], writes=[b_h3T[o]],
                              tag=f"h3_{ch}_{it}_{th}_{o}" + sfx)
                    for j in range(4 * th, 4 * th + 4):
                        def mk_tf(j_):
                            def f(e):
                                r = None
                                for fb in range(2):
                                    r = e.matmul(ps_a[:, 24 + j_:25 + j_],
                                                 AP(arena_bt,
                                                    o_h3T // 2 + fb * 1024 + 128 * (j_ % 4)
                                                    + (j_ // 4) * 512,
                                                    [[AB_ROW, 128], [1, 128]]),
                                                 w4t[:, fb:fb + 1], start=(fb == 0),
                                                 stop=(fb == 1), skip_group_check=True)
                                return r
                            return f
                        P.add("tensor", mk_tf(j), reads=list(b_h3T) + [b_w4],
                              writes=[b_pa], tag=f"tf{ch}_{it}_{j}" + sfx)
                    P.add("scalar",
                          (lambda th_: (lambda e: e.activation(
                              out=sigt[:, 4 * th_:4 * th_ + 4],
                              in_=ps_a[:, 24 + 4 * th_:28 + 4 * th_],
                              func=AF.Tanh, scale=0.5, bias=b4h[:])))(th),
                          reads=[b_pa, b_b4], writes=[b_sig], tag=f"sig{ch}_{it}_{th}" + sfx)
                    P.add("vector",
                          (lambda th_: (lambda e: e.tensor_scalar(
                              out=cpl[:, 4 * th_:4 * th_ + 4],
                              in0=sigt[:, 4 * th_:4 * th_ + 4],
                              scalar1=sc12[:, 1:2], scalar2=sc12[:, 0:1],
                              op0=ALU.mult, op1=ALU.add)))(th),
                          reads=[b_sig, b_sc], writes=[b_cpl], tag=f"cpl{ch}_{it}_{th}" + sfx)
                    P.add("vector",
                          (lambda th_: (lambda e: e.tensor_scalar_mul(
                              out=ncpl[:, 4 * th_:4 * th_ + 4],
                              in0=cpl[:, 4 * th_:4 * th_ + 4], scalar1=-1.0)))(th),
                          reads=[b_cpl], writes=[b_cpl], tag=f"ncpl{ch}_{it}_{th}" + sfx)
                    for j in range(4 * th, 4 * th + 4):
                        half = j % 2
                        for sv in range(2):
                            P.add("scalar",
                                  (lambda j_, s_, h_: (lambda e: e.tensor_copy(
                                      out=af(o_twt + h_ * 4096 + 1024 * s_, 256),
                                      in0=af(o_state + j_ * 4096 + 1024 * s_, 256))))(
                                          j, sv, half) if False else
                                  (lambda j_, s_, h_: (lambda e: e.activation(
                                      out=af(o_twt + h_ * 4096 + 1024 * s_, 256),
                                      in_=af(o_state + j_ * 4096 + 1024 * s_, 256),
                                      func=AF.Identity)))(j, sv, half),
                                  reads=[b_state[j]], writes=[b_twt[half * 4 + sv]],
                                  tag=f"twsv{ch}_{it}_{j}_{sv}" + sfx)
                        for (dc, s_src, neg, from_tmp) in (
                                (0, 2, False, False), (1, 3, True, False),
                                (2, 0, True, True), (3, 1, False, True)):
                            src_ap = (af(o_twt + (j % 2) * 4096 + 1024 * s_src, 256)
                                      if from_tmp else
                                      af(o_state + j * 4096 + 1024 * s_src, 256))
                            rbufs = ([b_twt[(j % 2) * 4 + s_src]] if from_tmp else [])                                 + [b_state[j], b_cpl]
                            P.add("vector",
                                  (lambda j_, d_, n_, sap: (lambda e: e.scalar_tensor_tensor(
                                      out=af(o_state + j_ * 4096 + 1024 * d_, 256),
                                      in0=sap, scalar=(ncpl if n_ else cpl)[:, j_:j_ + 1],
                                      in1=af(o_state + j_ * 4096 + 1024 * d_, 256),
                                      op0=ALU.mult, op1=ALU.add)))(j, dc, neg, src_ap),
                                  reads=rbufs, writes=[b_state[j]],
                                  tag=f"tw{ch}_{it}_{j}_{dc}" + sfx)

            for j in range(NJ):
                t = NJ * ch + j
                ln_stats(af(o_state + j * 4096, 1024), cSf + j, cQf + j,
                         [b_state[j]], bSf, bQf, tag=f"f{t}" + sfx)
            ln_rowmath(cSf, cQf, cAf, cBf, bSf, bQf, bAf, bBf, NJ, tag=f"f{ch}" + sfx)
            for j in range(NJ):
                t = NJ * ch + j
                P.add("scalar",
                      (lambda t_, j_: (lambda e: e.activation(
                          out=stream[:, 1024 * t_:1024 * (t_ + 1)],
                          in_=stream[:, 1024 * t_:1024 * (t_ + 1)], func=AF.Identity,
                          bias=stats[:, cBf + j_:cBf + j_ + 1])))(t, j),
                      reads=[bBf, b_stream[t]], writes=[b_stream[t]], tag=f"fapb{t}" + sfx)
                P.add("vector",
                      (lambda t_, j_: (lambda e: e.scalar_tensor_tensor(
                          out=stream[:, 1024 * t_:1024 * (t_ + 1)],
                          in0=af(o_state + j_ * 4096, 1024),
                          scalar=stats[:, cAf + j_:cAf + j_ + 1],
                          in1=stream[:, 1024 * t_:1024 * (t_ + 1)],
                          op0=ALU.mult, op1=ALU.add)))(t, j),
                      reads=[b_state[j], bAf, b_stream[t]], writes=[b_stream[t]],
                      tag=f"fres{t}" + sfx)
                outap = dout.ap()[s].rearrange("(t p) c -> p t c", p=128)[:, t:t + 1, :]
                P.add("sync",
                      (lambda t_, o0: (lambda e: e.dma_start(
                          out=o0, in_=stream[:, 1024 * t_:1024 * (t_ + 1)])))(t, outap),
                      reads=[b_stream[t]], writes=[b_stored[s * TT + t]], dma=True,
                      tag=f"stout{t}" + sfx)
                if ch == 0:
                    emit_st(1, j)
                    if j < 4:
                        castrf(1, 0, 0, js=[j])

    P.add("sync", lambda e: e.nop(), reads=b_stored, tag="fin")

    P.prepare()

    @block.tensor
    def _(eng):
        P.emit_engine("tensor", eng, sems)

    @block.vector
    def _(eng):
        P.emit_engine("vector", eng, sems)

    @block.scalar
    def _(eng):
        P.emit_engine("scalar", eng, sems)

    @block.sync
    def _(eng):
        P.emit_engine("sync", eng, sems)

    @block.gpsimd
    def _(eng):
        P.emit_engine("gpsimd", eng, sems)

    es.close()
    return nc



def _host_inputs(inp):
    perm = np.array([(j % 64) * 16 + j // 64 for j in range(D)], dtype=np.int64)
    qw = np.ascontiguousarray(np.asarray(inp["L0_attn_qw"], np.float32))[:, perm]
    kw = np.ascontiguousarray(np.asarray(inp["L0_attn_kw"], np.float32))[:, perm]
    vw = np.ascontiguousarray(np.asarray(inp["L0_attn_vw"], np.float32))[:, perm]
    ow = np.ascontiguousarray(np.asarray(inp["L0_attn_ow"], np.float32))[perm, :]

    def tile_pair(W):
        A = W.reshape(8, 128, 8, 128)
        return np.ascontiguousarray(A.transpose(2, 1, 0, 3).reshape(8, 128, 1024))

    wqkv = np.stack([tile_pair(qw), tile_pair(kw), tile_pair(vw)])
    owt = np.ascontiguousarray(ow.reshape(8, 128, 1024).transpose(1, 0, 2).reshape(128, 8192))

    def tile_of(W, no, nf):
        A = np.asarray(W, np.float32).reshape(nf, 128, no, 128)
        return np.ascontiguousarray(A.transpose(1, 2, 0, 3).reshape(128, no * nf * 128))

    w1t = tile_of(inp["L0_ffn_cn_w1"], 8, 8)
    w2t = tile_of(inp["L0_ffn_cn_w2"], 4, 8)
    w3t = tile_of(inp["L0_ffn_cn_w3"], 2, 4)
    w4t = np.ascontiguousarray(np.asarray(inp["L0_ffn_cn_w4"], np.float32).reshape(2, 128).T)
    gw1t = tile_of(inp["L0_ffn_gc_w1"], 4, 8)
    gw2t = tile_of(inp["L0_ffn_gc_w2"], 2, 4)
    gw3t = np.ascontiguousarray(np.asarray(inp["L0_ffn_gc_w3"], np.float32).reshape(2, 128).T)

    def bias_t(b, no):
        return np.ascontiguousarray(np.asarray(b, np.float32).reshape(no, 128).T)

    f32 = np.float32
    ident = np.eye(128, dtype=f32).astype(md.bfloat16)
    m = np.zeros((128, 4, 512), f32)
    pk = np.arange(128)[:, None]
    fq = np.arange(512)[None, :]
    for i, off in enumerate((0, 128, 256, 384)):
        m[:, i, :] = (fq >= pk + off).astype(f32)
    masksv = np.ascontiguousarray(m.reshape(128, 2048)).astype(md.bfloat16)
    selpair = np.zeros((128, 256), f32)
    selpair[:, 0:64] = 1.0
    selpair[:, 192:256] = 1.0
    nb = np.zeros((128, 2), f32)
    nb[0:64, 0] = 1.0
    nb[64:128, 1] = 1.0

    return dict(
        wqkv=wqkv.astype(f32), owt=owt.astype(f32), w1t=w1t, w2t=w2t, w3t=w3t,
        w4t=w4t, gw1t=gw1t, gw2t=gw2t, gw3t=gw3t,
        b1=bias_t(inp["L0_ffn_cn_b1"], 8), b2=bias_t(inp["L0_ffn_cn_b2"], 4),
        b3=bias_t(inp["L0_ffn_cn_b3"], 2),
        b4=np.asarray(inp["L0_ffn_cn_b4"], f32).reshape(1, 1),
        gb1=bias_t(inp["L0_ffn_gc_b1"], 4), gb2=bias_t(inp["L0_ffn_gc_b2"], 2),
        gb3=np.asarray(inp["L0_ffn_gc_b3"], f32).reshape(1, 1),
        ar=np.asarray(inp["L0_ffn_adaptive_range"], f32).reshape(1, 1),
        ident=ident, masks=masksv, ones_f=np.ones((1, 128), f32),
        selp=selpair.astype(md.bfloat16), nb128=nb.astype(md.bfloat16),
        ones128=np.ones((128, 1), f32),
        onec=np.ones((128, 1), f32).astype(md.bfloat16),
    )


def kernel(**inputs):
    x = np.ascontiguousarray(np.asarray(inputs["L0_attn_x"], dtype=np.float32))
    common = _host_inputs(inputs)
    nc = build_nc()
    in_maps = []
    for c in range(NCORES):
        m = dict(common)
        m["x"] = np.ascontiguousarray(x[2 * c:2 * c + 2])
        in_maps.append(m)
    res = run_bass_kernel_spmd(nc, in_maps, core_ids=list(range(NCORES)))
    out = np.concatenate([np.asarray(r["out"], dtype=np.float32) for r in res.results],
                         axis=0)
    return out


# version 25 (final)
# speedup vs baseline: 1.1887x; 1.0063x over previous
import numpy as np
import ml_dtypes as md
import concourse.bass as bass
import concourse.mybir as mybir
from concourse import dve_ops
from concourse.bass_utils import run_bass_kernel_spmd

F32 = mybir.dt.float32
BF16 = mybir.dt.bfloat16
AF = mybir.ActivationFunctionType
ALU = mybir.AluOpType
AXX = mybir.AxisListType.X
AP = bass.AP

NCORES = 8
SEQS, S, D = 2, 2048, 1024
NPAIR, TT, QC = 8, 16, 4
NITER = 3
NJ, NTF = 8, 8
OPLIMIT = [None]


ENGS = ("tensor", "vector", "scalar", "sync", "gpsimd")
DRAIN_ENGS = frozenset(("vector", "scalar"))
SELFSEM_ENGS = frozenset(("sync", "gpsimd"))


class Buf:
    __slots__ = ("name", "w", "rs", "psum")

    def __init__(self, name="b", init_deps=(), psum=False):
        self.name, self.w, self.rs = name, None, list(init_deps)
        self.psum = psum


class Op:
    __slots__ = ("eng", "idx", "fn", "raw", "war", "inc", "flag", "semval", "tag", "skey")


class Prog:
    def __init__(self):
        self.ops = {e: [] for e in ENGS}

    def add(self, eng, fn, reads=(), writes=(), dma=False, tag=""):
        if OPLIMIT[0] is not None:
            self.total = getattr(self, "total", 0) + 1
            if self.total > OPLIMIT[0]:
                return ("dropped", -1)
        idx = len(self.ops[eng])
        tok = (eng, idx)
        op = Op()
        op.eng, op.idx, op.fn, op.tag = eng, idx, fn, tag
        op.inc = 16 if dma else 1
        op.flag = False
        op.semval = 0
        raw, war = set(), set()
        for b in reads:
            if b.w is not None:
                raw.add(b.w)
            if b.psum and eng in ("vector", "scalar"):
                for rtok in b.rs:
                    if rtok[0] != eng and rtok[0] in ("vector", "scalar"):
                        raw.add(rtok)
        for b in writes:
            if b.w is not None:
                raw.add(b.w)
            war.update(b.rs)
        raw.discard(tok)
        war.discard(tok)
        op.raw, op.war = raw, war
        self.ops[eng].append(op)
        for b in reads:
            b.rs.append(tok)
        for b in writes:
            b.w = tok
            b.rs = []
        return tok

    def prepare(self):
        for e in ENGS:
            for op in self.ops[e]:
                op.skey = (e + "_c") if (e == "gpsimd" and op.inc == 1) else e
                if op.inc == 16:
                    op.flag = True
                for (de, di) in list(op.raw) + list(op.war):
                    if de != e or de in SELFSEM_ENGS:
                        self.ops[de][di].flag = True
        for e in ENGS:
            cnt = {}
            for op in self.ops[e]:
                if op.flag:
                    cnt[op.skey] = cnt.get(op.skey, 0) + op.inc
                op.semval = cnt.get(op.skey, 0)
        self.collect_waitvals()

    def _op_needs(self, ename, op, last_drain):
        needs = {}
        drain_upto = -1
        for kind, deps in (("raw", op.raw), ("war", op.war)):
            for (de, di) in deps:
                if de == ename and de not in SELFSEM_ENGS:
                    if di > last_drain and ename in DRAIN_ENGS:
                        drain_upto = max(drain_upto, di)
                    continue
                dep = self.ops[de][di]
                needs[dep.skey] = max(needs.get(dep.skey, 0), dep.semval)
        return needs, drain_upto

    def collect_waitvals(self):
        self.waitvals = {}
        for ename in ENGS:
            seen = {}
            last_drain = -1
            for op in self.ops[ename]:
                needs, drain_upto = self._op_needs(ename, op, last_drain)
                if drain_upto >= 0:
                    last_drain = op.idx - 1
                for dk, v in needs.items():
                    if v > seen.get(dk, 0):
                        self.waitvals.setdefault(dk, set()).add(v)
                        seen[dk] = v

    def emit_engine(self, ename, eng, sems):
        seen = {}
        last_drain = -1
        qval = 0
        wv = self.waitvals.get(ename, ()) if ename in SELFSEM_ENGS else ()
        for op in self.ops[ename]:
            needs, drain_upto = self._op_needs(ename, op, last_drain)
            if drain_upto >= 0:
                eng.drain()
                last_drain = op.idx - 1
            if op.flag and op.inc == 16 and qval in wv and qval > seen.get(ename, 0):
                needs[ename] = max(needs.get(ename, 0), qval)
            for dk in sorted(needs):
                if needs[dk] > seen.get(dk, 0):
                    eng.wait_ge(sems[dk], needs[dk])
                    seen[dk] = needs[dk]
            inst = op.fn(eng)
            if op.flag:
                assert inst is not None, f"flagged op lacks inst: {ename}#{op.idx} {op.tag}"
                inst.then_inc(sems[op.skey], op.inc)
                if op.inc == 16:
                    qval = op.semval


class Arena:
    def __init__(self, nbytes):
        self.nbytes = nbytes
        self.active = []
        self.cur = 0
        self.peak = 0

    def reset(self):
        self.cur = 0

    def alloc(self, name, nbytes, nbufs=1):
        start = (self.cur + 63) & ~63
        end = start + nbytes
        assert end <= self.nbytes, f"arena overflow: {name} needs {end}"
        self.cur = end
        self.peak = max(self.peak, end)
        init, keep = [], []
        for (s0, e0, b) in self.active:
            if s0 < end and e0 > start:
                if b.w is not None:
                    init.append(b.w)
                init.extend(b.rs)
                if s0 < start:
                    keep.append((s0, start, b))
                if e0 > end:
                    keep.append((end, e0, b))
            else:
                keep.append((s0, e0, b))
        bufs = [Buf(f"{name}{i}", init) for i in range(nbufs)]
        for b in bufs:
            keep.append((start, end, b))
        self.active = keep
        return start, (bufs if nbufs > 1 else bufs[0])



def build_nc():
    nc = bass.Bass("TRN2", target_bir_lowering=False)

    dx = nc.dram_tensor("x", [SEQS, S, D], F32, kind="ExternalInput")
    dwqkv = nc.dram_tensor("wqkv", [3, NPAIR, 128, 1024], F32, kind="ExternalInput")
    dowt = nc.dram_tensor("owt", [128, 8192], F32, kind="ExternalInput")
    dw1 = nc.dram_tensor("w1t", [128, 8192], F32, kind="ExternalInput")
    dw2 = nc.dram_tensor("w2t", [128, 4096], F32, kind="ExternalInput")
    dw3 = nc.dram_tensor("w3t", [128, 1024], F32, kind="ExternalInput")
    dw4 = nc.dram_tensor("w4t", [128, 2], F32, kind="ExternalInput")
    dgw1 = nc.dram_tensor("gw1t", [128, 4096], F32, kind="ExternalInput")
    dgw2 = nc.dram_tensor("gw2t", [128, 1024], F32, kind="ExternalInput")
    dgw3 = nc.dram_tensor("gw3t", [128, 2], F32, kind="ExternalInput")
    db1 = nc.dram_tensor("b1", [128, 8], F32, kind="ExternalInput")
    db2 = nc.dram_tensor("b2", [128, 4], F32, kind="ExternalInput")
    db3 = nc.dram_tensor("b3", [128, 2], F32, kind="ExternalInput")
    db4 = nc.dram_tensor("b4", [1, 1], F32, kind="ExternalInput")
    dgb1 = nc.dram_tensor("gb1", [128, 4], F32, kind="ExternalInput")
    dgb2 = nc.dram_tensor("gb2", [128, 2], F32, kind="ExternalInput")
    dgb3 = nc.dram_tensor("gb3", [1, 1], F32, kind="ExternalInput")
    dar = nc.dram_tensor("ar", [1, 1], F32, kind="ExternalInput")
    dident = nc.dram_tensor("ident", [128, 128], BF16, kind="ExternalInput")
    dmasks = nc.dram_tensor("masks", [128, 2048], BF16, kind="ExternalInput")
    dones = nc.dram_tensor("ones_f", [1, 128], F32, kind="ExternalInput")
    dones128 = nc.dram_tensor("ones128", [128, 1], F32, kind="ExternalInput")
    dselp = nc.dram_tensor("selp", [128, 256], BF16, kind="ExternalInput")
    dnb = nc.dram_tensor("nb128", [128, 2], BF16, kind="ExternalInput")
    donec = nc.dram_tensor("onec", [128, 1], BF16, kind="ExternalInput")
    dout = nc.dram_tensor("out", [SEQS, S, D], F32, kind="ExternalOutput")

    from contextlib import ExitStack
    es = ExitStack()

    def sbt(name, shape, dt):
        return es.enter_context(nc.sbuf_tensor(name, shape, dt))

    stream = sbt("stream", [128, TT * 1024], F32)
    lnsc = sbt("lnsc", [128, 1024], F32)
    lnsc2 = sbt("lnsc2", [128, 1024], F32)
    castb = sbt("castb", [128, 4096], BF16)
    stats = sbt("stats", [128, 184], F32)
    masks = sbt("masks_s", [128, 2048], BF16)
    ident = sbt("ident_s", [128, 128], BF16)
    ones_f = sbt("ones_fs", [1, 128], F32)
    ones_f128c = sbt("ones_f128", [128, 1], F32)
    selp = sbt("selp_s", [128, 256], BF16)
    nb128 = sbt("nb128_s", [128, 2], BF16)
    onec = sbt("onec_s", [128, 1], BF16)
    eps = sbt("eps", [128, 1], F32)
    w4t = sbt("w4t_s", [128, 2], BF16)
    b1s = sbt("b1s", [128, 8], F32)
    b2s = sbt("b2s", [128, 4], F32)
    b3s = sbt("b3s", [128, 2], F32)
    b4bc = sbt("b4bc", [128, 1], F32)
    b4h = sbt("b4h", [128, 1], F32)
    gb3h = sbt("gb3h", [1, 1], F32)
    gb1s = sbt("gb1s", [128, 4], F32)
    gb2s = sbt("gb2s", [128, 2], F32)
    gb3s = sbt("gb3s", [1, 1], F32)
    ars = sbt("ars", [1, 1], F32)
    rowz = sbt("rowz", [128, 1024], F32)
    rowzb = sbt("rowzb", [128, 1024], BF16)
    zbc = sbt("zbc", [128, 512], BF16)
    negM = sbt("negM", [128, 2], F32)
    sc12 = sbt("sc12", [128, 2], F32)
    cpl = sbt("cpl", [128, 8], F32)
    ncpl = sbt("ncpl", [128, 8], F32)
    sigt = sbt("sigt", [128, 8], F32)
    gcrowb = sbt("gcrowb", [1, 1024], BF16)
    gcT = sbt("gcT", [128, 8], BF16)
    hg1 = sbt("hg1", [128, 4], BF16)
    hg2 = sbt("hg2", [128, 2], BF16)
    grow = sbt("grow", [1, 8], F32)
    normrow = sbt("normrow", [2, 8], F32)

    ARENA_F32 = 28688
    arena_t = sbt("arena", [128, ARENA_F32], F32)
    arena_bt = arena_t.bitcast(BF16)
    AB_ROW = 2 * ARENA_F32
    arena = Arena(ARENA_F32 * 4)

    def af(off_b, cols, p0=0, np_=128):
        assert off_b % 4 == 0
        return AP(arena_t, (off_b // 4) + p0 * ARENA_F32, [[ARENA_F32, np_], [1, cols]])

    def ab(off_b, cols, p0=0, np_=128):
        assert off_b % 2 == 0
        return AP(arena_bt, (off_b // 2) + p0 * AB_ROW, [[AB_ROW, np_], [1, cols]])

    ps_s = [es.enter_context(nc.psum_tensor(f"ps_s{i}", [128, 1024], F32)) for i in range(2)]
    ps_a = es.enter_context(nc.psum_tensor("ps_a", [128, 512], F32))
    ps_b = es.enter_context(nc.psum_tensor("ps_b", [128, 512], F32))
    ps_t0 = es.enter_context(nc.psum_tensor("ps_t0", [128, 1024], BF16))
    ps_c = es.enter_context(nc.psum_tensor("ps_c", [128, 512], F32))
    ps_t = [ps_t0, ps_c.bitcast(BF16)]

    sems = {e: es.enter_context(nc.semaphore(f"sem_{e}")) for e in ENGS}
    sems["gpsimd_c"] = es.enter_context(nc.semaphore("sem_gpsimd_c"))
    block = es.enter_context(nc.Block())

    P = Prog()

    def ZR(qc):
        return (32 * qc) if qc < 3 else 0
    b_s = [[Buf("s0a", psum=True), Buf("s0b", psum=True)],
           [Buf("s1a", psum=True), Buf("s1b", psum=True)]]
    b_pa, b_pb = Buf("pa", psum=True), Buf("pb", psum=True)
    b_pt = [Buf("pt0", psum=True), Buf("pt1", psum=True)]
    b_stream = [Buf(f"st{t}") for t in range(TT)]
    b_lnsc = Buf("lnsc")
    b_lnsc2 = Buf("lnsc2")
    dumpctr = [0]
    b_castb = [Buf(f"cb{i}") for i in range(4)]
    b_masks, b_ident, b_ones, b_selp, b_nb = Buf("mk"), Buf("id"), Buf("on"), Buf("sp"), Buf("nb")
    b_onec = Buf("onec")
    b_bias, b_eps, b_w4, b_ar, b_b4 = Buf("bi"), Buf("ep"), Buf("w4"), Buf("ar"), Buf("b4")
    b_zq = [Buf(f"zq{i}") for i in range(4)]
    b_zrecb = Buf("zrecb")
    b_zbc = Buf("zbc")
    b_negMs = [Buf("negM0"), Buf("negM1")]
    b_norm = Buf("norm")
    b_cpl, b_sig, b_sc = Buf("cpl"), Buf("sig"), Buf("sc")
    b_grow, b_gc = Buf("grow"), Buf("gc")
    b_stored = [Buf(f"so{i}") for i in range(SEQS * TT)]
    b_stats = {}
    SCOL = [0]

    def stat_col(name, width=16):
        if name not in b_stats:
            b_stats[name] = (SCOL[0], Buf(name))
            SCOL[0] += width
            assert SCOL[0] <= 184, name
        return b_stats[name]

    def dma_to(dst, src, eng="sync", w=(), r=(), tag=""):
        P.add(eng, (lambda d0, s0: (lambda e: e.dma_start(out=d0, in_=s0)))(dst, src),
              reads=list(r), writes=list(w), dma=True, tag=tag)

    dma_to(masks[:], dmasks.ap(), w=[b_masks], tag="ldmask")
    dma_to(ident[:], dident.ap(), w=[b_ident])
    dma_to(ones_f[:], dones.ap(), w=[b_ones])
    dma_to(ones_f128c[:], dones128.ap(), r=[b_ones], w=[b_ones])
    dma_to(selp[:], dselp.ap(), w=[b_selp])
    dma_to(nb128[:], dnb.ap(), w=[b_nb])
    dma_to(onec[:], donec.ap(), w=[b_onec])
    dma_to(b1s[:], db1.ap(), w=[b_bias])
    dma_to(b2s[:], db2.ap(), r=[b_bias], w=[b_bias])
    dma_to(b3s[:], db3.ap(), r=[b_bias], w=[b_bias])
    dma_to(gb1s[:], dgb1.ap(), r=[b_bias], w=[b_bias])
    dma_to(gb2s[:], dgb2.ap(), r=[b_bias], w=[b_bias])
    dma_to(gb3s[:], dgb3.ap(), r=[b_bias], w=[b_bias])
    dma_to(ars[:], dar.ap(), w=[b_ar])
    dma_to(grow[0:1, 0:1], db4.ap(), w=[b_grow], tag="ldb4")
    P.add("vector", lambda e: e.memset(eps[:], 1e-5), writes=[b_eps])
    dma_to(w4t[:], dw4.ap(), eng="gpsimd", w=[b_w4])
    P.add("tensor", lambda e: e.matmul(ps_a[:, 0:1], ones_f[:], grow[0:1, 0:1],
                                       start=True, stop=True, skip_group_check=True),
          reads=[b_ones, b_grow], writes=[b_pa], tag="b4bc")
    P.add("vector", lambda e: e.tensor_copy(out=b4bc[:], in_=ps_a[:, 0:1]),
          reads=[b_pa], writes=[b_b4])
    P.add("vector", lambda e: e.tensor_scalar_mul(out=b4h[:], in0=b4bc[:], scalar1=0.5),
          reads=[b_b4], writes=[b_b4])
    P.add("vector", lambda e: e.tensor_scalar_mul(out=gb3h[0:1, :], in0=gb3s[0:1, :],
                                                  scalar1=0.5),
          reads=[b_bias], writes=[b_bias])

    def ln_stats(tile_ap, col_sum, col_ssq, src_bufs, bsum, bssq, tag=""):
        P.add("vector",
              (lambda a0, c0: (lambda e: e.reduce_sum(
                  out=stats[:, c0:c0 + 1], in_=a0, axis=AXX)))(tile_ap, col_sum),
              reads=list(src_bufs), writes=[bsum], tag="lnsum" + tag)
        dumpctr[0] ^= 1
        dmp, bdmp = (lnsc2, b_lnsc2) if dumpctr[0] else (lnsc, b_lnsc)
        P.add("scalar",
              (lambda a0, c0, d0: (lambda e: e.activation(
                  out=d0[:], in_=a0, func=AF.Square,
                  accum_out=stats[:, c0:c0 + 1])))(tile_ap, col_ssq, dmp),
              reads=list(src_bufs), writes=[bdmp, bssq], tag="lnssq" + tag)

    def ln_rowmath(c_sum, c_ssq, c_A, c_B, bsum, bssq, bA, bB, n, tag="", extra_wA=()):
        c_t, bt = stat_col("tmp0")
        P.add("vector",
              (lambda: (lambda e: e.tensor_scalar_mul(
                  out=stats[:, c_t:c_t + n], in0=stats[:, c_sum:c_sum + n],
                  scalar1=1.0 / D)))(),
              reads=[bsum], writes=[bt], tag="rm_m" + tag)
        P.add("vector",
              (lambda: (lambda e: e.tensor_mul(
                  out=stats[:, c_A:c_A + n], in0=stats[:, c_t:c_t + n],
                  in1=stats[:, c_t:c_t + n])))(),
              reads=[bt], writes=[bA], tag="rm_m2" + tag)
        P.add("vector",
              (lambda: (lambda e: e.scalar_tensor_tensor(
                  out=stats[:, c_B:c_B + n], in0=stats[:, c_ssq:c_ssq + n],
                  scalar=1.0 / D, in1=stats[:, c_A:c_A + n],
                  op0=ALU.mult, op1=ALU.subtract)))(),
              reads=[bssq, bA], writes=[bB], tag="rm_var" + tag)
        P.add("scalar",
              (lambda: (lambda e: e.activation(
                  out=stats[:, c_B:c_B + n], in_=stats[:, c_B:c_B + n],
                  func=AF.Ln, bias=eps[:])))(),
              reads=[bB, b_eps], writes=[bB], tag="rm_ln" + tag)
        P.add("scalar",
              (lambda: (lambda e: e.activation(
                  out=stats[:, c_A:c_A + n], in_=stats[:, c_B:c_B + n],
                  func=AF.Exp, scale=-0.5)))(),
              reads=[bB], writes=[bA] + list(extra_wA), tag="rm_exp" + tag)
        P.add("vector",
              (lambda: (lambda e: e.scalar_tensor_tensor(
                  out=stats[:, c_B:c_B + n], in0=stats[:, c_t:c_t + n],
                  scalar=-1.0, in1=stats[:, c_A:c_A + n],
                  op0=ALU.mult, op1=ALU.mult)))(),
              reads=[bt, bA], writes=[bB], tag="rm_B" + tag)

    for s in range(SEQS):
        sfx = f"_s{s}"
        arena.reset()
        o_ln1t, b_ln1t = arena.alloc("ln1t", TT * 1024 * 2, nbufs=TT)
        o_attnN, b_attnN = arena.alloc("attnN", NPAIR * 2048 * 2, nbufs=NPAIR * QC)
        o_qt, b_qt = arena.alloc("qt", 2048 * 2, nbufs=QC)
        o_kt0, b_kts = arena.alloc("kt", 2 * 2048 * 2, nbufs=2 * QC)
        o_v, b_v = arena.alloc("v", 16 * 130 * 2, nbufs=4)
        o_ptl, b_ptl = arena.alloc("ptl", 2 * 1024 * 2, nbufs=2)
        o_qkvw, b_qkvw = arena.alloc("qkvw", 2 * 3 * 1024 * 2, nbufs=6)
        o_owt, b_owt = arena.alloc("owt", 8192 * 2, nbufs=8)

        def ln1T_ap(f, c0, ncols, p0=0, np_=128):
            return AP(arena_bt, o_ln1t // 2 + 2048 * f + c0 + p0 * AB_ROW,
                      [[AB_ROW, np_], [1, ncols]])

        for base in (64, 129):
            P.add("vector",
                  (lambda b0, ov: (lambda e: e.memset(
                      AP(arena_bt, ov // 2 + b0, [[AB_ROW, 128], [130, 16], [1, 1]]),
                      1.0)))(base, o_v),
                  writes=list(b_v), tag="vones" + sfx)

        for g in range(4):
            src = dx.ap()[s].rearrange("(t p) c -> p t c", p=128)[:, 4 * g:4 * g + 4, :]
            dst = AP(stream, 4096 * g, [[TT * 1024, 128], [1024, 4], [1, 1024]])
            dma_to(dst, src, w=b_stream[4 * g:4 * g + 4], tag=f"ldx{g}" + sfx)

        cS, bS = stat_col("l1sum")
        cQ, bQ = stat_col("l1ssq")
        cA1, bA1 = stat_col("A1")
        cB1, bB1 = stat_col("B1")
        for t in range(TT):
            ln_stats(stream[:, 1024 * t:1024 * (t + 1)], cS + t, cQ + t,
                     [b_stream[t]], bS, bQ, tag=f"1_{t}" + sfx)
        ln_rowmath(cS, cQ, cA1, cB1, bS, bQ, bA1, bB1, TT, tag="1" + sfx)

        for t in range(TT):
            c = t % 4
            P.add("vector",
                  (lambda t_, c_: (lambda e: e.tensor_scalar(
                      out=castb[:, 1024 * c_:1024 * (c_ + 1)],
                      in0=stream[:, 1024 * t_:1024 * (t_ + 1)],
                      scalar1=stats[:, cA1 + t_:cA1 + t_ + 1],
                      scalar2=stats[:, cB1 + t_:cB1 + t_ + 1],
                      op0=ALU.mult, op1=ALU.add)))(t, c),
                  reads=[b_stream[t], bA1, bB1], writes=[b_castb[c]],
                  tag=f"ln1ap{t}" + sfx)
            pt, bpt = ps_t[t % 2], b_pt[t % 2]

            def mk_tr(c_, pt_):
                def f(e):
                    r = None
                    for fb in range(8):
                        r = e.transpose(pt_[:, 128 * fb:128 * fb + 128],
                                        castb[:, 1024 * c_ + 128 * fb:1024 * c_ + 128 * fb + 128],
                                        ident[:])
                    return r
                return f
            P.add("tensor", mk_tr(c, pt), reads=[b_castb[c], b_ident], writes=[bpt],
                  tag=f"ln1tr{t}" + sfx)
            dstap = AP(arena_bt, o_ln1t // 2 + 128 * t, [[AB_ROW, 128], [2048, 8], [1, 128]])
            P.add("scalar",
                  (lambda pt_, d0: (lambda e: e.copy(out=d0, in_=pt_[:, 0:1024])))(pt, dstap),
                  reads=[bpt], writes=[b_ln1t[t]], tag=f"ln1ev{t}" + sfx)

        def load_qkv(p):
            slot = p % 2
            for m in range(3):
                dst = ab(o_qkvw + (slot * 3 + m) * 2048, 1024)
                dma_to(dst, dwqkv.ap()[m, p], eng="gpsimd",
                       w=[b_qkvw[slot * 3 + m]], tag=f"ldqkv{p}_{m}" + sfx)

        load_qkv(0)
        load_qkv(1)

        def emit_norms(tp):
            kpar = tp % 2
            o_ktp = o_kt0 + kpar * 4096
            bkt = b_kts[kpar * 4: kpar * 4 + 4]
            sqs_ap = ab(o_ptl, 1024)
            for (src_o, sbufs, w_) in ((o_qt, b_qt, 0), (o_ktp, bkt, 1)):
                for h2 in range(2):
                    sl, sb2 = ps_s[h2], b_s[h2]
                    P.add("vector",
                          (lambda so, h_: (lambda e: e.tensor_mul(
                              out=sqs_ap, in0=ab(so + h_ * 2048, 1024),
                              in1=ab(so + h_ * 2048, 1024))))(src_o, h2),
                          reads=[sbufs[2 * h2], sbufs[2 * h2 + 1]], writes=[b_ptl[0]],
                          tag=f"sq{tp}" + sfx)
                    for cc in range(2):
                        P.add("tensor",
                              (lambda sl_, cc_: (lambda e: e.matmul(
                                  sl_[0:2, 512 * cc_:512 * cc_ + 512], nb128[:],
                                  ab(o_ptl + 1024 * cc_, 512),
                                  start=True, stop=True, skip_group_check=True)))(sl, cc),
                              reads=[b_ptl[0], b_nb], writes=[sb2[cc]], tag=f"nmm{tp}" + sfx)
                    P.add("vector",
                          (lambda sl_, col: (lambda e: e.reduce_max(
                              out=normrow[:, col:col + 1], in_=sl_[0:2, 0:1024],
                              axis=AXX)))(sl, 4 * w_ + h2),
                          reads=[sb2[0], sb2[1]], writes=[b_norm], tag=f"nrd{tp}" + sfx)
            P.add("vector", lambda e: e.tensor_max(out=normrow[:, 0:1],
                                                   in0=normrow[:, 0:1], in1=normrow[:, 1:2]),
                  reads=[b_norm], writes=[b_norm])
            P.add("vector", lambda e: e.tensor_max(out=normrow[:, 4:5],
                                                   in0=normrow[:, 4:5], in1=normrow[:, 5:6]),
                  reads=[b_norm], writes=[b_norm])
            P.add("scalar", lambda e: e.activation(out=normrow[:, 2:3], in_=normrow[:, 0:1],
                                                   func=AF.Ln),
                  reads=[b_norm], writes=[b_norm])
            P.add("scalar", lambda e: e.activation(out=normrow[:, 6:7], in_=normrow[:, 4:5],
                                                   func=AF.Ln),
                  reads=[b_norm], writes=[b_norm])
            P.add("vector", lambda e: e.tensor_add(out=normrow[:, 3:4], in0=normrow[:, 2:3],
                                                   in1=normrow[:, 6:7]),
                  reads=[b_norm], writes=[b_norm])
            P.add("scalar", lambda e: e.activation(out=normrow[:, 7:8], in_=normrow[:, 3:4],
                                                   func=AF.Exp, scale=0.5),
                  reads=[b_norm], writes=[b_norm])
            P.add("vector", lambda e: e.tensor_scalar_mul(out=normrow[0:1, 7:8],
                                                          in0=normrow[0:1, 7:8],
                                                          scalar1=-1.02),
                  reads=[b_norm], writes=[b_norm])
            P.add("tensor", lambda e: e.matmul(ps_a[:, 0:1], ones_f[:], normrow[0:1, 7:8],
                                               start=True, stop=True, skip_group_check=True),
                  reads=[b_norm, b_ones], writes=[b_pa], tag=f"Mbc{tp}" + sfx)
            P.add("vector",
                  (lambda kp: (lambda e: e.tensor_copy(
                      out=negM[:, kp:kp + 1], in_=ps_a[:, 0:1])))(kpar),
                  reads=[b_pa], writes=[b_negMs[kpar]])

        for p in range(NPAIR):
            slot = p % 2
            kpar = p % 2
            o_ktp = o_kt0 + kpar * 4096
            b_kt = b_kts[kpar * 4: kpar * 4 + 4]
            wq = ab(o_qkvw + (slot * 3 + 0) * 2048, 1024)
            wk = ab(o_qkvw + (slot * 3 + 1) * 2048, 1024)
            wv = ab(o_qkvw + (slot * 3 + 2) * 2048, 1024)
            bwq, bwk, bwv = (b_qkvw[slot * 3 + m] for m in range(3))

            if p == 0:
                for (wt, bw, dsto, dbufs, scale, nm) in (
                        (wq, bwq, o_qt, b_qt, 1.0 / 32, "q"),
                        (wk, bwk, o_ktp, b_kt, None, "k")):
                    for qc in range(QC):
                        sl, sb2 = ps_s[qc // 2], b_s[qc // 2]
                        half = qc % 2
                        pap = sl[:, 512 * half:512 * half + 512]

                        def mk_proj(wt_, qc_, pap_):
                            def f(e):
                                r = None
                                for fb in range(8):
                                    r = e.matmul(pap_, wt_[:, 128 * fb:128 * fb + 128],
                                                 ln1T_ap(fb, 512 * qc_, 512),
                                                 start=(fb == 0), stop=(fb == 7),
                                                 skip_group_check=True)
                                return r
                            return f
                        P.add("tensor", mk_proj(wt, qc, pap),
                              reads=[bw] + b_ln1t[4 * qc:4 * qc + 4], writes=[sb2[half]],
                              tag=f"prj{nm}{p}_{qc}" + sfx)
                        dstap = ab(dsto + 1024 * qc, 512)
                        if scale is None:
                            P.add("vector",
                                  (lambda pap_, d0: (lambda e: e.tensor_copy(
                                      out=d0, in_=pap_)))(pap, dstap),
                                  reads=[sb2[half]], writes=[dbufs[qc]],
                                  tag=f"ev{nm}{p}_{qc}" + sfx)
                        else:
                            P.add("vector",
                                  (lambda pap_, d0, sc_: (lambda e: e.tensor_scalar_mul(
                                      out=d0, in0=pap_, scalar1=sc_)))(pap, dstap, scale),
                                  reads=[sb2[half]], writes=[dbufs[qc]],
                                  tag=f"ev{nm}{p}_{qc}" + sfx)
                emit_norms(0)

            for g in range(4):
                sl, sb2 = ps_s[g % 2], b_s[g % 2]
                for j in range(4):
                    kt = 4 * g + j
                    half, sub = j // 2, j % 2
                    pap = sl[:, 512 * half + 128 * sub: 512 * half + 128 * sub + 128]

                    def mk_v(kt_, pap_, wv_):
                        def f(e):
                            r = None
                            for fb in range(8):
                                r = e.matmul(pap_, ln1T_ap(fb, 128 * kt_, 128),
                                             wv_[:, 128 * fb:128 * fb + 128],
                                             start=(fb == 0), stop=(fb == 7),
                                             skip_group_check=True)
                            return r
                        return f
                    P.add("tensor", mk_v(kt, pap, wv),
                          reads=[bwv, b_ln1t[kt]], writes=[sb2[half]],
                          tag=f"prjv{p}_{kt}" + sfx)
                for j in range(4):
                    kt = 4 * g + j
                    half, sub = j // 2, j % 2
                    srcap = AP(sl, 512 * half + 128 * sub, [[1024, 128], [64, 2], [1, 64]])
                    dstap = AP(arena_bt, o_v // 2 + 130 * kt, [[AB_ROW, 128], [65, 2], [1, 64]])
                    P.add("vector",
                          (lambda s0, d0: (lambda e: e.tensor_copy(out=d0, in_=s0)))(srcap, dstap),
                          reads=[sb2[half]], writes=[b_v[g]], tag=f"evv{p}_{kt}" + sfx)

            if p + 2 < NPAIR:
                load_qkv(p + 2)
            if p == 5:
                for fb in range(8):
                    dma_to(ab(o_owt + fb * 2048, 1024),
                           dowt.ap()[:, 1024 * fb:1024 * fb + 1024], eng="gpsimd",
                           w=[b_owt[fb]], tag=f"ldow{fb}" + sfx)

            def ptl_slot(kt):
                sl3 = kt % 3
                if sl3 < 2:
                    return arena_bt, (o_ptl + 2048 * sl3) // 2, AB_ROW, b_ptl[sl3]
                return lnsc2.bitcast(BF16), 0, 2048, b_lnsc2

            def emit_av(qc, kt, nkt):
                off = max(0, 128 * (kt - 4 * qc))
                ncols = 512 - off
                ptns, pbase, prow, pbuf = ptl_slot(kt)
                P.add("tensor",
                      (lambda kt_, o_, n_, f0, l0, tn_, pb_, pr_: (lambda e: e.matmul(
                          ps_a[0:65, o_:512],
                          AP(arena_bt, o_v // 2 + 130 * kt_, [[AB_ROW, 128], [1, 65]]),
                          AP(tn_, pb_ + o_,
                             [[pr_, 128], [1, n_]]),
                          start=f0, stop=l0,
                          skip_group_check=True)))(kt, off, ncols, kt == 0, kt == nkt - 1,
                                                   ptns, pbase, prow),
                      reads=[pbuf, b_v[kt // 4]], writes=[b_pa],
                      tag=f"avA{p}_{qc}_{kt}" + sfx)
                P.add("tensor",
                      (lambda kt_, o_, n_, f0, l0, tn_, pb_, pr_: (lambda e: e.matmul(
                          ps_b[0:65, o_:512],
                          AP(arena_bt, o_v // 2 + 130 * kt_ + 65, [[AB_ROW, 128], [1, 65]]),
                          AP(tn_, pb_ + 512 + o_,
                             [[pr_, 128], [1, n_]]),
                          start=f0, stop=l0,
                          skip_group_check=True)))(kt, off, ncols, kt == 0, kt == nkt - 1,
                                                   ptns, pbase, prow),
                      reads=[pbuf, b_v[kt // 4]], writes=[b_pb],
                      tag=f"avB{p}_{qc}_{kt}" + sfx)

            def interleave_unit(qc, which):
                nslot = (p + 1) % 2
                if which == "q":
                    wt_n = ab(o_qkvw + (nslot * 3 + 0) * 2048, 1024)
                    bw_n = b_qkvw[nslot * 3 + 0]
                    dstap = ab(o_qt + 1024 * qc, 512)
                    dbuf = b_qt[qc]
                else:
                    wt_n = ab(o_qkvw + (nslot * 3 + 1) * 2048, 1024)
                    bw_n = b_qkvw[nslot * 3 + 1]
                    dstap = ab(o_kt0 + nslot * 4096 + 1024 * qc, 512)
                    dbuf = b_kts[nslot * 4 + qc]

                def mk_prj(wt_, qc_):
                    def f(e):
                        r = None
                        for fb in range(8):
                            r = e.matmul(ps_c[:, 0:512], wt_[:, 128 * fb:128 * fb + 128],
                                         ln1T_ap(fb, 512 * qc_, 512),
                                         start=(fb == 0), stop=(fb == 7),
                                         skip_group_check=True)
                        return r
                    return f
                P.add("tensor", mk_prj(wt_n, qc),
                      reads=[bw_n] + b_ln1t[4 * qc:4 * qc + 4], writes=[b_pt[1]],
                      tag=f"iprj{which}{p + 1}_{qc}" + sfx)
                if which == "q":
                    P.add("vector",
                          (lambda d0: (lambda e: e.tensor_scalar_mul(
                              out=d0, in0=ps_c[:, 0:512], scalar1=1.0 / 32)))(dstap),
                          reads=[b_pt[1]], writes=[dbuf], tag=f"iev{which}{p + 1}_{qc}" + sfx)
                else:
                    P.add("vector",
                          (lambda d0: (lambda e: e.tensor_copy(
                              out=d0, in_=ps_c[:, 0:512])))(dstap),
                          reads=[b_pt[1]], writes=[dbuf], tag=f"iev{which}{p + 1}_{qc}" + sfx)

            for qc in range(QC):
                nkt = 4 * qc + 4
                for kt in range(nkt):
                    off = max(0, 128 * (kt - 4 * qc))
                    ncols = 512 - off
                    sl, sb2 = ps_s[kt % 2], b_s[kt % 2]
                    etns, ebase, erow, bptl = ptl_slot(kt)
                    P.add("tensor",
                          (lambda kt_, qc_, sl_, o_, n_, ok_: (lambda e: e.matmul(
                              sl_[:, o_:512],
                              AP(arena_bt, ok_ // 2 + 128 * kt_, [[AB_ROW, 64], [1, 128]]),
                              AP(arena_bt, o_qt // 2 + 512 * qc_ + o_, [[AB_ROW, 64], [1, n_]]),
                              start=True, stop=True, tile_position=(0, 0),
                              skip_group_check=True)))(kt, qc, sl, off, ncols, o_ktp),
                          reads=[b_kt[kt // 4], b_qt[qc]], writes=[sb2[0]],
                          tag=f"scA{p}_{qc}_{kt}" + sfx)
                    P.add("tensor",
                          (lambda kt_, qc_, sl_, o_, n_, ok_: (lambda e: e.matmul(
                              sl_[:, 512 + o_:1024],
                              AP(arena_bt, ok_ // 2 + 128 * kt_ + 64 * AB_ROW,
                                 [[AB_ROW, 64], [1, 128]]),
                              AP(arena_bt, o_qt // 2 + 512 * qc_ + o_ + 64 * AB_ROW,
                                 [[AB_ROW, 64], [1, n_]]),
                              start=True, stop=True, tile_position=(64, 0),
                              skip_group_check=True)))(kt, qc, sl, off, ncols, o_ktp),
                          reads=[b_kt[kt // 4], b_qt[qc]], writes=[sb2[1]],
                          tag=f"scB{p}_{qc}_{kt}" + sfx)
                    P.add("scalar",
                          (lambda kt_, sl_, o_, n_, kp, tn_, pb_, pr_: (lambda e: e.activation(
                              out=AP(tn_, pb_ + o_,
                                     [[pr_, 128], [512, 2], [1, n_]]),
                              in_=AP(sl_, o_, [[1024, 128], [512, 2], [1, n_]]),
                              func=AF.Exp, bias=negM[:, kp:kp + 1])))(kt, sl, off, ncols, kpar,
                                                                     etns, ebase, erow),
                          reads=[sb2[0], sb2[1], b_negMs[kpar]], writes=[bptl],
                          tag=f"exp{p}_{qc}_{kt}" + sfx)
                    if off or kt == 4 * qc:
                        P.add("vector",
                              (lambda kt_, o_, n_, mo_, tn_, pb_, pr_: (lambda e: e.tensor_mul(
                                  out=AP(tn_, pb_ + o_,
                                         [[pr_, 128], [512, 2], [1, n_]]),
                                  in0=AP(tn_, pb_ + o_,
                                         [[pr_, 128], [512, 2], [1, n_]]),
                                  in1=AP(masks, 512 * mo_ + o_,
                                         [[2048, 128], [0, 2], [1, n_]]))))(
                                             kt, off, ncols, kt - 4 * qc,
                                             etns, ebase, erow),
                              reads=[bptl, b_masks], writes=[bptl],
                              tag=f"msk{p}_{qc}_{kt}" + sfx)
                    if kt > 1:
                        emit_av(qc, kt - 2, nkt)
                    if kt == nkt // 2 and p + 1 < NPAIR:
                        interleave_unit(qc, "k")
                emit_av(qc, nkt - 2, nkt)
                emit_av(qc, nkt - 1, nkt)
                aidx = p * QC + qc
                P.add("vector",
                      (lambda p_, qc_: (lambda e: e.tensor_copy(
                          out=AP(arena_bt, o_attnN // 2 + 2048 * p_ + 512 * qc_,
                                 [[AB_ROW, 64], [1, 512]]),
                          in_=ps_a[0:64, :])))(p, qc),
                      reads=[b_pa], writes=[b_attnN[aidx]], tag=f"eva{p}_{qc}" + sfx)
                P.add("vector",
                      (lambda p_, qc_: (lambda e: e.tensor_copy(
                          out=AP(arena_bt, o_attnN // 2 + 2048 * p_ + 512 * qc_ + 64 * AB_ROW,
                                 [[AB_ROW, 64], [1, 512]]),
                          in_=ps_b[0:64, :])))(p, qc),
                      reads=[b_pb], writes=[b_attnN[aidx]], tag=f"evb{p}_{qc}" + sfx)
                if qc < 3:
                    P.add("vector",
                          (lambda qc_: (lambda e: e.tensor_copy(
                              out=rowz[ZR(qc_):ZR(qc_) + 1, 0:512],
                              in_=ps_a[64:65, 0:512])))(qc),
                          reads=[b_pa], writes=[b_zq[qc]], tag=f"zra{p}_{qc}" + sfx)
                    P.add("vector",
                          (lambda qc_: (lambda e: e.tensor_copy(
                              out=rowz[ZR(qc_):ZR(qc_) + 1, 512:1024],
                              in_=ps_b[64:65, 0:512])))(qc),
                          reads=[b_pb], writes=[b_zq[qc]], tag=f"zrb{p}_{qc}" + sfx)
                if p + 1 < NPAIR:
                    interleave_unit(qc, "q")

            P.add("vector", lambda e: e.tensor_copy(
                      out=rowz[96:97, 0:512], in_=ps_a[64:65, 0:512]),
                  reads=[b_pa], writes=[b_zq[3]], tag=f"zra{p}_3" + sfx)
            P.add("vector", lambda e: e.tensor_copy(
                      out=rowz[96:97, 512:1024], in_=ps_b[64:65, 0:512]),
                  reads=[b_pb], writes=[b_zq[3]], tag=f"zrb{p}_3" + sfx)
            if p + 1 < NPAIR:
                emit_norms(p + 1)
            for qc in range(QC):
                zi = qc
                srow = 96 if qc == 3 else 32 * qc
                P.add("scalar",
                      (lambda r_: (lambda e: e.activation(
                          out=rowz[r_:r_ + 1, :],
                          in_=rowz[r_:r_ + 1, :],
                          func=AF.Ln)))(srow),
                      reads=[b_zq[zi]], writes=[b_zq[zi]], tag=f"zln{p}_{qc}" + sfx)
                P.add("scalar",
                      (lambda qc_, r_: (lambda e: e.activation(
                          out=rowzb[ZR(qc_):ZR(qc_) + 1, :],
                          in_=rowz[r_:r_ + 1, :],
                          func=AF.Exp, scale=-1.0)))(qc, srow),
                      reads=[b_zq[zi]], writes=[b_zrecb], tag=f"zexp{p}_{qc}" + sfx)
                aidx = p * QC + qc
                P.add("tensor",
                      (lambda qc_: (lambda e: e.matmul(
                          ps_s[0][:, 0:512], selp[ZR(qc_):ZR(qc_) + 1, 0:128],
                          rowzb[ZR(qc_):ZR(qc_) + 1, 0:512],
                          start=True, stop=False, skip_group_check=True)))(qc),
                      reads=[b_zrecb, b_selp], writes=[b_s[0][0]], tag=f"zbcA{p}_{qc}" + sfx)
                P.add("tensor",
                      (lambda qc_: (lambda e: e.matmul(
                          ps_s[0][:, 0:512], selp[ZR(qc_):ZR(qc_) + 1, 128:256],
                          rowzb[ZR(qc_):ZR(qc_) + 1, 512:1024],
                          start=False, stop=True, skip_group_check=True)))(qc),
                      reads=[b_zrecb, b_selp], writes=[b_s[0][0]], tag=f"zbcB{p}_{qc}" + sfx)
                P.add("vector", lambda e: e.tensor_copy(out=zbc[:], in_=ps_s[0][:, 0:512]),
                      reads=[b_s[0][0]], writes=[b_zbc], tag=f"zbce{p}_{qc}" + sfx)
                P.add("vector",
                      (lambda p_, qc_: (lambda e: e.tensor_mul(
                          out=AP(arena_bt, o_attnN // 2 + 2048 * p_ + 512 * qc_,
                                 [[AB_ROW, 128], [1, 512]]),
                          in0=AP(arena_bt, o_attnN // 2 + 2048 * p_ + 512 * qc_,
                                 [[AB_ROW, 128], [1, 512]]),
                          in1=zbc[:])))(p, qc),
                      reads=[b_zbc, b_attnN[aidx]], writes=[b_attnN[aidx]],
                      tag=f"nrm{p}_{qc}" + sfx)

        cS2, bS2 = stat_col("l2sum")
        cQ2, bQ2 = stat_col("l2ssq")
        cA2, bA2 = stat_col("A2")
        cB2, bB2 = stat_col("B2")
        cSo, bSo = stat_col("losum", 2)
        cQo, bQo = stat_col("lossq", 2)
        cAo, bAo = stat_col("Ao", 2)
        cBo, bBo = stat_col("Bo", 2)
        bA2t = [Buf(f"A2t{t}") for t in range(TT)]

        def emit_gc_mm(t):
            for h2 in range(2):
                psg, bpsg = ((ps_a, b_pa), (ps_b, b_pb))[h2]
                P.add("tensor",
                      (lambda t_, h_, ps_: (lambda e: e.matmul(
                          ps_[0:1, :], stats[:, cA2 + t_:cA2 + t_ + 1],
                          stream[:, 1024 * t_ + 512 * h_:1024 * t_ + 512 * h_ + 512],
                          start=(t_ == 0), stop=(t_ == TT - 1),
                          skip_group_check=True)))(t, h2, psg),
                      reads=[bA2t[t], b_stream[t]], writes=[bpsg], tag=f"gc{t}_{h2}" + sfx)

        for t in range(TT):
            sl, sb2 = ps_s[t % 2], b_s[t % 2]

            def mk_oproj(t_, sl_):
                def f(e):
                    r = None
                    for pp in range(NPAIR):
                        for h2 in range(2):
                            r = e.matmul(
                                sl_[:, 512 * h2:512 * h2 + 512],
                                AP(arena_bt, o_attnN // 2 + 2048 * pp + 128 * t_,
                                   [[AB_ROW, 128], [1, 128]]),
                                AP(arena_bt, o_owt // 2 + 1024 * pp + 512 * h2,
                                   [[AB_ROW, 128], [1, 512]]),
                                start=(pp == 0), stop=(pp == NPAIR - 1),
                                skip_group_check=True)
                    return r
                return f
            P.add("tensor", mk_oproj(t, sl),
                  reads=[b_attnN[pp * QC + t // 4] for pp in range(NPAIR)] + list(b_owt),
                  writes=[sb2[0], sb2[1]], tag=f"oprj{t}" + sfx)
            if t >= 3:
                emit_gc_mm(t - 3)
            ln_stats(sl[:, 0:1024], cSo, cQo, [sb2[0], sb2[1]], bSo, bQo,
                     tag=f"o{t}" + sfx)
            ln_rowmath(cSo, cQo, cAo, cBo, bSo, bQo, bAo, bBo, 1, tag=f"o{t}" + sfx)
            P.add("scalar",
                  (lambda t_: (lambda e: e.activation(
                      out=stream[:, 1024 * t_:1024 * (t_ + 1)],
                      in_=stream[:, 1024 * t_:1024 * (t_ + 1)], func=AF.Identity,
                      bias=stats[:, cBo:cBo + 1])))(t),
                  reads=[bBo, b_stream[t]], writes=[b_stream[t]], tag=f"oresb{t}" + sfx)
            P.add("vector",
                  (lambda t_, sl_: (lambda e: e.scalar_tensor_tensor(
                      out=stream[:, 1024 * t_:1024 * (t_ + 1)],
                      in0=sl_[:, 0:1024], scalar=stats[:, cAo:cAo + 1],
                      in1=stream[:, 1024 * t_:1024 * (t_ + 1)],
                      op0=ALU.mult, op1=ALU.add)))(t, sl),
                  reads=[sb2[0], sb2[1], bAo, b_stream[t]], writes=[b_stream[t]],
                  tag=f"ores{t}" + sfx)
            ln_stats(stream[:, 1024 * t:1024 * (t + 1)], cS2 + t, cQ2 + t,
                     [b_stream[t]], bS2, bQ2, tag=f"2_{t}" + sfx)
            ln_rowmath(cS2 + t, cQ2 + t, cA2 + t, cB2 + t, bS2, bQ2, bA2, bB2, 1,
                       tag=f"2_{t}" + sfx, extra_wA=[bA2t[t]])
        for t in range(TT - 3, TT):
            emit_gc_mm(t)

        arena.reset()
        o_w1, b_w1 = arena.alloc("w1", 8192 * 2, nbufs=1)
        o_w2, b_w2 = arena.alloc("w2", 4096 * 2, nbufs=1)
        o_w3, b_w3 = arena.alloc("w3", 1024 * 2, nbufs=1)
        o_gw1, b_gw1 = arena.alloc("gw1", 4096 * 2, nbufs=1)
        o_gw2, b_gw2 = arena.alloc("gw2", 1024 * 2, nbufs=1)
        o_gw3, b_gw3 = arena.alloc("gw3", 4, nbufs=1)
        o_state, b_state = arena.alloc("state", 8 * 1024 * 4, nbufs=8)
        o_outT, b_outTh = arena.alloc("outT", 8192 * 2, nbufs=2)
        o_h1T, b_h1T = arena.alloc("h1T", 4096 * 2, nbufs=8)
        o_h2T, b_h2T = arena.alloc("h2T", 2048 * 2, nbufs=4)
        o_h3T, b_h3T = arena.alloc("h3T", 2048 * 2, nbufs=2)
        o_twt, b_twt = arena.alloc("twt", 2 * 1024 * 4, nbufs=8)

        dma_to(ab(o_w1, 8192), dw1.ap(), eng="gpsimd", w=[b_w1], tag="ldw1" + sfx)
        dma_to(ab(o_w2, 4096), dw2.ap(), eng="gpsimd", w=[b_w2], tag="ldw2" + sfx)
        dma_to(ab(o_w3, 1024), dw3.ap(), eng="gpsimd", w=[b_w3], tag="ldw3" + sfx)
        dma_to(ab(o_gw1, 4096), dgw1.ap(), eng="gpsimd", w=[b_gw1], tag="ldgw1" + sfx)
        dma_to(ab(o_gw2, 1024), dgw2.ap(), eng="gpsimd", w=[b_gw2], tag="ldgw2" + sfx)
        dma_to(ab(o_gw3, 2), dgw3.ap(), eng="gpsimd", w=[b_gw3], tag="ldgw3" + sfx)

        def emit_gcnet():
            P.add("tensor", lambda e: e.matmul(ps_s[0][0:1, 0:16], ones_f128c[:],
                                                stats[:, cB2:cB2 + 16],
                                                start=True, stop=True,
                                                skip_group_check=True),
                  reads=[bB2, b_ones], writes=[b_s[0][0]], tag="b2sum_mm" + sfx)
            P.add("vector", lambda e: e.reduce_sum(out=grow[0:1, 5:6], in_=ps_s[0][0:1, 0:16],
                                                   axis=AXX),
                  reads=[b_s[0][0]], writes=[b_grow], tag="b2sum_red" + sfx)
            P.add("vector", lambda e: e.tensor_scalar_mul(out=grow[0:1, 6:7],
                                                          in0=grow[0:1, 5:6], scalar1=1.0 / S),
                  reads=[b_grow], writes=[b_grow], tag="b2sum_sc" + sfx)
            P.add("scalar", lambda e: e.activation(out=gcrowb[0:1, 0:512], in_=ps_a[0:1, :],
                                                   func=AF.Identity, scale=1.0 / S,
                                                   bias=grow[0:1, 6:7]),
                  reads=[b_pa, b_grow], writes=[b_gc], tag="gcev0" + sfx)
            P.add("scalar", lambda e: e.activation(out=gcrowb[0:1, 512:1024], in_=ps_b[0:1, :],
                                                   func=AF.Identity, scale=1.0 / S,
                                                   bias=grow[0:1, 6:7]),
                  reads=[b_pb, b_grow], writes=[b_gc], tag="gcev1" + sfx)

            def mk_gctr():
                def f(e):
                    r = None
                    for fb in range(8):
                        r = e.transpose(ps_t[0][:, 2 * fb:2 * fb + 1],
                                        gcrowb[0:1, 128 * fb:128 * fb + 128], ident[0:1, 0:1])
                    return r
                return f
            P.add("tensor", mk_gctr(), reads=[b_gc, b_ident], writes=[b_pt[0]], tag="gctr" + sfx)
            P.add("vector", lambda e: e.tensor_copy(
                out=gcT[:], in_=AP(ps_t[0], 0, [[1024, 128], [2, 8], [1, 1]])),
                  reads=[b_pt[0]], writes=[b_gc], tag="gctev" + sfx)

            def mk_g1(o_):
                def f(e):
                    r = None
                    for fb in range(8):
                        r = e.matmul(ps_a[:, o_:o_ + 1],
                                     ab(o_gw1 + (o_ * 8 + fb) * 256, 128),
                                     gcT[:, fb:fb + 1], start=(fb == 0), stop=(fb == 7),
                                     skip_group_check=True)
                    return r
                return f
            for o in range(4):
                P.add("tensor", mk_g1(o), reads=[b_gw1, b_gc], writes=[b_pa],
                      tag=f"g1_{o}" + sfx)
            for o in range(4):
                P.add("scalar",
                      (lambda o_: (lambda e: e.activation(
                          out=hg1[:, o_:o_ + 1], in_=ps_a[:, o_:o_ + 1], func=AF.Gelu,
                          bias=gb1s[:, o_:o_ + 1])))(o),
                      reads=[b_pa, b_bias], writes=[b_gc], tag=f"g1g{o}" + sfx)

            def mk_g2(o_):
                def f(e):
                    r = None
                    for fb in range(4):
                        r = e.matmul(ps_b[:, o_:o_ + 1],
                                     ab(o_gw2 + (o_ * 4 + fb) * 256, 128),
                                     hg1[:, fb:fb + 1], start=(fb == 0), stop=(fb == 3),
                                     skip_group_check=True)
                    return r
                return f
            for o in range(2):
                P.add("tensor", mk_g2(o), reads=[b_gw2, b_gc], writes=[b_pb],
                      tag=f"g2_{o}" + sfx)
            for o in range(2):
                P.add("scalar",
                      (lambda o_: (lambda e: e.activation(
                          out=hg2[:, o_:o_ + 1], in_=ps_b[:, o_:o_ + 1], func=AF.Gelu,
                          bias=gb2s[:, o_:o_ + 1])))(o),
                      reads=[b_pb, b_bias], writes=[b_gc], tag=f"g2g{o}" + sfx)

            def mk_g3():
                def f(e):
                    r = None
                    for fb in range(2):
                        r = e.matmul(ps_a[0:1, 8:9], ab(o_gw3 + fb * 2, 1),
                                     hg2[:, fb:fb + 1], start=(fb == 0), stop=(fb == 1),
                                     skip_group_check=True)
                    return r
                return f
            P.add("tensor", mk_g3(), reads=[b_gw3, b_gc], writes=[b_pa], tag="g3" + sfx)
            P.add("scalar", lambda e: e.activation(out=grow[0:1, 0:1], in_=ps_a[0:1, 8:9],
                                                   func=AF.Tanh, scale=0.5,
                                                   bias=gb3h[0:1, 0:1]),
                  reads=[b_pa, b_bias], writes=[b_grow], tag="gf" + sfx)
            P.add("vector", lambda e: e.tensor_scalar(out=grow[0:1, 1:2], in0=grow[0:1, 0:1],
                                                      scalar1=0.35, scalar2=-0.15,
                                                      op0=ALU.mult, op1=ALU.add),
                  reads=[b_grow], writes=[b_grow])
            P.add("vector", lambda e: e.tensor_mul(out=grow[0:1, 2:3], in0=grow[0:1, 1:2],
                                                   in1=ars[0:1, 0:1]),
                  reads=[b_grow, b_ar], writes=[b_grow])
            P.add("vector", lambda e: e.tensor_scalar(out=grow[0:1, 3:4], in0=grow[0:1, 2:3],
                                                      scalar1=2.0, scalar2=0.1,
                                                      op0=ALU.mult, op1=ALU.add),
                  reads=[b_grow], writes=[b_grow], tag="s1" + sfx)
            P.add("vector", lambda e: e.tensor_scalar_mul(out=grow[0:1, 4:5], in0=ars[0:1, 0:1],
                                                          scalar1=0.3),
                  reads=[b_ar], writes=[b_grow], tag="s2" + sfx)
            P.add("vector", lambda e: e.tensor_add(out=grow[0:1, 3:4], in0=grow[0:1, 3:4],
                                                   in1=grow[0:1, 4:5]),
                  reads=[b_grow], writes=[b_grow], tag="s1p" + sfx)
            P.add("tensor", lambda e: e.matmul(ps_a[:, 16:18], ones_f[:], grow[0:1, 3:5],
                                               start=True, stop=True, skip_group_check=True),
                  reads=[b_grow, b_ones], writes=[b_pa], tag="s12bc" + sfx)
            P.add("vector", lambda e: e.tensor_copy(out=sc12[:], in_=ps_a[:, 16:18]),
                  reads=[b_pa], writes=[b_sc], tag="s12ev" + sfx)



        cSf, bSf = stat_col("lfsum", 8)
        cQf, bQf = stat_col("lfssq", 8)
        cAf, bAf = stat_col("Af", 8)
        cBf, bBf = stat_col("Bf", 8)
        def emit_st(ch, j):
            t = NJ * ch + j
            P.add("vector",
                  (lambda t_, j_: (lambda e: e.tensor_scalar(
                      out=af(o_state + j_ * 4096, 1024),
                      in0=stream[:, 1024 * t_:1024 * (t_ + 1)],
                      scalar1=stats[:, cA2 + t_:cA2 + t_ + 1],
                      scalar2=stats[:, cB2 + t_:cB2 + t_ + 1],
                      op0=ALU.mult, op1=ALU.add)))(t, j),
                  reads=[b_stream[t], bA2, bB2], writes=[b_state[j]],
                  tag=f"st{ch}_{j}" + sfx)

        def castrf(ch, it, th, js=None):
            for j in (range(4 * th, 4 * th + 4) if js is None else js):
                c4 = j % 4
                P.add("vector",
                      (lambda j_, c_: (lambda e: e.tensor_copy(
                          out=castb[:, 1024 * c_:1024 * (c_ + 1)],
                          in_=af(o_state + j_ * 4096, 1024))))(j, c4),
                      reads=[b_state[j]], writes=[b_castb[c4]],
                      tag=f"cst{ch}_{it}_{j}" + sfx)
                pt, bpt = ps_t[j % 2], b_pt[j % 2]

                def mk_trf(c_, pt_):
                    def f(e):
                        r = None
                        for fb in range(8):
                            r = e.transpose(pt_[:, 128 * fb:128 * fb + 128],
                                            castb[:, 1024 * c_ + 128 * fb:
                                                  1024 * c_ + 128 * fb + 128],
                                            ident[:])
                        return r
                    return f
                P.add("tensor", mk_trf(c4, pt), reads=[b_castb[c4], b_ident],
                      writes=[bpt], tag=f"trf{ch}_{it}_{j}" + sfx)
                dstap = AP(arena_bt, o_outT // 2 + 128 * (j % 4) + 512 * th,
                           [[AB_ROW, 128], [1024, 8], [1, 128]])
                P.add("scalar",
                      (lambda pt_, d0: (lambda e: e.activation(
                          out=d0, in_=pt_[:, 0:1024], func=AF.Identity)))(pt, dstap),
                      reads=[bpt], writes=[b_outTh[th]], tag=f"tev{ch}_{it}_{j}" + sfx)

        for ch in range(2):
            if ch == 0:
                for j in range(NJ):
                    emit_st(0, j)
                castrf(0, 0, 0)
                emit_gcnet()
            for it in range(NITER):
                for th in range(2):
                    for o in range(8):
                        sl, sb2 = ps_s[(o // 2) % 2], b_s[(o // 2) % 2]
                        half = o % 2
                        pap = sl[:, 512 * half:512 * half + 512]

                        def mk_w1(o_, th_, pap_):
                            def f(e):
                                r = None
                                for fb in range(8):
                                    r = e.matmul(pap_, ab(o_w1 + (o_ * 8 + fb) * 256, 128),
                                                 ab(o_outT + fb * 2048 + th_ * 1024, 512),
                                                 start=(fb == 0), stop=(fb == 7),
                                                 skip_group_check=True)
                                return r
                            return f
                        P.add("tensor", mk_w1(o, th, pap), reads=[b_w1, b_outTh[th]],
                              writes=[sb2[half]], tag=f"w1_{ch}_{it}_{th}_{o}" + sfx)
                        P.add("scalar",
                              (lambda o_, th_, pap_: (lambda e: e.activation(
                                  out=ab(o_h1T + o_ * 1024, 512), in_=pap_,
                                  func=AF.Gelu, bias=b1s[:, o_:o_ + 1])))(o, th, pap),
                              reads=[sb2[half], b_bias], writes=[b_h1T[o]],
                              tag=f"h1_{ch}_{it}_{th}_{o}" + sfx)
                    if th == 0:
                        castrf(ch, it, 1)
                    elif it + 1 < NITER:
                        castrf(ch, it + 1, 0)
                    for o in range(4):
                        sl, sb2 = ps_s[o % 2], b_s[o % 2]
                        pap = sl[:, 0:512]

                        def mk_w2(o_, th_, pap_):
                            def f(e):
                                r = None
                                for fb in range(8):
                                    r = e.matmul(pap_, ab(o_w2 + (o_ * 8 + fb) * 256, 128),
                                                 ab(o_h1T + fb * 1024, 512),
                                                 start=(fb == 0), stop=(fb == 7),
                                                 skip_group_check=True)
                                return r
                            return f
                        P.add("tensor", mk_w2(o, th, pap), reads=[b_w2] + list(b_h1T),
                              writes=[sb2[0]], tag=f"w2_{ch}_{it}_{th}_{o}" + sfx)
                        P.add("scalar",
                              (lambda o_, th_, pap_: (lambda e: e.activation(
                                  out=ab(o_h2T + o_ * 1024, 512), in_=pap_,
                                  func=AF.Gelu, bias=b2s[:, o_:o_ + 1])))(o, th, pap),
                              reads=[sb2[0], b_bias], writes=[b_h2T[o]],
                              tag=f"h2_{ch}_{it}_{th}_{o}" + sfx)
                    for o in range(2):
                        sl, sb2 = ps_s[o % 2], b_s[o % 2]
                        pap = sl[:, 512:1024]

                        def mk_w3(o_, th_, pap_):
                            def f(e):
                                r = None
                                for fb in range(4):
                                    r = e.matmul(pap_, ab(o_w3 + (o_ * 4 + fb) * 256, 128),
                                                 ab(o_h2T + fb * 1024, 512),
                                                 start=(fb == 0), stop=(fb == 3),
                                                 skip_group_check=True)
                                return r
                            return f
                        P.add("tensor", mk_w3(o, th, pap), reads=[b_w3] + list(b_h2T),
                              writes=[sb2[1]], tag=f"w3_{ch}_{it}_{th}_{o}" + sfx)
                        P.add("scalar",
                              (lambda o_, th_, pap_: (lambda e: e.activation(
                                  out=ab(o_h3T + o_ * 2048 + th_ * 1024, 512), in_=pap_,
                                  func=AF.Gelu, bias=b3s[:, o_:o_ + 1])))(o, th, pap),
                              reads=[sb2[1], b_bias], writes=[b_h3T[o]],
                              tag=f"h3_{ch}_{it}_{th}_{o}" + sfx)
                    for j in range(4 * th, 4 * th + 4):
                        def mk_tf(j_):
                            def f(e):
                                r = None
                                for fb in range(2):
                                    r = e.matmul(ps_a[:, 24 + j_:25 + j_],
                                                 AP(arena_bt,
                                                    o_h3T // 2 + fb * 1024 + 128 * (j_ % 4)
                                                    + (j_ // 4) * 512,
                                                    [[AB_ROW, 128], [1, 128]]),
                                                 w4t[:, fb:fb + 1], start=(fb == 0),
                                                 stop=(fb == 1), skip_group_check=True)
                                return r
                            return f
                        P.add("tensor", mk_tf(j), reads=list(b_h3T) + [b_w4],
                              writes=[b_pa], tag=f"tf{ch}_{it}_{j}" + sfx)
                    P.add("scalar",
                          (lambda th_: (lambda e: e.activation(
                              out=sigt[:, 4 * th_:4 * th_ + 4],
                              in_=ps_a[:, 24 + 4 * th_:28 + 4 * th_],
                              func=AF.Tanh, scale=0.5, bias=b4h[:])))(th),
                          reads=[b_pa, b_b4], writes=[b_sig], tag=f"sig{ch}_{it}_{th}" + sfx)
                    P.add("vector",
                          (lambda th_: (lambda e: e.tensor_scalar(
                              out=cpl[:, 4 * th_:4 * th_ + 4],
                              in0=sigt[:, 4 * th_:4 * th_ + 4],
                              scalar1=sc12[:, 1:2], scalar2=sc12[:, 0:1],
                              op0=ALU.mult, op1=ALU.add)))(th),
                          reads=[b_sig, b_sc], writes=[b_cpl], tag=f"cpl{ch}_{it}_{th}" + sfx)
                    P.add("vector",
                          (lambda th_: (lambda e: e.tensor_scalar_mul(
                              out=ncpl[:, 4 * th_:4 * th_ + 4],
                              in0=cpl[:, 4 * th_:4 * th_ + 4], scalar1=-1.0)))(th),
                          reads=[b_cpl], writes=[b_cpl], tag=f"ncpl{ch}_{it}_{th}" + sfx)
                    for j in range(4 * th, 4 * th + 4):
                        half = j % 2
                        for sv in range(2):
                            P.add("scalar",
                                  (lambda j_, s_, h_: (lambda e: e.tensor_copy(
                                      out=af(o_twt + h_ * 4096 + 1024 * s_, 256),
                                      in0=af(o_state + j_ * 4096 + 1024 * s_, 256))))(
                                          j, sv, half) if False else
                                  (lambda j_, s_, h_: (lambda e: e.activation(
                                      out=af(o_twt + h_ * 4096 + 1024 * s_, 256),
                                      in_=af(o_state + j_ * 4096 + 1024 * s_, 256),
                                      func=AF.Identity)))(j, sv, half),
                                  reads=[b_state[j]], writes=[b_twt[half * 4 + sv]],
                                  tag=f"twsv{ch}_{it}_{j}_{sv}" + sfx)
                        for (dc, s_src, neg, from_tmp) in (
                                (0, 2, False, False), (1, 3, True, False),
                                (2, 0, True, True), (3, 1, False, True)):
                            src_ap = (af(o_twt + (j % 2) * 4096 + 1024 * s_src, 256)
                                      if from_tmp else
                                      af(o_state + j * 4096 + 1024 * s_src, 256))
                            rbufs = ([b_twt[(j % 2) * 4 + s_src]] if from_tmp else [])                                 + [b_state[j], b_cpl]
                            P.add("vector",
                                  (lambda j_, d_, n_, sap: (lambda e: e.scalar_tensor_tensor(
                                      out=af(o_state + j_ * 4096 + 1024 * d_, 256),
                                      in0=sap, scalar=(ncpl if n_ else cpl)[:, j_:j_ + 1],
                                      in1=af(o_state + j_ * 4096 + 1024 * d_, 256),
                                      op0=ALU.mult, op1=ALU.add)))(j, dc, neg, src_ap),
                                  reads=rbufs, writes=[b_state[j]],
                                  tag=f"tw{ch}_{it}_{j}_{dc}" + sfx)

            for j in range(NJ):
                t = NJ * ch + j
                ln_stats(af(o_state + j * 4096, 1024), cSf + j, cQf + j,
                         [b_state[j]], bSf, bQf, tag=f"f{t}" + sfx)
            ln_rowmath(cSf, cQf, cAf, cBf, bSf, bQf, bAf, bBf, NJ, tag=f"f{ch}" + sfx)
            for j in range(NJ):
                t = NJ * ch + j
                P.add("scalar",
                      (lambda t_, j_: (lambda e: e.activation(
                          out=stream[:, 1024 * t_:1024 * (t_ + 1)],
                          in_=stream[:, 1024 * t_:1024 * (t_ + 1)], func=AF.Identity,
                          bias=stats[:, cBf + j_:cBf + j_ + 1])))(t, j),
                      reads=[bBf, b_stream[t]], writes=[b_stream[t]], tag=f"fapb{t}" + sfx)
                P.add("vector",
                      (lambda t_, j_: (lambda e: e.scalar_tensor_tensor(
                          out=stream[:, 1024 * t_:1024 * (t_ + 1)],
                          in0=af(o_state + j_ * 4096, 1024),
                          scalar=stats[:, cAf + j_:cAf + j_ + 1],
                          in1=stream[:, 1024 * t_:1024 * (t_ + 1)],
                          op0=ALU.mult, op1=ALU.add)))(t, j),
                      reads=[b_state[j], bAf, b_stream[t]], writes=[b_stream[t]],
                      tag=f"fres{t}" + sfx)
                outap = dout.ap()[s].rearrange("(t p) c -> p t c", p=128)[:, t:t + 1, :]
                P.add("sync",
                      (lambda t_, o0: (lambda e: e.dma_start(
                          out=o0, in_=stream[:, 1024 * t_:1024 * (t_ + 1)])))(t, outap),
                      reads=[b_stream[t]], writes=[b_stored[s * TT + t]], dma=True,
                      tag=f"stout{t}" + sfx)
                if ch == 0:
                    emit_st(1, j)
                    if j < 4:
                        castrf(1, 0, 0, js=[j])

    P.add("sync", lambda e: e.nop(), reads=b_stored, tag="fin")

    P.prepare()

    @block.tensor
    def _(eng):
        P.emit_engine("tensor", eng, sems)

    @block.vector
    def _(eng):
        P.emit_engine("vector", eng, sems)

    @block.scalar
    def _(eng):
        P.emit_engine("scalar", eng, sems)

    @block.sync
    def _(eng):
        P.emit_engine("sync", eng, sems)

    @block.gpsimd
    def _(eng):
        P.emit_engine("gpsimd", eng, sems)

    es.close()
    return nc



def _host_inputs(inp):
    perm = np.array([(j % 64) * 16 + j // 64 for j in range(D)], dtype=np.int64)
    qw = np.ascontiguousarray(np.asarray(inp["L0_attn_qw"], np.float32))[:, perm]
    kw = np.ascontiguousarray(np.asarray(inp["L0_attn_kw"], np.float32))[:, perm]
    vw = np.ascontiguousarray(np.asarray(inp["L0_attn_vw"], np.float32))[:, perm]
    ow = np.ascontiguousarray(np.asarray(inp["L0_attn_ow"], np.float32))[perm, :]

    def tile_pair(W):
        A = W.reshape(8, 128, 8, 128)
        return np.ascontiguousarray(A.transpose(2, 1, 0, 3).reshape(8, 128, 1024))

    wqkv = np.stack([tile_pair(qw), tile_pair(kw), tile_pair(vw)])
    owt = np.ascontiguousarray(ow.reshape(8, 128, 1024).transpose(1, 0, 2).reshape(128, 8192))

    def tile_of(W, no, nf):
        A = np.asarray(W, np.float32).reshape(nf, 128, no, 128)
        return np.ascontiguousarray(A.transpose(1, 2, 0, 3).reshape(128, no * nf * 128))

    w1t = tile_of(inp["L0_ffn_cn_w1"], 8, 8)
    w2t = tile_of(inp["L0_ffn_cn_w2"], 4, 8)
    w3t = tile_of(inp["L0_ffn_cn_w3"], 2, 4)
    w4t = np.ascontiguousarray(np.asarray(inp["L0_ffn_cn_w4"], np.float32).reshape(2, 128).T)
    gw1t = tile_of(inp["L0_ffn_gc_w1"], 4, 8)
    gw2t = tile_of(inp["L0_ffn_gc_w2"], 2, 4)
    gw3t = np.ascontiguousarray(np.asarray(inp["L0_ffn_gc_w3"], np.float32).reshape(2, 128).T)

    def bias_t(b, no):
        return np.ascontiguousarray(np.asarray(b, np.float32).reshape(no, 128).T)

    f32 = np.float32
    ident = np.eye(128, dtype=f32).astype(md.bfloat16)
    m = np.zeros((128, 4, 512), f32)
    pk = np.arange(128)[:, None]
    fq = np.arange(512)[None, :]
    for i, off in enumerate((0, 128, 256, 384)):
        m[:, i, :] = (fq >= pk + off).astype(f32)
    masksv = np.ascontiguousarray(m.reshape(128, 2048)).astype(md.bfloat16)
    selpair = np.zeros((128, 256), f32)
    selpair[:, 0:64] = 1.0
    selpair[:, 192:256] = 1.0
    nb = np.zeros((128, 2), f32)
    nb[0:64, 0] = 1.0
    nb[64:128, 1] = 1.0

    return dict(
        wqkv=wqkv.astype(f32), owt=owt.astype(f32), w1t=w1t, w2t=w2t, w3t=w3t,
        w4t=w4t, gw1t=gw1t, gw2t=gw2t, gw3t=gw3t,
        b1=bias_t(inp["L0_ffn_cn_b1"], 8), b2=bias_t(inp["L0_ffn_cn_b2"], 4),
        b3=bias_t(inp["L0_ffn_cn_b3"], 2),
        b4=np.asarray(inp["L0_ffn_cn_b4"], f32).reshape(1, 1),
        gb1=bias_t(inp["L0_ffn_gc_b1"], 4), gb2=bias_t(inp["L0_ffn_gc_b2"], 2),
        gb3=np.asarray(inp["L0_ffn_gc_b3"], f32).reshape(1, 1),
        ar=np.asarray(inp["L0_ffn_adaptive_range"], f32).reshape(1, 1),
        ident=ident, masks=masksv, ones_f=np.ones((1, 128), f32),
        selp=selpair.astype(md.bfloat16), nb128=nb.astype(md.bfloat16),
        ones128=np.ones((128, 1), f32),
        onec=np.ones((128, 1), f32).astype(md.bfloat16),
    )


def kernel(**inputs):
    x = np.ascontiguousarray(np.asarray(inputs["L0_attn_x"], dtype=np.float32))
    common = _host_inputs(inputs)
    nc = build_nc()
    in_maps = []
    for c in range(NCORES):
        m = dict(common)
        m["x"] = np.ascontiguousarray(x[2 * c:2 * c + 2])
        in_maps.append(m)
    res = run_bass_kernel_spmd(nc, in_maps, core_ids=list(range(NCORES)))
    out = np.concatenate([np.asarray(r["out"], dtype=np.float32) for r in res.results],
                         axis=0)
    return out
